# version 01
# baseline (speedup 1.0000x reference)
import numpy as np
import ml_dtypes
from contextlib import ExitStack

import concourse.bass as bass
import concourse.mybir as mybir
from concourse.bass_utils import run_bass_kernel_spmd

F32 = mybir.dt.float32
BF16 = mybir.dt.bfloat16
AF = mybir.ActivationFunctionType
ALU = mybir.AluOpType
AX = mybir.AxisListType
BF = ml_dtypes.bfloat16

NCORES = 8
T_ALL = 16384
TC = T_ALL // NCORES
D = 1024
E0 = 2048
E1 = 2048
S1 = 16
EC = E1 // NCORES
KD = D // 128
KE = E0 // 128


class Prog:

    ENGINES = ("sync", "gpsimd", "vector", "scalar", "tensor")

    def __init__(self, name="L"):
        self.nc = bass.Bass("TRN2", target_bir_lowering=False)
        self.stk = ExitStack()
        self.items = []

    def do(self, eng, fn, dma=False):
        assert eng in self.ENGINES
        self.items.append((eng, fn, 16 if dma else 1))

    def const(self, val, p=128):
        assert not getattr(self, "_finishing", False), "const() inside emit lambda"
        if not hasattr(self, "_ctile"):
            self._ctile = self.sbuf("consts", [128, 16], F32)
            self._cmap = {}
        if val not in self._cmap:
            col = len(self._cmap)
            assert col < 16
            self._cmap[val] = col
            self.do("vector", lambda e, col=col, val=val: e.memset(
                self._ctile[:, col:col + 1], float(val)))
        col = self._cmap[val]
        return self._ctile[0:p, col:col + 1]

    def sbuf(self, name, shape, dt):
        return self.stk.enter_context(self.nc.sbuf_tensor(name, shape, dt))

    def psum(self, name, shape, dt=F32):
        return self.stk.enter_context(self.nc.psum_tensor(name, shape, dt))

    def din(self, name, shape, dt=F32):
        return self.nc.dram_tensor(name, shape, dt, kind="ExternalInput").ap()

    def dout(self, name, shape, dt=F32):
        return self.nc.dram_tensor(name, shape, dt, kind="ExternalOutput").ap()

    def finish(self):
        self._finishing = True
        g = self.stk.enter_context(self.nc.semaphore("g"))
        pre = []
        tot = 0
        for (_, _, amt) in self.items:
            pre.append(tot)
            tot += amt
        with self.stk:
            blk = self.stk.enter_context(self.nc.Block())
            for engname in self.ENGINES:
                def body(eng, engname=engname):
                    for (e, fn, amt), w in zip(self.items, pre):
                        if e != engname:
                            continue
                        if w:
                            eng.wait_ge(g, w)
                        instr = fn(eng)
                        instr.then_inc(g, amt)
                    if engname == "gpsimd":
                        eng.wait_ge(g, tot)
                getattr(blk, engname)(body)
        return self.nc


def fmslice(hbm_ap, t0, tn, p=128):
    return hbm_ap.rearrange("(c p) t -> p c t", p=p)[:, :, t0:t0 + tn]


def chunks(sb, tile, n):
    return sb[:].rearrange("p (c t) -> p c t", c=n)


def _layernorm_emit(P, sx, nch, tile, ones_col, ones_row, srow, p_sum, p_ssq,
                    p_mb, p_rb, scratch, out_v, eps=1e-5, sub_mean=True):
    row_m = srow[0:1, 0 * tile:1 * tile]
    row_s = srow[0:1, 1 * tile:2 * tile]
    row_v = srow[0:1, 2 * tile:3 * tile]
    P.do("scalar", lambda e: e.square(scratch[:], sx[:]))
    if sub_mean:
        for kc in range(nch):
            P.do("tensor", lambda e, kc=kc: e.matmul(
                p_sum, ones_col[:], sx[:, kc * tile:(kc + 1) * tile],
                start=(kc == 0), stop=(kc == nch - 1)))
    for kc in range(nch):
        P.do("tensor", lambda e, kc=kc: e.matmul(
            p_ssq, ones_col[:], scratch[:, kc * tile:(kc + 1) * tile],
            start=(kc == 0), stop=(kc == nch - 1)))
    nd = nch * 128
    P.do("vector", lambda e: e.tensor_scalar(row_s, p_ssq, 1.0 / nd, None, ALU.mult))
    if sub_mean:
        P.do("vector", lambda e: e.tensor_scalar(row_m, p_sum, 1.0 / nd, None, ALU.mult))
        P.do("vector", lambda e: e.tensor_tensor(row_v, row_m, row_m, ALU.mult))
        P.do("vector", lambda e: e.tensor_tensor(row_v, row_s, row_v, ALU.subtract))
    else:
        P.do("vector", lambda e: e.tensor_copy(row_v, row_s))
    c_eps = P.const(eps, 1)
    c_mh = P.const(-0.5, 1)
    P.do("scalar", lambda e: e.activation(row_v, row_v, AF.Ln, bias=c_eps))
    P.do("scalar", lambda e: e.activation(row_v, row_v, AF.Exp, bias=0.0, scale=c_mh))
    if sub_mean:
        P.do("tensor", lambda e: e.matmul(p_mb[:], ones_row[:], row_m, start=True, stop=True))
    P.do("tensor", lambda e: e.matmul(p_rb[:], ones_row[:], row_v, start=True, stop=True))
    xv = sx[:].rearrange("p (c t) -> p c t", c=nch)
    sv = chunks(scratch, tile, nch)
    ov = out_v
    rb = p_rb[:].unsqueeze(1).to_broadcast([128, nch, tile])
    if sub_mean:
        mb = p_mb[:].unsqueeze(1).to_broadcast([128, nch, tile])
        P.do("vector", lambda e: e.tensor_tensor(sv, xv, mb, ALU.subtract))
        P.do("vector", lambda e: e.tensor_tensor(ov, sv, rb, ALU.mult))
    else:
        P.do("vector", lambda e: e.tensor_tensor(ov, xv, rb, ALU.mult))


def build_L1(tile=256):
    P = Prog("L1")
    ntile = TC // tile
    xfm = P.din("xfm", [D, TC])
    w_in = P.din("w_in", [D, 2 * E0])
    w_xp = P.din("w_xp", [E0, 129])
    w_b = P.din("w_b", [64, E0])
    w_c = P.din("w_c", [64, E0])
    d_vec = P.din("d_vec", [128, KE])
    a_log = P.din("a_log", [1, 1])
    o_U = P.dout("o_U", [E0, TC])
    o_abar = P.dout("o_abar", [1, TC])
    o_M0 = P.dout("o_M0", [E0, TC], BF16)
    o_DXY = P.dout("o_DXY", [E0, TC], BF16)

    sw_in = P.sbuf("sw_in", [128, KD * 2 * E0], BF16)
    sw_xp = P.sbuf("sw_xp", [128, KE * 129], BF16)
    sw_b = P.sbuf("sw_b", [64, E0], BF16)
    sw_c = P.sbuf("sw_c", [64, E0], BF16)
    sd = P.sbuf("sd", [128, KE], F32)
    sal = P.sbuf("sal", [1, 1], F32)
    s_A = P.sbuf("s_A", [1, 1], F32)
    ones_col = P.sbuf("ones_col", [128, 1], F32)
    ones_row = P.sbuf("ones_row", [1, 128], F32)
    sx = P.sbuf("sx", [128, KD * tile], F32)
    scr = P.sbuf("scr", [128, KD * tile], F32)
    sxn = P.sbuf("sxn", [128, KD * tile], BF16)
    sxi = P.sbuf("sxi", [128, KE * tile], BF16)
    ssz = P.sbuf("ssz", [128, KE * tile], BF16)
    sU = P.sbuf("sU", [128, KE * tile], F32)
    sM0 = P.sbuf("sM0", [128, KE * tile], BF16)
    sDXY = P.sbuf("sDXY", [128, KE * tile], BF16)
    sbc = P.sbuf("sbc", [64, tile], BF16)
    scc = P.sbuf("scc", [64, tile], BF16)
    srow = P.sbuf("srow", [1, 8 * tile], F32)
    sabar = P.sbuf("sabar", [1, TC], F32)
    p_stat = P.psum("p_stat", [1, 2 * tile])
    p_mb = P.psum("p_mb", [128, tile])
    p_rb = P.psum("p_rb", [128, tile])
    p_mm = [P.psum(f"p_mm{i}", [128, tile]) for i in range(2)]
    p_dr = P.psum("p_dr", [1, tile])
    p_br = P.psum("p_br", [64, tile])
    p_cr = P.psum("p_cr", [64, tile])

    def row(i):
        return srow[0:1, i * tile:(i + 1) * tile]

    P.do("vector", lambda e: e.memset(ones_col[:], 1.0))
    P.do("vector", lambda e: e.memset(ones_row[:], 1.0))
    P.do("gpsimd", lambda e: e.dma_start(
        out=chunks(sw_in, 2 * E0, KD), in_=w_in.rearrange("(c p) m -> p c m", p=128)), dma=True)
    P.do("gpsimd", lambda e: e.dma_start(
        out=chunks(sw_xp, 129, KE), in_=w_xp.rearrange("(c p) m -> p c m", p=128)), dma=True)
    P.do("gpsimd", lambda e: e.dma_start(out=sw_b[:], in_=w_b), dma=True)
    P.do("gpsimd", lambda e: e.dma_start(out=sw_c[:], in_=w_c), dma=True)
    P.do("sync", lambda e: e.dma_start(out=sd[:], in_=d_vec), dma=True)
    P.do("sync", lambda e: e.dma_start(out=sal[:], in_=a_log), dma=True)
    P.do("scalar", lambda e: e.activation(s_A[:], sal[:], AF.Exp))
    P.do("vector", lambda e: e.tensor_scalar(s_A[:], s_A[:], -1.0, None, ALU.mult))

    for it in range(ntile):
        t0 = it * tile
        P.do("sync", lambda e, t0=t0: e.dma_start(
            out=chunks(sx, tile, KD), in_=fmslice(xfm, t0, tile)), dma=True)
        _layernorm_emit(P, sx, KD, tile, ones_col, ones_row, srow,
                        p_stat[0:1, 0:tile], p_stat[0:1, tile:2 * tile],
                        p_mb, p_rb, scr, chunks(sxn, tile, KD))
        for mc in range(2 * KE):
            pt = p_mm[mc % 2]
            for kc in range(KD):
                P.do("tensor", lambda e, pt=pt, mc=mc, kc=kc: e.matmul(
                    pt[:], sw_in[:, kc * 2 * E0 + mc * 128: kc * 2 * E0 + (mc + 1) * 128],
                    sxn[:, kc * tile:(kc + 1) * tile],
                    start=(kc == 0), stop=(kc == KD - 1)))
            if mc < KE:
                P.do("vector", lambda e, pt=pt, mc=mc: e.tensor_copy(
                    sxi[:, mc * tile:(mc + 1) * tile], pt[:]))
            else:
                P.do("scalar", lambda e, pt=pt, mc=mc: e.activation(
                    ssz[:, (mc - KE) * tile:(mc - KE + 1) * tile], pt[:], AF.Silu))
        for kc in range(KE):
            P.do("tensor", lambda e, kc=kc: e.matmul(
                p_dr[:], sw_xp[:, kc * 129:kc * 129 + 1],
                sxi[:, kc * tile:(kc + 1) * tile],
                start=(kc == 0), stop=(kc == KE - 1)))
        for kc in range(KE):
            P.do("tensor", lambda e, kc=kc: e.matmul(
                p_br[:], sw_xp[:, kc * 129 + 1:kc * 129 + 65],
                sxi[:, kc * tile:(kc + 1) * tile],
                start=(kc == 0), stop=(kc == KE - 1)))
        for kc in range(KE):
            P.do("tensor", lambda e, kc=kc: e.matmul(
                p_cr[:], sw_xp[:, kc * 129 + 65:kc * 129 + 129],
                sxi[:, kc * tile:(kc + 1) * tile],
                start=(kc == 0), stop=(kc == KE - 1)))
        P.do("vector", lambda e: e.tensor_copy(sbc[:], p_br[:]))
        P.do("vector", lambda e: e.tensor_copy(scc[:], p_cr[:]))
        c_m1 = P.const(-1.0, 1)
        c_p1 = P.const(1.0, 1)
        P.do("scalar", lambda e: e.activation(row(3), p_dr[:], AF.Abs))
        P.do("scalar", lambda e, c_m1=c_m1: e.activation(row(3), row(3), AF.Exp, bias=0.0, scale=c_m1))
        P.do("scalar", lambda e, c_p1=c_p1: e.activation(row(3), row(3), AF.Ln, bias=c_p1))
        P.do("vector", lambda e: e.tensor_scalar(row(4), p_dr[:], 0.0, None, ALU.max))
        P.do("vector", lambda e: e.tensor_tensor(row(4), row(4), row(3), ALU.add))
        P.do("vector", lambda e: e.tensor_scalar(row(4), row(4), s_A[0:1, 0:1], None, ALU.mult))
        P.do("scalar", lambda e, t0=t0: e.activation(
            sabar[0:1, t0:t0 + tile], row(4), AF.Exp))
        for mc in range(KE):
            P.do("tensor", lambda e, mc=mc: e.matmul(
                p_mm[0][:], sw_b[:, mc * 128:(mc + 1) * 128], sbc[:], start=True, stop=True))
            P.do("vector", lambda e, mc=mc: e.tensor_tensor(
                sU[:, mc * tile:(mc + 1) * tile], p_mm[0][:],
                sxi[:, mc * tile:(mc + 1) * tile], ALU.mult))
            P.do("tensor", lambda e, mc=mc: e.matmul(
                p_mm[1][:], sw_c[:, mc * 128:(mc + 1) * 128], scc[:], start=True, stop=True))
            P.do("vector", lambda e, mc=mc: e.tensor_tensor(
                sM0[:, mc * tile:(mc + 1) * tile], p_mm[1][:],
                ssz[:, mc * tile:(mc + 1) * tile], ALU.mult))
            P.do("vector", lambda e, mc=mc: e.scalar_tensor_tensor(
                sDXY[:, mc * tile:(mc + 1) * tile], sxi[:, mc * tile:(mc + 1) * tile],
                sd[:, mc:mc + 1], ssz[:, mc * tile:(mc + 1) * tile], ALU.mult, ALU.mult))
        P.do("gpsimd", lambda e, t0=t0: e.dma_start(
            out=fmslice(o_U, t0, tile), in_=chunks(sU, tile, KE)), dma=True)
        P.do("gpsimd", lambda e, t0=t0: e.dma_start(
            out=fmslice(o_M0, t0, tile), in_=chunks(sM0, tile, KE)), dma=True)
        P.do("gpsimd", lambda e, t0=t0: e.dma_start(
            out=fmslice(o_DXY, t0, tile), in_=chunks(sDXY, tile, KE)), dma=True)
    P.do("gpsimd", lambda e: e.dma_start(out=o_abar, in_=sabar[:]), dma=True)
    return P.finish()



def build_L2(slab=4096):
    P = Prog("L2")
    ns = T_ALL // slab
    u_e = P.din("u_e", [EC, T_ALL])
    ab = P.din("ab", [128, T_ALL])
    o_h = P.dout("o_h", [EC, T_ALL], BF16)
    s_u = P.sbuf("s_u", [128, 2 * slab], F32)
    s_ab = P.sbuf("s_ab", [128, slab], F32)
    s_h = P.sbuf("s_h", [128, 2 * slab], F32)
    s_cy = P.sbuf("s_cy", [128, 2], F32)
    for sl in range(ns):
        t0 = sl * slab
        P.do("sync", lambda e, t0=t0: e.dma_start(
            out=chunks(s_u, slab, 2),
            in_=u_e.rearrange("(c p) t -> p c t", p=128)[:, :, t0:t0 + slab]), dma=True)
        P.do("sync", lambda e, t0=t0: e.dma_start(
            out=s_ab[:], in_=ab[:, t0:t0 + slab]), dma=True)
        for ch in range(2):
            init = 0.0 if sl == 0 else s_cy[:, ch:ch + 1]
            P.do("vector", lambda e, ch=ch, init=init: e.tensor_tensor_scan(
                s_h[:, ch * slab:(ch + 1) * slab], s_ab[:],
                s_u[:, ch * slab:(ch + 1) * slab], init, ALU.mult, ALU.add))
            P.do("vector", lambda e, ch=ch: e.tensor_copy(
                s_cy[:, ch:ch + 1], s_h[:, (ch + 1) * slab - 1:(ch + 1) * slab]))
        P.do("gpsimd", lambda e, t0=t0: e.dma_start(
            out=o_h.rearrange("(c p) t -> p c t", p=128)[:, :, t0:t0 + slab],
            in_=chunks(s_h, slab, 2)), dma=True)
    return P.finish()


def build_L3(tile=256):
    P = Prog("L3")
    ntile = TC // tile
    xfm = P.din("xfm", [D, TC])
    h0 = P.din("h0", [E0, TC], BF16)
    M0 = P.din("M0", [E0, TC], BF16)
    DXY = P.din("DXY", [E0, TC], BF16)
    w_out = P.din("w_out", [E0, D])
    tsig = P.din("tsig", [8, D])
    wff = P.din("wff", [8, D, D])
    bff = P.din("bff", [8, D])
    ipT = P.din("ipT", [D, 2 * E1])
    ident = P.din("ident", [128, 128], BF16)
    sel8 = P.din("sel8", [8, 8 * 128], BF16)
    o_x = P.dout("o_x", [D, TC])
    o_ln3 = P.dout("o_ln3", [D, TC], BF16)
    o_xin = P.dout("o_xin", [E1, TC], BF16)
    o_sz1 = P.dout("o_sz1", [E1, TC], BF16)
    x_scr = P.nc.dram_tensor("x_scr", [D, TC], F32, kind="Internal").ap()

    ARB = P.sbuf("ARB", [128, 48 * 1024], BF16)
    aw_out = ARB[:, 0:16 * 1024]
    ah0 = ARB[:, 16 * 1024:20 * 1024]
    aM0 = ARB[:, 20 * 1024:24 * 1024]
    aDXY = ARB[:, 24 * 1024:28 * 1024]
    awj = ARB[:, 0:8 * 1024]
    axm = ARB[:, 8 * 1024:12 * 1024]
    aipT = ARB[:, 0:32 * 1024]
    aln3 = ARB[:, 32 * 1024:34 * 1024]
    axn3 = ARB[:, 34 * 1024:36 * 1024]
    axin = ARB[:, 36 * 1024:40 * 1024]
    asz1 = ARB[:, 40 * 1024:44 * 1024]
    sxn2 = P.sbuf("sxn2", [128, KD * TC], BF16)
    sacc = P.sbuf("sacc", [128, KD * TC], BF16)
    smask = P.sbuf("smask", [8, TC], BF16)
    smtm = P.sbuf("smtm", [128, 8], BF16)
    sident = P.sbuf("sident", [128, 128], BF16)
    s_sel8 = P.sbuf("s_sel8", [8, 8 * 128], BF16)
    ssgn = P.sbuf("ssgn", [8, D], BF16)
    ssgnT = P.sbuf("ssgnT", [128, KD * 8], BF16)
    sbff = P.sbuf("sbff", [8, D], BF16)
    sx = P.sbuf("sx", [128, KD * tile], F32)
    scr = P.sbuf("scr", [128, KD * tile], F32)
    srow = P.sbuf("srow", [1, 4 * tile], F32)
    smax = P.sbuf("smax", [128, 1], F32)
    ones_col = P.sbuf("ones_col", [128, 1], F32)
    ones_row = P.sbuf("ones_row", [1, 128], F32)
    ones_rowb = P.sbuf("ones_rowb", [1, 128], BF16)
    p_sum = P.psum("p_sum", [1, tile])
    p_ssq = P.psum("p_ssq", [1, tile])
    p_mb = P.psum("p_mb", [128, tile])
    p_rb = P.psum("p_rb", [128, tile])
    p_mm = [P.psum(f"p_mm{i}", [128, 2 * tile]) for i in range(2)]
    p_tr = P.psum("p_tr", [128, 128], BF16)
    p_bc = P.psum("p_bc", [128, 2 * tile])

    P.do("vector", lambda e: e.memset(ones_col[:], 1.0))
    P.do("vector", lambda e: e.memset(ones_row[:], 1.0))
    P.do("vector", lambda e: e.memset(ones_rowb[:], 1.0))
    P.do("sync", lambda e: e.dma_start(out=sident[:], in_=ident), dma=True)
    P.do("sync", lambda e: e.dma_start(out=s_sel8[:], in_=sel8), dma=True)
    P.do("gpsimd", lambda e: e.dma_start(out=ssgn[:], in_=tsig), dma=True)
    P.do("gpsimd", lambda e: e.dma_start(out=sbff[:], in_=bff), dma=True)
    P.do("gpsimd", lambda e: e.dma_start(
        out=chunks(aw_out, D, KE), in_=w_out.rearrange("(c p) m -> p c m", p=128)), dma=True)
    P.do("scalar", lambda e: e.sign(ssgn[:], ssgn[:]))
    for kc in range(KD):
        P.do("tensor", lambda e, kc=kc: e.transpose(
            p_tr[:, 0:8], ssgn[:, kc * 128:(kc + 1) * 128], sident[0:8, 0:8]))
        P.do("vector", lambda e, kc=kc: e.tensor_copy(
            ssgnT[:, kc * 8:(kc + 1) * 8], p_tr[:, 0:8]))

    for it in range(ntile):
        t0 = it * tile
        P.do("sync", lambda e, t0=t0: e.dma_start(
            out=chunks(sx, tile, KD), in_=fmslice(xfm, t0, tile)), dma=True)
        P.do("sync", lambda e, t0=t0: e.dma_start(
            out=chunks(ah0, tile, KE), in_=fmslice(h0, t0, tile)), dma=True)
        P.do("sync", lambda e, t0=t0: e.dma_start(
            out=chunks(aM0, tile, KE), in_=fmslice(M0, t0, tile)), dma=True)
        P.do("sync", lambda e, t0=t0: e.dma_start(
            out=chunks(aDXY, tile, KE), in_=fmslice(DXY, t0, tile)), dma=True)
        P.do("vector", lambda e: e.tensor_tensor(ah0, ah0, aM0, ALU.mult))
        P.do("vector", lambda e: e.tensor_tensor(ah0, ah0, aDXY, ALU.add))
        for mc in range(KD):
            pt = p_mm[mc % 2][:, 0:tile]
            for kc in range(KE):
                P.do("tensor", lambda e, pt=pt, mc=mc, kc=kc: e.matmul(
                    pt, aw_out[:, kc * D + mc * 128:kc * D + (mc + 1) * 128],
                    ah0[:, kc * tile:(kc + 1) * tile],
                    start=(kc == 0), stop=(kc == KE - 1)))
            P.do("vector", lambda e, pt=pt, mc=mc: e.tensor_tensor(
                sx[:, mc * tile:(mc + 1) * tile], pt,
                sx[:, mc * tile:(mc + 1) * tile], ALU.add))
        xn2_blk = sxn2[:, it * KD * tile:(it + 1) * KD * tile]
        _layernorm_emit(P, sx, KD, tile, ones_col, ones_row, srow, p_sum[0:1, :],
                        p_ssq[0:1, :], p_mb, p_rb, scr,
                        xn2_blk.rearrange("p (c t) -> p c t", c=KD))
        P.do("gpsimd", lambda e, t0=t0: e.dma_start(
            out=fmslice(x_scr, t0, tile), in_=chunks(sx, tile, KD)), dma=True)
        for tg in range(tile // 128):
            psc = p_mm[0][:, 0:8]
            base = it * KD * tile
            for kc in range(KD):
                P.do("tensor", lambda e, kc=kc, tg=tg, psc=psc, base=base: e.matmul(
                    psc, sxn2[:, base + kc * tile + tg * 128:base + kc * tile + tg * 128 + 128],
                    ssgnT[:, kc * 8:(kc + 1) * 8],
                    start=(kc == 0), stop=(kc == KD - 1)))
            P.do("vector", lambda e, psc=psc: e.reduce_max(smax[:], psc, AX.X))
            P.do("vector", lambda e, psc=psc: e.tensor_scalar(
                smtm[:], psc, smax[:, 0:1], None, ALU.is_ge))
            P.do("tensor", lambda e: e.transpose(p_tr[0:8, :], smtm[:], sident[:]))
            P.do("vector", lambda e, t0=t0, tg=tg: e.tensor_copy(
                smask[:, t0 + tg * 128:t0 + tg * 128 + 128], p_tr[0:8, 0:128]))

    for j in range(8):
        P.do("gpsimd", lambda e, j=j: e.dma_start(
            out=chunks(awj, D, KD),
            in_=wff[j].rearrange("(c p) m -> p c m", p=128)), dma=True)
        for it in range(ntile):
            b0 = it * tile
            base = it * KD * tile
            P.do("tensor", lambda e, j=j, b0=b0: e.matmul(
                p_bc[:, 0:tile], s_sel8[:, j * 128:(j + 1) * 128],
                smask[:, b0:b0 + tile], start=True, stop=True))
            xm_v = axm[:, 0:KD * tile].rearrange("p (c t) -> p c t", c=KD)
            xn_v = sxn2[:, base:base + KD * tile].rearrange("p (c t) -> p c t", c=KD)
            bc = p_bc[:, 0:tile].unsqueeze(1).to_broadcast([128, KD, tile])
            P.do("vector", lambda e, xn_v=xn_v, bc=bc, xm_v=xm_v: e.tensor_tensor(
                xm_v, xn_v, bc, ALU.mult))
            for mc in range(KD):
                pt = p_mm[mc % 2][:, 0:tile]
                for kc in range(KD):
                    P.do("tensor", lambda e, pt=pt, mc=mc, kc=kc: e.matmul(
                        pt, awj[:, kc * D + mc * 128:kc * D + (mc + 1) * 128],
                        axm[:, kc * tile:(kc + 1) * tile],
                        start=(kc == 0), stop=(kc == KD - 1)))
                accs = sacc[:, mc * TC + b0:mc * TC + b0 + tile]
                if j == 0:
                    P.do("vector", lambda e, pt=pt, accs=accs: e.tensor_copy(accs, pt))
                else:
                    P.do("vector", lambda e, pt=pt, accs=accs: e.tensor_tensor(
                        accs, pt, accs, ALU.add))

    P.do("gpsimd", lambda e: e.dma_start(
        out=chunks(aipT, 2 * E1, KD), in_=ipT.rearrange("(c p) m -> p c m", p=128)), dma=True)
    for it in range(ntile):
        t0 = it * tile
        P.do("sync", lambda e, t0=t0: e.dma_start(
            out=chunks(sx, tile, KD), in_=fmslice(x_scr, t0, tile)), dma=True)
        for mc in range(KD):
            pt = p_mm[mc % 2][:, 0:tile]
            P.do("tensor", lambda e, pt=pt, mc=mc, t0=t0: e.matmul(
                pt, sbff[:, mc * 128:(mc + 1) * 128], smask[:, t0:t0 + tile],
                start=True, stop=True))
            accs = sacc[:, mc * TC + t0:mc * TC + t0 + tile]
            sxs = sx[:, mc * tile:(mc + 1) * tile]
            P.do("vector", lambda e, sxs=sxs, accs=accs: e.tensor_tensor(
                sxs, accs, sxs, ALU.add))
            P.do("vector", lambda e, sxs=sxs, pt=pt: e.tensor_tensor(
                sxs, pt, sxs, ALU.add))
        _layernorm_emit(P, sx, KD, tile, ones_col, ones_row, srow, p_sum[0:1, :],
                        p_ssq[0:1, :], p_mb, p_rb, scr,
                        aln3.rearrange("p (c t) -> p c t", c=KD))
        P.do("gpsimd", lambda e, t0=t0: e.dma_start(
            out=fmslice(o_ln3, t0, tile), in_=chunks(aln3, tile, KD)), dma=True)
        _layernorm_emit(P, aln3, KD, tile, ones_col, ones_row, srow, p_sum[0:1, :],
                        p_ssq[0:1, :], p_mb, p_rb, scr,
                        axn3.rearrange("p (c t) -> p c t", c=KD),
                        eps=1.1920928955078125e-07, sub_mean=False)
        for mc in range(2 * KE):
            pt = p_mm[mc % 2][:, 0:tile]
            for kc in range(KD):
                P.do("tensor", lambda e, pt=pt, mc=mc, kc=kc: e.matmul(
                    pt, aipT[:, kc * 2 * E1 + mc * 128:kc * 2 * E1 + (mc + 1) * 128],
                    axn3[:, kc * tile:(kc + 1) * tile],
                    start=(kc == 0), stop=(kc == KD - 1)))
            if mc < KE:
                P.do("vector", lambda e, pt=pt, mc=mc: e.tensor_copy(
                    axin[:, mc * tile:(mc + 1) * tile], pt))
            else:
                P.do("scalar", lambda e, pt=pt, mc=mc: e.activation(
                    asz1[:, (mc - KE) * tile:(mc - KE + 1) * tile], pt, AF.Silu))
        P.do("gpsimd", lambda e, t0=t0: e.dma_start(
            out=fmslice(o_xin, t0, tile), in_=chunks(axin, tile, KE)), dma=True)
        P.do("gpsimd", lambda e, t0=t0: e.dma_start(
            out=fmslice(o_sz1, t0, tile), in_=chunks(asz1, tile, KE)), dma=True)
        P.do("gpsimd", lambda e, t0=t0: e.dma_start(
            out=fmslice(o_x, t0, tile), in_=chunks(sx, tile, KD)), dma=True)
    return P.finish()



def build_L4(tile=512):
    P = Prog("L4")
    ntile = TC // tile
    xin_ext = P.din("xin_ext", [E1, TC + 3], BF16)
    cw = P.din("cw", [128, KE * 4])
    cb = P.din("cb", [128, KE])
    dvec1 = P.din("dvec1", [128, KE])
    dtb = P.din("dtb", [128, KE])
    sz1 = P.din("sz1", [E1, TC], BF16)
    xpT = P.din("xpT", [E1, 96])
    dtpT = P.din("dtpT", [64, E1])
    o_dt = P.dout("o_dt", [E1, TC])
    o_w = P.dout("o_w", [E1, TC])
    o_B = P.dout("o_B", [16, TC])
    o_C = P.dout("o_C", [16, TC])
    o_xcd = P.dout("o_xcd", [E1, TC], BF16)

    s_cw = P.sbuf("s_cw", [128, KE * 4], F32)
    s_cb = P.sbuf("s_cb", [128, KE], F32)
    s_dv = P.sbuf("s_dv", [128, KE], F32)
    s_db = P.sbuf("s_db", [128, KE], F32)
    s_xpT = P.sbuf("s_xpT", [128, KE * 96], BF16)
    s_dtpT = P.sbuf("s_dtpT", [64, E1], BF16)
    s_win = P.sbuf("s_win", [128, KE * (tile + 3)], F32)
    s_sz = P.sbuf("s_sz", [128, KE * tile], BF16)
    s_acc = P.sbuf("s_acc", [128, KE * tile], F32)
    s_xc = P.sbuf("s_xc", [128, KE * tile], F32)
    s_xcb = P.sbuf("s_xcb", [128, KE * tile], BF16)
    s_dt = P.sbuf("s_dt", [128, KE * tile], F32)
    s_dl = P.sbuf("s_dl", [64, tile], BF16)
    s_bc16 = P.sbuf("s_bc16", [32, tile], F32)
    s_t3 = P.sbuf("s_t3", [128, tile], F32)
    s_t4 = P.sbuf("s_t4", [128, tile], F32)
    s_xcd = P.sbuf("s_xcd", [128, KE * tile], BF16)
    p_xd = P.psum("p_xd", [96, tile])
    p_mm = [P.psum(f"p_mm{i}", [128, tile]) for i in range(2)]

    P.do("sync", lambda e: e.dma_start(out=s_cw[:], in_=cw), dma=True)
    P.do("sync", lambda e: e.dma_start(out=s_cb[:], in_=cb), dma=True)
    P.do("sync", lambda e: e.dma_start(out=s_dv[:], in_=dvec1), dma=True)
    P.do("sync", lambda e: e.dma_start(out=s_db[:], in_=dtb), dma=True)
    P.do("gpsimd", lambda e: e.dma_start(
        out=chunks(s_xpT, 96, KE), in_=xpT.rearrange("(c p) m -> p c m", p=128)), dma=True)
    P.do("gpsimd", lambda e: e.dma_start(out=s_dtpT[:], in_=dtpT), dma=True)
    c_m1 = P.const(-1.0)
    c_p1 = P.const(1.0)
    for it in range(ntile):
        t0 = it * tile
        P.do("gpsimd", lambda e, t0=t0: e.dma_start(
            out=chunks(s_win, tile + 3, KE),
            in_=xin_ext.rearrange("(c p) t -> p c t", p=128)[:, :, t0:t0 + tile + 3]), dma=True)
        P.do("sync", lambda e, t0=t0: e.dma_start(
            out=chunks(s_sz, tile, KE), in_=fmslice(sz1, t0, tile)), dma=True)
        for kc in range(KE):
            accs = s_acc[:, kc * tile:(kc + 1) * tile]
            win = lambda k, kc=kc: s_win[:, kc * (tile + 3) + k:kc * (tile + 3) + k + tile]
            P.do("vector", lambda e, accs=accs, kc=kc, win=win: e.tensor_scalar(
                accs, win(0), s_cw[:, kc * 4:kc * 4 + 1], None, ALU.mult))
            for k in range(1, 4):
                P.do("vector", lambda e, accs=accs, kc=kc, k=k, win=win: e.scalar_tensor_tensor(
                    accs, win(k), s_cw[:, kc * 4 + k:kc * 4 + k + 1], accs,
                    ALU.mult, ALU.add))
            P.do("scalar", lambda e, accs=accs, kc=kc: e.activation(
                s_xc[:, kc * tile:(kc + 1) * tile], accs, AF.Silu,
                bias=s_cb[:, kc:kc + 1]))
        P.do("vector", lambda e: e.tensor_copy(s_xcb[:], s_xc[:]))
        for kc in range(KE):
            P.do("tensor", lambda e, kc=kc: e.matmul(
                p_xd[:], s_xpT[:, kc * 96:(kc + 1) * 96],
                s_xcb[:, kc * tile:(kc + 1) * tile],
                start=(kc == 0), stop=(kc == KE - 1)))
        P.do("vector", lambda e: e.tensor_copy(s_dl[:], p_xd[0:64, :]))
        P.do("vector", lambda e: e.tensor_copy(s_bc16[:], p_xd[64:96, :]))
        P.do("gpsimd", lambda e, t0=t0: e.dma_start(
            out=o_B[:, t0:t0 + tile], in_=s_bc16[0:16, :]), dma=True)
        P.do("gpsimd", lambda e, t0=t0: e.dma_start(
            out=o_C[:, t0:t0 + tile], in_=s_bc16[16:32, :]), dma=True)
        for mc in range(KE):
            pt = p_mm[mc % 2]
            P.do("tensor", lambda e, pt=pt, mc=mc: e.matmul(
                pt[:], s_dtpT[:, mc * 128:(mc + 1) * 128], s_dl[:],
                start=True, stop=True))
            P.do("scalar", lambda e, pt=pt, mc=mc: e.activation(
                s_t3[:], pt[:], AF.Abs, bias=s_db[:, mc:mc + 1]))
            P.do("scalar", lambda e: e.activation(s_t3[:], s_t3[:], AF.Exp,
                                                  bias=0.0, scale=c_m1))
            P.do("scalar", lambda e: e.activation(s_t3[:], s_t3[:], AF.Ln, bias=c_p1))
            P.do("vector", lambda e, pt=pt, mc=mc: e.tensor_scalar(
                s_t4[:], pt[:], s_db[:, mc:mc + 1], 0.0, ALU.add, ALU.max))
            P.do("vector", lambda e, mc=mc: e.tensor_tensor(
                s_dt[:, mc * tile:(mc + 1) * tile], s_t3[:], s_t4[:], ALU.add))
        P.do("vector", lambda e: e.tensor_tensor(s_acc[:], s_dt[:], s_xc[:], ALU.mult))
        for kc in range(KE):
            P.do("vector", lambda e, kc=kc: e.scalar_tensor_tensor(
                s_xcd[:, kc * tile:(kc + 1) * tile],
                s_xc[:, kc * tile:(kc + 1) * tile], s_dv[:, kc:kc + 1],
                s_sz[:, kc * tile:(kc + 1) * tile], ALU.mult, ALU.mult))
        P.do("gpsimd", lambda e, t0=t0: e.dma_start(
            out=fmslice(o_dt, t0, tile), in_=chunks(s_dt, tile, KE)), dma=True)
        P.do("gpsimd", lambda e, t0=t0: e.dma_start(
            out=fmslice(o_w, t0, tile), in_=chunks(s_acc, tile, KE)), dma=True)
        P.do("gpsimd", lambda e, t0=t0: e.dma_start(
            out=fmslice(o_xcd, t0, tile), in_=chunks(s_xcd, tile, KE)), dma=True)
    return P.finish()


def build_L5(slab=512):
    P = Prog("L5")
    ns = T_ALL // slab
    a_e = P.din("a_e", [128, 2 * S1])
    dt_e = P.din("dt_e", [EC, T_ALL])
    w_e = P.din("w_e", [EC, T_ALL])
    Bsm = P.din("Bsm", [S1, T_ALL])
    Csm = P.din("Csm", [S1, T_ALL])
    sel = P.din("sel", [S1, S1 * 128])
    o_ys = P.dout("o_ys", [EC, T_ALL])

    s_A = P.sbuf("s_A", [128, 2 * S1], F32)
    s_dt = P.sbuf("s_dt", [128, 2 * slab], F32)
    s_w = P.sbuf("s_w", [128, 2 * slab], F32)
    s_B16 = P.sbuf("s_B16", [S1, slab], F32)
    s_C16 = P.sbuf("s_C16", [S1, slab], F32)
    s_sel = P.sbuf("s_sel", [S1, S1 * 128], F32)
    s_Bb = P.sbuf("s_Bb", [128, S1 * slab], F32)
    s_Cb = P.sbuf("s_Cb", [128, S1 * slab], F32)
    s_dA = P.sbuf("s_dA", [128, S1 * slab], F32)
    s_P = P.sbuf("s_P", [128, S1 * slab], F32)
    s_h = P.sbuf("s_h", [128, S1 * slab], F32)
    s_y = P.sbuf("s_y", [128, 2 * slab], F32)
    s_cy = P.sbuf("s_cy", [128, 2 * S1], F32)
    ones_row = P.sbuf("ones_row", [1, 128], F32)
    p_bc = P.psum("p_bc", [128, slab])

    P.do("vector", lambda e: e.memset(ones_row[:], 1.0))
    P.do("sync", lambda e: e.dma_start(out=s_sel[:], in_=sel), dma=True)
    P.do("sync", lambda e: e.dma_start(out=s_A[:], in_=a_e), dma=True)
    P.do("scalar", lambda e: e.activation(s_A[:], s_A[:], AF.Exp))
    P.do("vector", lambda e: e.tensor_scalar(s_A[:], s_A[:], -1.0, None, ALU.mult))
    nb = slab // 512
    for sl in range(ns):
        t0 = sl * slab
        P.do("sync", lambda e, t0=t0: e.dma_start(
            out=chunks(s_dt, slab, 2),
            in_=dt_e.rearrange("(c p) t -> p c t", p=128)[:, :, t0:t0 + slab]), dma=True)
        P.do("sync", lambda e, t0=t0: e.dma_start(
            out=chunks(s_w, slab, 2),
            in_=w_e.rearrange("(c p) t -> p c t", p=128)[:, :, t0:t0 + slab]), dma=True)
        P.do("sync", lambda e, t0=t0: e.dma_start(
            out=s_B16[:], in_=Bsm[:, t0:t0 + slab]), dma=True)
        P.do("sync", lambda e, t0=t0: e.dma_start(
            out=s_C16[:], in_=Csm[:, t0:t0 + slab]), dma=True)
        for s in range(S1):
            for b in range(nb):
                P.do("tensor", lambda e, s=s, b=b: e.matmul(
                    p_bc[:, 0:512], s_sel[:, s * 128:(s + 1) * 128],
                    s_B16[:, b * 512:(b + 1) * 512], start=True, stop=True))
                P.do("vector", lambda e, s=s, b=b: e.tensor_copy(
                    s_Bb[:, s * slab + b * 512:s * slab + (b + 1) * 512], p_bc[:, 0:512]))
                P.do("tensor", lambda e, s=s, b=b: e.matmul(
                    p_bc[:, 0:512], s_sel[:, s * 128:(s + 1) * 128],
                    s_C16[:, b * 512:(b + 1) * 512], start=True, stop=True))
                P.do("vector", lambda e, s=s, b=b: e.tensor_copy(
                    s_Cb[:, s * slab + b * 512:s * slab + (b + 1) * 512], p_bc[:, 0:512]))
        for ch in range(2):
            dts = s_dt[:, ch * slab:(ch + 1) * slab]
            ws = s_w[:, ch * slab:(ch + 1) * slab]
            for s in range(S1):
                P.do("scalar", lambda e, s=s, ch=ch, dts=dts: e.activation(
                    s_dA[:, s * slab:(s + 1) * slab], dts, AF.Exp, bias=0.0,
                    scale=s_A[:, ch * S1 + s:ch * S1 + s + 1]))
            wv = ws.unsqueeze(1).to_broadcast([128, S1, slab])
            P.do("vector", lambda e, wv=wv: e.tensor_tensor(
                chunks(s_P, slab, S1), wv, chunks(s_Bb, slab, S1), ALU.mult))
            for s in range(S1):
                init = 0.0 if sl == 0 else s_cy[:, ch * S1 + s:ch * S1 + s + 1]
                P.do("vector", lambda e, s=s, init=init: e.tensor_tensor_scan(
                    s_h[:, s * slab:(s + 1) * slab], s_dA[:, s * slab:(s + 1) * slab],
                    s_P[:, s * slab:(s + 1) * slab], init, ALU.mult, ALU.add))
            P.do("vector", lambda e, ch=ch: e.tensor_copy(
                s_cy[:, ch * S1:(ch + 1) * S1],
                s_h[:].rearrange("p (s t) -> p s t", s=S1)[:, :, slab - 1:slab]))
            P.do("vector", lambda e: e.tensor_tensor(
                chunks(s_dA, slab, S1), chunks(s_h, slab, S1),
                chunks(s_Cb, slab, S1), ALU.mult))
            P.do("vector", lambda e, ch=ch: e.tensor_reduce(
                s_y[:, ch * slab:(ch + 1) * slab],
                s_dA[:].rearrange("p (s t) -> p t s", s=S1), AX.X, ALU.add))
        P.do("gpsimd", lambda e, t0=t0: e.dma_start(
            out=o_ys.rearrange("(c p) t -> p c t", p=128)[:, :, t0:t0 + slab],
            in_=chunks(s_y, slab, 2)), dma=True)
    return P.finish()


def build_L6(tile=256):
    P = Prog("L6")
    ntile = TC // tile
    x3 = P.din("x3", [D, TC])
    ln3 = P.din("ln3", [D, TC], BF16)
    ys = P.din("ys", [E1, TC])
    M1 = P.din("M1", [E1, TC], BF16)
    xcd = P.din("xcd", [E1, TC], BF16)
    opT = P.din("opT", [E1, D])
    cfcT = P.din("cfcT", [D, 2 * D])
    cprT = P.din("cprT", [2 * D, D])
    o_x6 = P.dout("o_x6", [D, TC])

    s_opT = P.sbuf("s_opT", [128, KE * D], BF16)
    s_cfcT = P.sbuf("s_cfcT", [128, KD * 2 * D], BF16)
    s_cprT = P.sbuf("s_cprT", [128, KE * D], BF16)
    s_y = P.sbuf("s_y", [128, KE * tile], BF16)
    s_m1 = P.sbuf("s_m1", [128, KE * tile], BF16)
    s_xcd = P.sbuf("s_xcd", [128, KE * tile], BF16)
    s_ln3 = P.sbuf("s_ln3", [128, KD * tile], BF16)
    sx = P.sbuf("sx", [128, KD * tile], F32)
    s_xi1 = P.sbuf("s_xi1", [128, KD * tile], F32)
    scr = P.sbuf("scr", [128, KD * tile], F32)
    s_h1 = P.sbuf("s_h1", [128, KD * tile], BF16)
    s_hh = P.sbuf("s_hh", [128, KE * tile], BF16)
    srow = P.sbuf("srow", [1, 4 * tile], F32)
    ones_col = P.sbuf("ones_col", [128, 1], F32)
    ones_row = P.sbuf("ones_row", [1, 128], F32)
    p_sum = P.psum("p_sum", [1, tile])
    p_ssq = P.psum("p_ssq", [1, tile])
    p_mb = P.psum("p_mb", [128, tile])
    p_rb = P.psum("p_rb", [128, tile])
    p_mm = [P.psum(f"p_mm{i}", [128, tile]) for i in range(2)]

    P.do("vector", lambda e: e.memset(ones_col[:], 1.0))
    P.do("vector", lambda e: e.memset(ones_row[:], 1.0))
    P.do("gpsimd", lambda e: e.dma_start(
        out=chunks(s_opT, D, KE), in_=opT.rearrange("(c p) m -> p c m", p=128)), dma=True)
    P.do("gpsimd", lambda e: e.dma_start(
        out=chunks(s_cfcT, 2 * D, KD), in_=cfcT.rearrange("(c p) m -> p c m", p=128)), dma=True)
    P.do("gpsimd", lambda e: e.dma_start(
        out=chunks(s_cprT, D, KE), in_=cprT.rearrange("(c p) m -> p c m", p=128)), dma=True)
    for it in range(ntile):
        t0 = it * tile
        P.do("sync", lambda e, t0=t0: e.dma_start(
            out=chunks(sx, tile, KD), in_=fmslice(x3, t0, tile)), dma=True)
        P.do("sync", lambda e, t0=t0: e.dma_start(
            out=chunks(s_ln3, tile, KD), in_=fmslice(ln3, t0, tile)), dma=True)
        P.do("gpsimd", lambda e, t0=t0: e.dma_start(
            out=chunks(s_y, tile, KE), in_=fmslice(ys, t0, tile)), dma=True)
        P.do("sync", lambda e, t0=t0: e.dma_start(
            out=chunks(s_m1, tile, KE), in_=fmslice(M1, t0, tile)), dma=True)
        P.do("sync", lambda e, t0=t0: e.dma_start(
            out=chunks(s_xcd, tile, KE), in_=fmslice(xcd, t0, tile)), dma=True)
        P.do("vector", lambda e: e.tensor_tensor(s_y[:], s_y[:], s_m1[:], ALU.mult))
        P.do("vector", lambda e: e.tensor_tensor(s_y[:], s_y[:], s_xcd[:], ALU.add))
        for mc in range(KD):
            pt = p_mm[mc % 2]
            for kc in range(KE):
                P.do("tensor", lambda e, pt=pt, mc=mc, kc=kc: e.matmul(
                    pt[:], s_opT[:, kc * D + mc * 128:kc * D + (mc + 1) * 128],
                    s_y[:, kc * tile:(kc + 1) * tile],
                    start=(kc == 0), stop=(kc == KE - 1)))
            P.do("vector", lambda e, pt=pt, mc=mc: e.tensor_tensor(
                s_xi1[:, mc * tile:(mc + 1) * tile], pt[:],
                s_ln3[:, mc * tile:(mc + 1) * tile], ALU.add))
            P.do("vector", lambda e, mc=mc: e.tensor_tensor(
                sx[:, mc * tile:(mc + 1) * tile], s_xi1[:, mc * tile:(mc + 1) * tile],
                sx[:, mc * tile:(mc + 1) * tile], ALU.add))
        _layernorm_emit(P, s_xi1, KD, tile, ones_col, ones_row, srow, p_sum[0:1, :],
                        p_ssq[0:1, :], p_mb, p_rb, scr, chunks(s_h1, tile, KD),
                        eps=1.1920928955078125e-07, sub_mean=False)
        for mc in range(2 * KD):
            pt = p_mm[mc % 2]
            for kc in range(KD):
                P.do("tensor", lambda e, pt=pt, mc=mc, kc=kc: e.matmul(
                    pt[:], s_cfcT[:, kc * 2 * D + mc * 128:kc * 2 * D + (mc + 1) * 128],
                    s_h1[:, kc * tile:(kc + 1) * tile],
                    start=(kc == 0), stop=(kc == KD - 1)))
            P.do("scalar", lambda e, pt=pt: e.activation(
                scr[:, 0:tile], pt[:], AF.Relu))
            P.do("scalar", lambda e, mc=mc: e.activation(
                s_hh[:, mc * tile:(mc + 1) * tile], scr[:, 0:tile], AF.Square))
        for mc in range(KD):
            pt = p_mm[mc % 2]
            for kc in range(KE):
                P.do("tensor", lambda e, pt=pt, mc=mc, kc=kc: e.matmul(
                    pt[:], s_cprT[:, kc * D + mc * 128:kc * D + (mc + 1) * 128],
                    s_hh[:, kc * tile:(kc + 1) * tile],
                    start=(kc == 0), stop=(kc == KE - 1)))
            P.do("vector", lambda e, pt=pt, mc=mc: e.tensor_tensor(
                sx[:, mc * tile:(mc + 1) * tile], pt[:],
                sx[:, mc * tile:(mc + 1) * tile], ALU.add))
        P.do("gpsimd", lambda e, t0=t0: e.dma_start(
            out=fmslice(o_x6, t0, tile), in_=chunks(sx, tile, KD)), dma=True)
    return P.finish()


def build_L7(tile=512):
    P = Prog("L7")
    ntile = TC // tile
    x6 = P.din("x6", [D, TC])
    xhalo = P.din("xhalo", [128, KD])
    murT = P.din("murT", [128, KD])
    mukT = P.din("mukT", [128, KD])
    wrT = P.din("wrT", [D, D])
    wkT = P.din("wkT", [D, D])
    wvT = P.din("wvT", [D, D])
    o_xf = P.dout("o_xf", [D, TC])

    s_wr = P.sbuf("s_wr", [128, KD * D], BF16)
    s_wk = P.sbuf("s_wk", [128, KD * D], BF16)
    s_wv = P.sbuf("s_wv", [128, KD * D], BF16)
    s_mur = P.sbuf("s_mur", [128, KD], F32)
    s_muk = P.sbuf("s_muk", [128, KD], F32)
    s_murm = P.sbuf("s_murm", [128, KD], F32)
    s_mukm = P.sbuf("s_mukm", [128, KD], F32)
    s_halo = P.sbuf("s_halo", [128, KD], F32)
    s_ln4p = P.sbuf("s_ln4p", [128, KD], F32)
    sx = P.sbuf("sx", [128, KD * tile], F32)
    scr = P.sbuf("scr", [128, KD * tile], F32)
    s_ln4 = P.sbuf("s_ln4", [128, KD * tile], F32)
    s_prev = P.sbuf("s_prev", [128, KD * tile], F32)
    s_xr = P.sbuf("s_xr", [128, KD * tile], BF16)
    s_xk = P.sbuf("s_xk", [128, KD * tile], BF16)
    s_r = P.sbuf("s_r", [128, KD * tile], BF16)
    s_k = P.sbuf("s_k", [128, KD * tile], BF16)
    s_t = P.sbuf("s_t", [128, tile], F32)
    srow = P.sbuf("srow", [1, 4 * tile], F32)
    ones_col = P.sbuf("ones_col", [128, 1], F32)
    ones_row = P.sbuf("ones_row", [1, 128], F32)
    p_sum = P.psum("p_sum", [1, tile])
    p_ssq = P.psum("p_ssq", [1, tile])
    p_mb = P.psum("p_mb", [128, tile])
    p_rb = P.psum("p_rb", [128, tile])
    p_mm = [P.psum(f"p_mm{i}", [128, tile]) for i in range(2)]

    P.do("vector", lambda e: e.memset(ones_col[:], 1.0))
    P.do("vector", lambda e: e.memset(ones_row[:], 1.0))
    P.do("sync", lambda e: e.dma_start(out=s_mur[:], in_=murT), dma=True)
    P.do("sync", lambda e: e.dma_start(out=s_muk[:], in_=mukT), dma=True)
    P.do("sync", lambda e: e.dma_start(out=s_halo[:], in_=xhalo), dma=True)
    P.do("gpsimd", lambda e: e.dma_start(
        out=chunks(s_wr, D, KD), in_=wrT.rearrange("(c p) m -> p c m", p=128)), dma=True)
    P.do("gpsimd", lambda e: e.dma_start(
        out=chunks(s_wk, D, KD), in_=wkT.rearrange("(c p) m -> p c m", p=128)), dma=True)
    P.do("gpsimd", lambda e: e.dma_start(
        out=chunks(s_wv, D, KD), in_=wvT.rearrange("(c p) m -> p c m", p=128)), dma=True)
    P.do("vector", lambda e: e.tensor_scalar(s_murm[:], s_mur[:], -1.0, 1.0, ALU.mult, ALU.add))
    P.do("vector", lambda e: e.tensor_scalar(s_mukm[:], s_muk[:], -1.0, 1.0, ALU.mult, ALU.add))
    P.do("scalar", lambda e: e.square(scr[:, 0:KD], s_halo[:]))
    for kc in range(KD):
        P.do("tensor", lambda e, kc=kc: e.matmul(
            p_sum[0:1, 0:1], ones_col[:], s_halo[:, kc:kc + 1],
            start=(kc == 0), stop=(kc == KD - 1)))
        P.do("tensor", lambda e, kc=kc: e.matmul(
            p_ssq[0:1, 0:1], ones_col[:], scr[:, kc:kc + 1],
            start=(kc == 0), stop=(kc == KD - 1)))
    r0 = srow[0:1, 0:1]
    r1 = srow[0:1, 1:2]
    c_eps = P.const(1e-5, 1)
    c_mh = P.const(-0.5, 1)
    P.do("vector", lambda e: e.tensor_scalar(r0, p_sum[0:1, 0:1], 1.0 / D, None, ALU.mult))
    P.do("vector", lambda e: e.tensor_scalar(r1, p_ssq[0:1, 0:1], 1.0 / D, None, ALU.mult))
    P.do("vector", lambda e: e.scalar_tensor_tensor(r1, r0, 0.0, r1, ALU.mult, ALU.add)) if False else None
    P.do("vector", lambda e: e.tensor_tensor(srow[0:1, 2:3], r0, r0, ALU.mult))
    P.do("vector", lambda e: e.tensor_tensor(r1, r1, srow[0:1, 2:3], ALU.subtract))
    P.do("scalar", lambda e: e.activation(r1, r1, AF.Ln, bias=c_eps))
    P.do("scalar", lambda e: e.activation(r1, r1, AF.Exp, bias=0.0, scale=c_mh))
    P.do("tensor", lambda e: e.matmul(p_mb[:, 0:1], ones_row[:], r0, start=True, stop=True))
    P.do("tensor", lambda e: e.matmul(p_rb[:, 0:1], ones_row[:], r1, start=True, stop=True))
    mbb = p_mb[:, 0:1].unsqueeze(1).to_broadcast([128, KD, 1])
    rbb = p_rb[:, 0:1].unsqueeze(1).to_broadcast([128, KD, 1])
    hv = s_halo[:].unsqueeze(2)
    pv = s_ln4p[:].unsqueeze(2)
    P.do("vector", lambda e: e.tensor_tensor(hv, hv, mbb, ALU.subtract))
    P.do("vector", lambda e: e.tensor_tensor(pv, hv, rbb, ALU.mult))

    for it in range(ntile):
        t0 = it * tile
        P.do("sync", lambda e, t0=t0: e.dma_start(
            out=chunks(sx, tile, KD), in_=fmslice(x6, t0, tile)), dma=True)
        _layernorm_emit(P, sx, KD, tile, ones_col, ones_row, srow, p_sum[0:1, :],
                        p_ssq[0:1, :], p_mb, p_rb, scr, chunks(s_ln4, tile, KD))
        P.do("vector", lambda e: e.tensor_copy(
            chunks(s_prev, tile, KD)[:, :, 1:tile],
            chunks(s_ln4, tile, KD)[:, :, 0:tile - 1]))
        P.do("vector", lambda e: e.tensor_copy(
            chunks(s_prev, tile, KD)[:, :, 0:1], s_ln4p[:].unsqueeze(2)))
        P.do("vector", lambda e: e.tensor_copy(
            s_ln4p[:].unsqueeze(2), chunks(s_ln4, tile, KD)[:, :, tile - 1:tile]))
        for kc in range(KD):
            ln = s_ln4[:, kc * tile:(kc + 1) * tile]
            pr = s_prev[:, kc * tile:(kc + 1) * tile]
            P.do("vector", lambda e, kc=kc, ln=ln: e.tensor_scalar(
                s_t[:], ln, s_mur[:, kc:kc + 1], None, ALU.mult))
            P.do("vector", lambda e, kc=kc, pr=pr: e.scalar_tensor_tensor(
                s_xr[:, kc * tile:(kc + 1) * tile], pr, s_murm[:, kc:kc + 1],
                s_t[:], ALU.mult, ALU.add))
            P.do("vector", lambda e, kc=kc, ln=ln: e.tensor_scalar(
                s_t[:], ln, s_muk[:, kc:kc + 1], None, ALU.mult))
            P.do("vector", lambda e, kc=kc, pr=pr: e.scalar_tensor_tensor(
                s_xk[:, kc * tile:(kc + 1) * tile], pr, s_mukm[:, kc:kc + 1],
                s_t[:], ALU.mult, ALU.add))
        for mc in range(KD):
            pt = p_mm[mc % 2]
            for kc in range(KD):
                P.do("tensor", lambda e, pt=pt, mc=mc, kc=kc: e.matmul(
                    pt[:], s_wr[:, kc * D + mc * 128:kc * D + (mc + 1) * 128],
                    s_xr[:, kc * tile:(kc + 1) * tile],
                    start=(kc == 0), stop=(kc == KD - 1)))
            P.do("scalar", lambda e, pt=pt, mc=mc: e.activation(
                s_r[:, mc * tile:(mc + 1) * tile], pt[:], AF.Sigmoid))
        for mc in range(KD):
            pt = p_mm[mc % 2]
            for kc in range(KD):
                P.do("tensor", lambda e, pt=pt, mc=mc, kc=kc: e.matmul(
                    pt[:], s_wk[:, kc * D + mc * 128:kc * D + (mc + 1) * 128],
                    s_xk[:, kc * tile:(kc + 1) * tile],
                    start=(kc == 0), stop=(kc == KD - 1)))
            P.do("scalar", lambda e, pt=pt: e.activation(s_t[:], pt[:], AF.Relu))
            P.do("scalar", lambda e, mc=mc: e.activation(
                s_k[:, mc * tile:(mc + 1) * tile], s_t[:], AF.Square))
        for mc in range(KD):
            pt = p_mm[mc % 2]
            for kc in range(KD):
                P.do("tensor", lambda e, pt=pt, mc=mc, kc=kc: e.matmul(
                    pt[:], s_wv[:, kc * D + mc * 128:kc * D + (mc + 1) * 128],
                    s_k[:, kc * tile:(kc + 1) * tile],
                    start=(kc == 0), stop=(kc == KD - 1)))
            P.do("vector", lambda e, pt=pt, mc=mc: e.tensor_tensor(
                s_t[:], pt[:], s_r[:, mc * tile:(mc + 1) * tile], ALU.mult))
            P.do("vector", lambda e, mc=mc: e.tensor_tensor(
                sx[:, mc * tile:(mc + 1) * tile], s_t[:],
                sx[:, mc * tile:(mc + 1) * tile], ALU.add))
        P.do("gpsimd", lambda e, t0=t0: e.dma_start(
            out=fmslice(o_xf, t0, tile), in_=chunks(sx, tile, KD)), dma=True)
    return P.finish()


def _to_fm(x2d):
    return np.ascontiguousarray(x2d.T)


_NC_CACHE = {}


def _nc(name, builder):
    if name not in _NC_CACHE:
        _NC_CACHE[name] = builder()
    return _NC_CACHE[name]


def _col(v):
    n = v.shape[0] // 128
    return np.ascontiguousarray(v.reshape(n, 128).T)


def _run(nc, in_maps):
    res = run_bass_kernel_spmd(nc, in_maps, core_ids=list(range(NCORES)))
    return res.results


def kernel(**inputs):
    f32 = np.float32
    x_full = inputs["L0_attn_x"][0]
    xfm = [np.ascontiguousarray(x_full[c * TC:(c + 1) * TC].T) for c in range(NCORES)]

    im1 = []
    d_vec = _col(inputs["L0_attn_D"].astype(f32))
    a_log = np.array(inputs["L0_attn_A_log"], f32).reshape(1, 1)
    for c in range(NCORES):
        im1.append({
            "xfm": xfm[c],
            "w_in": inputs["L0_attn_W_in"],
            "w_xp": inputs["L0_attn_W_xp"],
            "w_b": inputs["L0_attn_W_B"],
            "w_c": inputs["L0_attn_W_C"],
            "d_vec": d_vec,
            "a_log": a_log,
        })
    r1 = _run(_nc("L1", build_L1), im1)

    U_all = np.concatenate([r["o_U"] for r in r1], axis=1)
    abar_all = np.concatenate([r["o_abar"][0] for r in r1])
    ab_b = np.ascontiguousarray(np.broadcast_to(abar_all[None, :], (128, T_ALL)))
    im2 = [{"u_e": np.ascontiguousarray(U_all[c * EC:(c + 1) * EC]),
            "ab": ab_b} for c in range(NCORES)]
    r2 = _run(_nc("L2", build_L2), im2)

    h_all = np.concatenate([r["o_h"] for r in r2], axis=0)
    ident = np.eye(128, dtype=BF)
    sel8 = np.zeros((8, 8 * 128), dtype=BF)
    for j in range(8):
        sel8[j, j * 128:(j + 1) * 128] = 1
    im3 = []
    for c in range(NCORES):
        im3.append({
            "xfm": xfm[c],
            "h0": np.ascontiguousarray(h_all[:, c * TC:(c + 1) * TC]),
            "M0": r1[c]["o_M0"],
            "DXY": r1[c]["o_DXY"],
            "w_out": inputs["L0_attn_W_out"],
            "tsig": inputs["L0_ffn_tile_sigs"],
            "wff": inputs["L0_ffn_W"],
            "bff": inputs["L0_ffn_b"],
            "ipT": np.ascontiguousarray(inputs["L1_attn_in_proj_w"].T),
            "ident": ident,
            "sel8": sel8,
        })
    r3 = _run(_nc("L3", build_L3), im3)

    xin_all = np.concatenate([r["o_xin"] for r in r3], axis=1)
    cw = np.ascontiguousarray(
        inputs["L1_attn_conv_w"].reshape(KE, 128, 4).transpose(1, 0, 2).reshape(128, KE * 4))
    cb = _col(inputs["L1_attn_conv_b"])
    dvec1 = _col(inputs["L1_attn_D"])
    dtb = _col(inputs["L1_attn_dt_proj_b"])
    im4 = []
    for c in range(NCORES):
        if c == 0:
            halo = np.zeros((E1, 3), dtype=BF)
        else:
            halo = xin_all[:, c * TC - 3:c * TC]
        im4.append({
            "xin_ext": np.ascontiguousarray(
                np.concatenate([halo, xin_all[:, c * TC:(c + 1) * TC]], axis=1)),
            "cw": cw, "cb": cb, "dvec1": dvec1, "dtb": dtb,
            "sz1": r3[c]["o_sz1"],
            "xpT": np.ascontiguousarray(inputs["L1_attn_x_proj_w"].T),
            "dtpT": np.ascontiguousarray(inputs["L1_attn_dt_proj_w"].T),
        })
    r4 = _run(_nc("L4", build_L4), im4)

    dt_all = np.concatenate([r["o_dt"] for r in r4], axis=1)
    w_all = np.concatenate([r["o_w"] for r in r4], axis=1)
    Bsm = np.concatenate([r["o_B"] for r in r4], axis=1)
    Csm = np.concatenate([r["o_C"] for r in r4], axis=1)
    sel = np.zeros((S1, S1 * 128), dtype=f32)
    for s in range(S1):
        sel[s, s * 128:(s + 1) * 128] = 1
    im5 = []
    for c in range(NCORES):
        a_e = np.ascontiguousarray(
            inputs["L1_attn_A_log"][c * EC:(c + 1) * EC].reshape(2, 128, S1)
            .transpose(1, 0, 2).reshape(128, 2 * S1))
        im5.append({
            "a_e": a_e,
            "dt_e": np.ascontiguousarray(dt_all[c * EC:(c + 1) * EC]),
            "w_e": np.ascontiguousarray(w_all[c * EC:(c + 1) * EC]),
            "Bsm": Bsm, "Csm": Csm, "sel": sel,
        })
    r5 = _run(_nc("L5", build_L5), im5)

    ys_all = np.concatenate([r["o_ys"] for r in r5], axis=0)
    im6 = []
    for c in range(NCORES):
        im6.append({
            "x3": r3[c]["o_x"],
            "ln3": r3[c]["o_ln3"],
            "ys": np.ascontiguousarray(ys_all[:, c * TC:(c + 1) * TC]),
            "M1": r3[c]["o_sz1"],
            "xcd": r4[c]["o_xcd"],
            "opT": np.ascontiguousarray(inputs["L1_attn_out_proj_w"].T),
            "cfcT": np.ascontiguousarray(inputs["L1_attn_c_fc_w"].T),
            "cprT": np.ascontiguousarray(inputs["L1_attn_c_proj_w"].T),
        })
    r6 = _run(_nc("L6", build_L6), im6)

    murT = _col(inputs["L1_ffn_mu_r"])
    mukT = _col(inputs["L1_ffn_mu_k"])
    im7 = []
    for c in range(NCORES):
        if c == 0:
            xhalo = np.zeros((128, KD), f32)
        else:
            xhalo = np.ascontiguousarray(
                r6[c - 1]["o_x6"][:, TC - 1].reshape(KD, 128).T)
        im7.append({
            "x6": r6[c]["o_x6"],
            "xhalo": xhalo,
            "murT": murT, "mukT": mukT,
            "wrT": np.ascontiguousarray(inputs["L1_ffn_W_r"].T),
            "wkT": np.ascontiguousarray(inputs["L1_ffn_W_k"].T),
            "wvT": np.ascontiguousarray(inputs["L1_ffn_W_v"].T),
        })
    r7 = _run(_nc("L7", build_L7), im7)

    out = np.empty((T_ALL, D), f32)
    for c in range(NCORES):
        out[c * TC:(c + 1) * TC] = r7[c]["o_xf"].T
    return out[None]


# version 02
# speedup vs baseline: 1.1103x; 1.1103x over previous
import numpy as np
import ml_dtypes
from contextlib import ExitStack

import concourse.bass as bass
import concourse.mybir as mybir
from concourse.bass_utils import run_bass_kernel_spmd

F32 = mybir.dt.float32
BF16 = mybir.dt.bfloat16
AF = mybir.ActivationFunctionType
ALU = mybir.AluOpType
AX = mybir.AxisListType
BF = ml_dtypes.bfloat16

NCORES = 8
T_ALL = 16384
TC = T_ALL // NCORES
D = 1024
E0 = 2048
E1 = 2048
S1 = 16
EC = E1 // NCORES
KD = D // 128
KE = E0 // 128


class Prog:

    ENGINES = ("sync", "gpsimd", "vector", "scalar", "tensor")

    def __init__(self, name="L"):
        self.nc = bass.Bass("TRN2", target_bir_lowering=False)
        self.stk = ExitStack()
        self.items = []

    def do(self, eng, fn, dma=False):
        assert eng in self.ENGINES
        self.items.append((eng, fn, 16 if dma else 1))

    def const(self, val, p=128):
        assert not getattr(self, "_finishing", False), "const() inside emit lambda"
        if not hasattr(self, "_ctile"):
            self._ctile = self.stk.enter_context(
                self.nc.sbuf_tensor("consts", [128, 16], F32))
            self._cmap = {}
        if val not in self._cmap:
            col = len(self._cmap)
            assert col < 16
            self._cmap[val] = col
            self.do("vector", lambda e, col=col, val=val: e.memset(
                self._ctile[:, col:col + 1], float(val)))
        col = self._cmap[val]
        return self._ctile[0:p, col:col + 1]

    ARENA_COLS = 50 * 1024

    def _ensure_arena(self):
        if not hasattr(self, "_arena"):
            self._arena = self.stk.enter_context(
                self.nc.sbuf_tensor("ARENA", [128, self.ARENA_COLS], F32))
            self._arena_off = 0
            self._pbanks = [self.stk.enter_context(
                self.nc.psum_tensor(f"PB{i}", [128, 512], F32)) for i in range(8)]
            self._pboff = 0

    def new_phase(self):
        self._ensure_arena()
        self._arena_off = 0
        self._pboff = 0

    def sbuf(self, name, shape, dt):
        self._ensure_arena()
        p, cols = shape
        f32cols = cols if dt == F32 else (cols + 1) // 2
        f32cols = (f32cols + 3) & ~3
        assert self._arena_off + f32cols <= self.ARENA_COLS,             f"arena overflow at {name}: {self._arena_off}+{f32cols}"
        base = self._arena_off
        self._arena_off += f32cols
        view = self._arena[0:p, base:base + f32cols]
        if dt == BF16:
            view = view.bitcast(BF16)[:, 0:cols]
        elif dt != F32:
            view = view.bitcast(dt)[:, 0:cols]
        else:
            view = view[:, 0:cols]
        return _SbufView(view)

    def psum(self, name, shape, dt=F32):
        self._ensure_arena()
        assert self._pboff < 8, "psum pool exhausted"
        bank = self._pbanks[self._pboff]
        self._pboff += 1
        p, cols = shape
        if dt == F32:
            return _SbufView(bank[0:p, 0:cols])
        return _SbufView(bank[:].bitcast(dt)[0:p, 0:cols])

    def din(self, name, shape, dt=F32):
        return self.nc.dram_tensor(name, shape, dt, kind="ExternalInput").ap()

    def dout(self, name, shape, dt=F32):
        return self.nc.dram_tensor(name, shape, dt, kind="ExternalOutput").ap()

    def finish(self):
        self._finishing = True
        g = self.stk.enter_context(self.nc.semaphore("g"))
        pre = []
        tot = 0
        for (_, _, amt) in self.items:
            pre.append(tot)
            tot += amt
        with self.stk:
            blk = self.stk.enter_context(self.nc.Block())
            for engname in self.ENGINES:
                def body(eng, engname=engname):
                    covered = 0
                    own_since = 0
                    elide = engname in ("tensor", "sync", "gpsimd")
                    for (e, fn, amt), w in zip(self.items, pre):
                        if e != engname:
                            continue
                        if (w - own_since > covered) if elide else (w > covered):
                            eng.wait_ge(g, w)
                            covered = w
                            own_since = 0
                        instr = fn(eng)
                        instr.then_inc(g, amt)
                        own_since += amt
                    if engname == "gpsimd":
                        eng.wait_ge(g, tot)
                getattr(blk, engname)(body)
        return self.nc


class _SbufView:

    def __init__(self, ap):
        self._ap = ap

    def __getitem__(self, key):
        return self._ap[key]

    def ap(self):
        return self._ap


def fmslice(hbm_ap, t0, tn, p=128):
    return hbm_ap.rearrange("(c p) t -> p c t", p=p)[:, :, t0:t0 + tn]


def chunks(sb, tile, n):
    return sb[:].rearrange("p (c t) -> p c t", c=n)


def _layernorm_emit(P, sx, nch, tile, ones_col, ones_row, srow, p_sum, p_ssq,
                    p_mb, p_rb, scratch, out_v, eps=1e-5, sub_mean=True):
    row_m = srow[0:1, 0 * tile:1 * tile]
    row_s = srow[0:1, 1 * tile:2 * tile]
    row_v = srow[0:1, 2 * tile:3 * tile]
    P.do("scalar", lambda e: e.square(scratch[:], sx[:]))
    if sub_mean:
        for kc in range(nch):
            P.do("tensor", lambda e, kc=kc: e.matmul(
                p_sum, ones_col[:], sx[:, kc * tile:(kc + 1) * tile],
                start=(kc == 0), stop=(kc == nch - 1)))
    for kc in range(nch):
        P.do("tensor", lambda e, kc=kc: e.matmul(
            p_ssq, ones_col[:], scratch[:, kc * tile:(kc + 1) * tile],
            start=(kc == 0), stop=(kc == nch - 1)))
    nd = nch * 128
    P.do("vector", lambda e: e.tensor_scalar(row_s, p_ssq, 1.0 / nd, None, ALU.mult))
    if sub_mean:
        P.do("vector", lambda e: e.tensor_scalar(row_m, p_sum, 1.0 / nd, None, ALU.mult))
        P.do("vector", lambda e: e.tensor_tensor(row_v, row_m, row_m, ALU.mult))
        P.do("vector", lambda e: e.tensor_tensor(row_v, row_s, row_v, ALU.subtract))
    else:
        P.do("vector", lambda e: e.tensor_copy(row_v, row_s))
    c_eps = P.const(eps, 1)
    c_mh = P.const(-0.5, 1)
    P.do("scalar", lambda e: e.activation(row_v, row_v, AF.Ln, bias=c_eps))
    P.do("scalar", lambda e: e.activation(row_v, row_v, AF.Exp, bias=0.0, scale=c_mh))
    if sub_mean:
        P.do("tensor", lambda e: e.matmul(p_mb[:], ones_row[:], row_m, start=True, stop=True))
    P.do("tensor", lambda e: e.matmul(p_rb[:], ones_row[:], row_v, start=True, stop=True))
    xv = sx[:].rearrange("p (c t) -> p c t", c=nch)
    sv = chunks(scratch, tile, nch)
    ov = out_v
    rb = p_rb[:].unsqueeze(1).to_broadcast([128, nch, tile])
    if sub_mean:
        mb = p_mb[:].unsqueeze(1).to_broadcast([128, nch, tile])
        P.do("vector", lambda e: e.tensor_tensor(sv, xv, mb, ALU.subtract))
        P.do("vector", lambda e: e.tensor_tensor(ov, sv, rb, ALU.mult))
    else:
        P.do("vector", lambda e: e.tensor_tensor(ov, xv, rb, ALU.mult))


def build_L1(tile=256):
    P = Prog("L1")
    ntile = TC // tile
    xfm = P.din("xfm", [D, TC])
    w_in = P.din("w_in", [D, 2 * E0])
    w_xp = P.din("w_xp", [E0, 129])
    w_b = P.din("w_b", [64, E0])
    w_c = P.din("w_c", [64, E0])
    d_vec = P.din("d_vec", [128, KE])
    a_log = P.din("a_log", [1, 1])
    o_U = P.dout("o_U", [E0, TC])
    o_abar = P.dout("o_abar", [1, TC])
    o_M0 = P.dout("o_M0", [E0, TC], BF16)
    o_DXY = P.dout("o_DXY", [E0, TC], BF16)

    sw_in = P.sbuf("sw_in", [128, KD * 2 * E0], BF16)
    sw_xp = P.sbuf("sw_xp", [128, KE * 129], BF16)
    sw_b = P.sbuf("sw_b", [64, E0], BF16)
    sw_c = P.sbuf("sw_c", [64, E0], BF16)
    sd = P.sbuf("sd", [128, KE], F32)
    sal = P.sbuf("sal", [1, 1], F32)
    s_A = P.sbuf("s_A", [1, 1], F32)
    ones_col = P.sbuf("ones_col", [128, 1], F32)
    ones_row = P.sbuf("ones_row", [1, 128], F32)
    sx = P.sbuf("sx", [128, KD * tile], F32)
    scr = P.sbuf("scr", [128, KD * tile], F32)
    sxn = P.sbuf("sxn", [128, KD * tile], BF16)
    sxi = P.sbuf("sxi", [128, KE * tile], BF16)
    ssz = P.sbuf("ssz", [128, KE * tile], BF16)
    sU = P.sbuf("sU", [128, KE * tile], F32)
    sM0 = P.sbuf("sM0", [128, KE * tile], BF16)
    sDXY = P.sbuf("sDXY", [128, KE * tile], BF16)
    sbc = P.sbuf("sbc", [64, tile], BF16)
    scc = P.sbuf("scc", [64, tile], BF16)
    srow = P.sbuf("srow", [1, 8 * tile], F32)
    sabar = P.sbuf("sabar", [1, TC], F32)
    p_stat = P.psum("p_stat", [1, 2 * tile])
    p_mb = P.psum("p_mb", [128, tile])
    p_rb = P.psum("p_rb", [128, tile])
    p_mm = [P.psum(f"p_mm{i}", [128, tile]) for i in range(2)]
    p_dr = P.psum("p_dr", [1, tile])
    p_br = P.psum("p_br", [64, tile])
    p_cr = P.psum("p_cr", [64, tile])

    def row(i):
        return srow[0:1, i * tile:(i + 1) * tile]

    P.do("vector", lambda e: e.memset(ones_col[:], 1.0))
    P.do("vector", lambda e: e.memset(ones_row[:], 1.0))
    P.do("gpsimd", lambda e: e.dma_start(
        out=chunks(sw_in, 2 * E0, KD), in_=w_in.rearrange("(c p) m -> p c m", p=128)), dma=True)
    P.do("gpsimd", lambda e: e.dma_start(
        out=chunks(sw_xp, 129, KE), in_=w_xp.rearrange("(c p) m -> p c m", p=128)), dma=True)
    P.do("gpsimd", lambda e: e.dma_start(out=sw_b[:], in_=w_b), dma=True)
    P.do("gpsimd", lambda e: e.dma_start(out=sw_c[:], in_=w_c), dma=True)
    P.do("sync", lambda e: e.dma_start(out=sd[:], in_=d_vec), dma=True)
    P.do("sync", lambda e: e.dma_start(out=sal[:], in_=a_log), dma=True)
    P.do("scalar", lambda e: e.activation(s_A[:], sal[:], AF.Exp))
    P.do("vector", lambda e: e.tensor_scalar(s_A[:], s_A[:], -1.0, None, ALU.mult))

    for it in range(ntile):
        t0 = it * tile
        P.do("sync", lambda e, t0=t0: e.dma_start(
            out=chunks(sx, tile, KD), in_=fmslice(xfm, t0, tile)), dma=True)
        _layernorm_emit(P, sx, KD, tile, ones_col, ones_row, srow,
                        p_stat[0:1, 0:tile], p_stat[0:1, tile:2 * tile],
                        p_mb, p_rb, scr, chunks(sxn, tile, KD))
        for mc in range(2 * KE):
            pt = p_mm[mc % 2]
            for kc in range(KD):
                P.do("tensor", lambda e, pt=pt, mc=mc, kc=kc: e.matmul(
                    pt[:], sw_in[:, kc * 2 * E0 + mc * 128: kc * 2 * E0 + (mc + 1) * 128],
                    sxn[:, kc * tile:(kc + 1) * tile],
                    start=(kc == 0), stop=(kc == KD - 1)))
            if mc < KE:
                P.do("vector", lambda e, pt=pt, mc=mc: e.tensor_copy(
                    sxi[:, mc * tile:(mc + 1) * tile], pt[:]))
            else:
                P.do("scalar", lambda e, pt=pt, mc=mc: e.activation(
                    ssz[:, (mc - KE) * tile:(mc - KE + 1) * tile], pt[:], AF.Silu))
        for kc in range(KE):
            P.do("tensor", lambda e, kc=kc: e.matmul(
                p_dr[:], sw_xp[:, kc * 129:kc * 129 + 1],
                sxi[:, kc * tile:(kc + 1) * tile],
                start=(kc == 0), stop=(kc == KE - 1)))
        for kc in range(KE):
            P.do("tensor", lambda e, kc=kc: e.matmul(
                p_br[:], sw_xp[:, kc * 129 + 1:kc * 129 + 65],
                sxi[:, kc * tile:(kc + 1) * tile],
                start=(kc == 0), stop=(kc == KE - 1)))
        for kc in range(KE):
            P.do("tensor", lambda e, kc=kc: e.matmul(
                p_cr[:], sw_xp[:, kc * 129 + 65:kc * 129 + 129],
                sxi[:, kc * tile:(kc + 1) * tile],
                start=(kc == 0), stop=(kc == KE - 1)))
        P.do("vector", lambda e: e.tensor_copy(sbc[:], p_br[:]))
        P.do("vector", lambda e: e.tensor_copy(scc[:], p_cr[:]))
        c_m1 = P.const(-1.0, 1)
        c_p1 = P.const(1.0, 1)
        P.do("scalar", lambda e: e.activation(row(3), p_dr[:], AF.Abs))
        P.do("scalar", lambda e, c_m1=c_m1: e.activation(row(3), row(3), AF.Exp, bias=0.0, scale=c_m1))
        P.do("scalar", lambda e, c_p1=c_p1: e.activation(row(3), row(3), AF.Ln, bias=c_p1))
        P.do("vector", lambda e: e.tensor_scalar(row(4), p_dr[:], 0.0, None, ALU.max))
        P.do("vector", lambda e: e.tensor_tensor(row(4), row(4), row(3), ALU.add))
        P.do("vector", lambda e: e.tensor_scalar(row(4), row(4), s_A[0:1, 0:1], None, ALU.mult))
        P.do("scalar", lambda e, t0=t0: e.activation(
            sabar[0:1, t0:t0 + tile], row(4), AF.Exp))
        for mc in range(KE):
            P.do("tensor", lambda e, mc=mc: e.matmul(
                p_mm[0][:], sw_b[:, mc * 128:(mc + 1) * 128], sbc[:], start=True, stop=True))
            P.do("vector", lambda e, mc=mc: e.tensor_tensor(
                sU[:, mc * tile:(mc + 1) * tile], p_mm[0][:],
                sxi[:, mc * tile:(mc + 1) * tile], ALU.mult))
            P.do("tensor", lambda e, mc=mc: e.matmul(
                p_mm[1][:], sw_c[:, mc * 128:(mc + 1) * 128], scc[:], start=True, stop=True))
            P.do("vector", lambda e, mc=mc: e.tensor_tensor(
                sM0[:, mc * tile:(mc + 1) * tile], p_mm[1][:],
                ssz[:, mc * tile:(mc + 1) * tile], ALU.mult))
            P.do("vector", lambda e, mc=mc: e.scalar_tensor_tensor(
                sDXY[:, mc * tile:(mc + 1) * tile], sxi[:, mc * tile:(mc + 1) * tile],
                sd[:, mc:mc + 1], ssz[:, mc * tile:(mc + 1) * tile], ALU.mult, ALU.mult))
        P.do("gpsimd", lambda e, t0=t0: e.dma_start(
            out=fmslice(o_U, t0, tile), in_=chunks(sU, tile, KE)), dma=True)
        P.do("gpsimd", lambda e, t0=t0: e.dma_start(
            out=fmslice(o_M0, t0, tile), in_=chunks(sM0, tile, KE)), dma=True)
        P.do("gpsimd", lambda e, t0=t0: e.dma_start(
            out=fmslice(o_DXY, t0, tile), in_=chunks(sDXY, tile, KE)), dma=True)
    P.do("gpsimd", lambda e: e.dma_start(out=o_abar, in_=sabar[:]), dma=True)
    return P.finish()



def build_L2(slab=4096):
    P = Prog("L2")
    ns = T_ALL // slab
    u_e = P.din("u_e", [EC, T_ALL])
    ab = P.din("ab", [128, T_ALL])
    o_h = P.dout("o_h", [EC, T_ALL], BF16)
    s_u = P.sbuf("s_u", [128, 2 * slab], F32)
    s_ab = P.sbuf("s_ab", [128, slab], F32)
    s_h = P.sbuf("s_h", [128, 2 * slab], F32)
    s_cy = P.sbuf("s_cy", [128, 2], F32)
    for sl in range(ns):
        t0 = sl * slab
        P.do("sync", lambda e, t0=t0: e.dma_start(
            out=chunks(s_u, slab, 2),
            in_=u_e.rearrange("(c p) t -> p c t", p=128)[:, :, t0:t0 + slab]), dma=True)
        P.do("sync", lambda e, t0=t0: e.dma_start(
            out=s_ab[:], in_=ab[:, t0:t0 + slab]), dma=True)
        for ch in range(2):
            init = 0.0 if sl == 0 else s_cy[:, ch:ch + 1]
            P.do("vector", lambda e, ch=ch, init=init: e.tensor_tensor_scan(
                s_h[:, ch * slab:(ch + 1) * slab], s_ab[:],
                s_u[:, ch * slab:(ch + 1) * slab], init, ALU.mult, ALU.add))
            P.do("vector", lambda e, ch=ch: e.tensor_copy(
                s_cy[:, ch:ch + 1], s_h[:, (ch + 1) * slab - 1:(ch + 1) * slab]))
        P.do("gpsimd", lambda e, t0=t0: e.dma_start(
            out=o_h.rearrange("(c p) t -> p c t", p=128)[:, :, t0:t0 + slab],
            in_=chunks(s_h, slab, 2)), dma=True)
    return P.finish()


def build_L3(tile=256):
    P = Prog("L3")
    ntile = TC // tile
    xfm = P.din("xfm", [D, TC])
    h0 = P.din("h0", [E0, TC], BF16)
    M0 = P.din("M0", [E0, TC], BF16)
    DXY = P.din("DXY", [E0, TC], BF16)
    w_out = P.din("w_out", [E0, D])
    tsig = P.din("tsig", [8, D])
    wff = P.din("wff", [8, D, D])
    bff = P.din("bff", [8, D])
    ipT = P.din("ipT", [D, 2 * E1])
    ident = P.din("ident", [128, 128], BF16)
    sel8 = P.din("sel8", [8, 8 * 128], BF16)
    o_x = P.dout("o_x", [D, TC])
    o_ln3 = P.dout("o_ln3", [D, TC], BF16)
    o_xin = P.dout("o_xin", [E1, TC], BF16)
    o_sz1 = P.dout("o_sz1", [E1, TC], BF16)
    x_scr = P.nc.dram_tensor("x_scr", [D, TC], F32, kind="Internal").ap()

    ARB = P.sbuf("ARB", [128, 48 * 1024], BF16)
    aw_out = ARB[:, 0:16 * 1024]
    ah0 = ARB[:, 16 * 1024:20 * 1024]
    aM0 = ARB[:, 20 * 1024:24 * 1024]
    aDXY = ARB[:, 24 * 1024:28 * 1024]
    awj = ARB[:, 0:8 * 1024]
    axm = ARB[:, 8 * 1024:12 * 1024]
    aipT = ARB[:, 0:32 * 1024]
    aln3 = ARB[:, 32 * 1024:34 * 1024]
    axn3 = ARB[:, 34 * 1024:36 * 1024]
    axin = ARB[:, 36 * 1024:40 * 1024]
    asz1 = ARB[:, 40 * 1024:44 * 1024]
    sxn2 = P.sbuf("sxn2", [128, KD * TC], BF16)
    sacc = P.sbuf("sacc", [128, KD * TC], BF16)
    smask = P.sbuf("smask", [8, TC], BF16)
    smtm = P.sbuf("smtm", [128, 8], BF16)
    sident = P.sbuf("sident", [128, 128], BF16)
    s_sel8 = P.sbuf("s_sel8", [8, 8 * 128], BF16)
    ssgn = P.sbuf("ssgn", [8, D], BF16)
    ssgnT = P.sbuf("ssgnT", [128, KD * 8], BF16)
    sbff = P.sbuf("sbff", [8, D], BF16)
    sx = P.sbuf("sx", [128, KD * tile], F32)
    scr = P.sbuf("scr", [128, KD * tile], F32)
    srow = P.sbuf("srow", [1, 4 * tile], F32)
    smax = P.sbuf("smax", [128, 1], F32)
    ones_col = P.sbuf("ones_col", [128, 1], F32)
    ones_row = P.sbuf("ones_row", [1, 128], F32)
    ones_rowb = P.sbuf("ones_rowb", [1, 128], BF16)
    p_sum = P.psum("p_sum", [1, tile])
    p_ssq = P.psum("p_ssq", [1, tile])
    p_mb = P.psum("p_mb", [128, tile])
    p_rb = P.psum("p_rb", [128, tile])
    p_mm = [P.psum(f"p_mm{i}", [128, 2 * tile]) for i in range(2)]
    p_tr = P.psum("p_tr", [128, 128], BF16)
    p_bc = P.psum("p_bc", [128, 2 * tile])

    P.do("vector", lambda e: e.memset(ones_col[:], 1.0))
    P.do("vector", lambda e: e.memset(ones_row[:], 1.0))
    P.do("vector", lambda e: e.memset(ones_rowb[:], 1.0))
    P.do("sync", lambda e: e.dma_start(out=sident[:], in_=ident), dma=True)
    P.do("sync", lambda e: e.dma_start(out=s_sel8[:], in_=sel8), dma=True)
    P.do("gpsimd", lambda e: e.dma_start(out=ssgn[:], in_=tsig), dma=True)
    P.do("gpsimd", lambda e: e.dma_start(out=sbff[:], in_=bff), dma=True)
    P.do("gpsimd", lambda e: e.dma_start(
        out=chunks(aw_out, D, KE), in_=w_out.rearrange("(c p) m -> p c m", p=128)), dma=True)
    P.do("scalar", lambda e: e.sign(ssgn[:], ssgn[:]))
    for kc in range(KD):
        P.do("tensor", lambda e, kc=kc: e.transpose(
            p_tr[:, 0:8], ssgn[:, kc * 128:(kc + 1) * 128], sident[0:8, 0:8]))
        P.do("vector", lambda e, kc=kc: e.tensor_copy(
            ssgnT[:, kc * 8:(kc + 1) * 8], p_tr[:, 0:8]))

    for it in range(ntile):
        t0 = it * tile
        P.do("sync", lambda e, t0=t0: e.dma_start(
            out=chunks(sx, tile, KD), in_=fmslice(xfm, t0, tile)), dma=True)
        P.do("sync", lambda e, t0=t0: e.dma_start(
            out=chunks(ah0, tile, KE), in_=fmslice(h0, t0, tile)), dma=True)
        P.do("sync", lambda e, t0=t0: e.dma_start(
            out=chunks(aM0, tile, KE), in_=fmslice(M0, t0, tile)), dma=True)
        P.do("sync", lambda e, t0=t0: e.dma_start(
            out=chunks(aDXY, tile, KE), in_=fmslice(DXY, t0, tile)), dma=True)
        P.do("vector", lambda e: e.tensor_tensor(ah0, ah0, aM0, ALU.mult))
        P.do("vector", lambda e: e.tensor_tensor(ah0, ah0, aDXY, ALU.add))
        for mc in range(KD):
            pt = p_mm[mc % 2][:, 0:tile]
            for kc in range(KE):
                P.do("tensor", lambda e, pt=pt, mc=mc, kc=kc: e.matmul(
                    pt, aw_out[:, kc * D + mc * 128:kc * D + (mc + 1) * 128],
                    ah0[:, kc * tile:(kc + 1) * tile],
                    start=(kc == 0), stop=(kc == KE - 1)))
            P.do("vector", lambda e, pt=pt, mc=mc: e.tensor_tensor(
                sx[:, mc * tile:(mc + 1) * tile], pt,
                sx[:, mc * tile:(mc + 1) * tile], ALU.add))
        xn2_blk = sxn2[:, it * KD * tile:(it + 1) * KD * tile]
        _layernorm_emit(P, sx, KD, tile, ones_col, ones_row, srow, p_sum[0:1, :],
                        p_ssq[0:1, :], p_mb, p_rb, scr,
                        xn2_blk.rearrange("p (c t) -> p c t", c=KD))
        P.do("gpsimd", lambda e, t0=t0: e.dma_start(
            out=fmslice(x_scr, t0, tile), in_=chunks(sx, tile, KD)), dma=True)
        for tg in range(tile // 128):
            psc = p_mm[0][:, 0:8]
            base = it * KD * tile
            for kc in range(KD):
                P.do("tensor", lambda e, kc=kc, tg=tg, psc=psc, base=base: e.matmul(
                    psc, sxn2[:, base + kc * tile + tg * 128:base + kc * tile + tg * 128 + 128],
                    ssgnT[:, kc * 8:(kc + 1) * 8],
                    start=(kc == 0), stop=(kc == KD - 1)))
            P.do("vector", lambda e, psc=psc: e.reduce_max(smax[:], psc, AX.X))
            P.do("vector", lambda e, psc=psc: e.tensor_scalar(
                smtm[:], psc, smax[:, 0:1], None, ALU.is_ge))
            P.do("tensor", lambda e: e.transpose(p_tr[0:8, :], smtm[:], sident[:]))
            P.do("vector", lambda e, t0=t0, tg=tg: e.tensor_copy(
                smask[:, t0 + tg * 128:t0 + tg * 128 + 128], p_tr[0:8, 0:128]))

    for j in range(8):
        P.do("gpsimd", lambda e, j=j: e.dma_start(
            out=chunks(awj, D, KD),
            in_=wff[j].rearrange("(c p) m -> p c m", p=128)), dma=True)
        for it in range(ntile):
            b0 = it * tile
            base = it * KD * tile
            P.do("tensor", lambda e, j=j, b0=b0: e.matmul(
                p_bc[:, 0:tile], s_sel8[:, j * 128:(j + 1) * 128],
                smask[:, b0:b0 + tile], start=True, stop=True))
            xm_v = axm[:, 0:KD * tile].rearrange("p (c t) -> p c t", c=KD)
            xn_v = sxn2[:, base:base + KD * tile].rearrange("p (c t) -> p c t", c=KD)
            bc = p_bc[:, 0:tile].unsqueeze(1).to_broadcast([128, KD, tile])
            P.do("vector", lambda e, xn_v=xn_v, bc=bc, xm_v=xm_v: e.tensor_tensor(
                xm_v, xn_v, bc, ALU.mult))
            for mc in range(KD):
                pt = p_mm[mc % 2][:, 0:tile]
                for kc in range(KD):
                    P.do("tensor", lambda e, pt=pt, mc=mc, kc=kc: e.matmul(
                        pt, awj[:, kc * D + mc * 128:kc * D + (mc + 1) * 128],
                        axm[:, kc * tile:(kc + 1) * tile],
                        start=(kc == 0), stop=(kc == KD - 1)))
                accs = sacc[:, mc * TC + b0:mc * TC + b0 + tile]
                if j == 0:
                    P.do("vector", lambda e, pt=pt, accs=accs: e.tensor_copy(accs, pt))
                else:
                    P.do("vector", lambda e, pt=pt, accs=accs: e.tensor_tensor(
                        accs, pt, accs, ALU.add))

    P.do("gpsimd", lambda e: e.dma_start(
        out=chunks(aipT, 2 * E1, KD), in_=ipT.rearrange("(c p) m -> p c m", p=128)), dma=True)
    for it in range(ntile):
        t0 = it * tile
        P.do("sync", lambda e, t0=t0: e.dma_start(
            out=chunks(sx, tile, KD), in_=fmslice(x_scr, t0, tile)), dma=True)
        for mc in range(KD):
            pt = p_mm[mc % 2][:, 0:tile]
            P.do("tensor", lambda e, pt=pt, mc=mc, t0=t0: e.matmul(
                pt, sbff[:, mc * 128:(mc + 1) * 128], smask[:, t0:t0 + tile],
                start=True, stop=True))
            accs = sacc[:, mc * TC + t0:mc * TC + t0 + tile]
            sxs = sx[:, mc * tile:(mc + 1) * tile]
            P.do("vector", lambda e, sxs=sxs, accs=accs: e.tensor_tensor(
                sxs, accs, sxs, ALU.add))
            P.do("vector", lambda e, sxs=sxs, pt=pt: e.tensor_tensor(
                sxs, pt, sxs, ALU.add))
        _layernorm_emit(P, sx, KD, tile, ones_col, ones_row, srow, p_sum[0:1, :],
                        p_ssq[0:1, :], p_mb, p_rb, scr,
                        aln3.rearrange("p (c t) -> p c t", c=KD))
        P.do("gpsimd", lambda e, t0=t0: e.dma_start(
            out=fmslice(o_ln3, t0, tile), in_=chunks(aln3, tile, KD)), dma=True)
        _layernorm_emit(P, aln3, KD, tile, ones_col, ones_row, srow, p_sum[0:1, :],
                        p_ssq[0:1, :], p_mb, p_rb, scr,
                        axn3.rearrange("p (c t) -> p c t", c=KD),
                        eps=1.1920928955078125e-07, sub_mean=False)
        for mc in range(2 * KE):
            pt = p_mm[mc % 2][:, 0:tile]
            for kc in range(KD):
                P.do("tensor", lambda e, pt=pt, mc=mc, kc=kc: e.matmul(
                    pt, aipT[:, kc * 2 * E1 + mc * 128:kc * 2 * E1 + (mc + 1) * 128],
                    axn3[:, kc * tile:(kc + 1) * tile],
                    start=(kc == 0), stop=(kc == KD - 1)))
            if mc < KE:
                P.do("vector", lambda e, pt=pt, mc=mc: e.tensor_copy(
                    axin[:, mc * tile:(mc + 1) * tile], pt))
            else:
                P.do("scalar", lambda e, pt=pt, mc=mc: e.activation(
                    asz1[:, (mc - KE) * tile:(mc - KE + 1) * tile], pt, AF.Silu))
        P.do("gpsimd", lambda e, t0=t0: e.dma_start(
            out=fmslice(o_xin, t0, tile), in_=chunks(axin, tile, KE)), dma=True)
        P.do("gpsimd", lambda e, t0=t0: e.dma_start(
            out=fmslice(o_sz1, t0, tile), in_=chunks(asz1, tile, KE)), dma=True)
        P.do("gpsimd", lambda e, t0=t0: e.dma_start(
            out=fmslice(o_x, t0, tile), in_=chunks(sx, tile, KD)), dma=True)
    return P.finish()



def build_L4(tile=512):
    P = Prog("L4")
    ntile = TC // tile
    xin_ext = P.din("xin_ext", [E1, TC + 3], BF16)
    cw = P.din("cw", [128, KE * 4])
    cb = P.din("cb", [128, KE])
    dvec1 = P.din("dvec1", [128, KE])
    dtb = P.din("dtb", [128, KE])
    sz1 = P.din("sz1", [E1, TC], BF16)
    xpT = P.din("xpT", [E1, 96])
    dtpT = P.din("dtpT", [64, E1])
    o_dt = P.dout("o_dt", [E1, TC])
    o_w = P.dout("o_w", [E1, TC])
    o_B = P.dout("o_B", [16, TC])
    o_C = P.dout("o_C", [16, TC])
    o_xcd = P.dout("o_xcd", [E1, TC], BF16)

    s_cw = P.sbuf("s_cw", [128, KE * 4], F32)
    s_cb = P.sbuf("s_cb", [128, KE], F32)
    s_dv = P.sbuf("s_dv", [128, KE], F32)
    s_db = P.sbuf("s_db", [128, KE], F32)
    s_xpT = P.sbuf("s_xpT", [128, KE * 96], BF16)
    s_dtpT = P.sbuf("s_dtpT", [64, E1], BF16)
    s_win = P.sbuf("s_win", [128, KE * (tile + 3)], F32)
    s_sz = P.sbuf("s_sz", [128, KE * tile], BF16)
    s_acc = P.sbuf("s_acc", [128, KE * tile], F32)
    s_xc = P.sbuf("s_xc", [128, KE * tile], F32)
    s_xcb = P.sbuf("s_xcb", [128, KE * tile], BF16)
    s_dt = P.sbuf("s_dt", [128, KE * tile], F32)
    s_dl = P.sbuf("s_dl", [64, tile], BF16)
    s_bc16 = P.sbuf("s_bc16", [32, tile], F32)
    s_t3 = P.sbuf("s_t3", [128, tile], F32)
    s_t4 = P.sbuf("s_t4", [128, tile], F32)
    s_xcd = P.sbuf("s_xcd", [128, KE * tile], BF16)
    p_xd = P.psum("p_xd", [96, tile])
    p_mm = [P.psum(f"p_mm{i}", [128, tile]) for i in range(2)]

    P.do("sync", lambda e: e.dma_start(out=s_cw[:], in_=cw), dma=True)
    P.do("sync", lambda e: e.dma_start(out=s_cb[:], in_=cb), dma=True)
    P.do("sync", lambda e: e.dma_start(out=s_dv[:], in_=dvec1), dma=True)
    P.do("sync", lambda e: e.dma_start(out=s_db[:], in_=dtb), dma=True)
    P.do("gpsimd", lambda e: e.dma_start(
        out=chunks(s_xpT, 96, KE), in_=xpT.rearrange("(c p) m -> p c m", p=128)), dma=True)
    P.do("gpsimd", lambda e: e.dma_start(out=s_dtpT[:], in_=dtpT), dma=True)
    c_m1 = P.const(-1.0)
    c_p1 = P.const(1.0)
    for it in range(ntile):
        t0 = it * tile
        P.do("gpsimd", lambda e, t0=t0: e.dma_start(
            out=chunks(s_win, tile + 3, KE),
            in_=xin_ext.rearrange("(c p) t -> p c t", p=128)[:, :, t0:t0 + tile + 3]), dma=True)
        P.do("sync", lambda e, t0=t0: e.dma_start(
            out=chunks(s_sz, tile, KE), in_=fmslice(sz1, t0, tile)), dma=True)
        for kc in range(KE):
            accs = s_acc[:, kc * tile:(kc + 1) * tile]
            win = lambda k, kc=kc: s_win[:, kc * (tile + 3) + k:kc * (tile + 3) + k + tile]
            P.do("vector", lambda e, accs=accs, kc=kc, win=win: e.tensor_scalar(
                accs, win(0), s_cw[:, kc * 4:kc * 4 + 1], None, ALU.mult))
            for k in range(1, 4):
                P.do("vector", lambda e, accs=accs, kc=kc, k=k, win=win: e.scalar_tensor_tensor(
                    accs, win(k), s_cw[:, kc * 4 + k:kc * 4 + k + 1], accs,
                    ALU.mult, ALU.add))
            P.do("scalar", lambda e, accs=accs, kc=kc: e.activation(
                s_xc[:, kc * tile:(kc + 1) * tile], accs, AF.Silu,
                bias=s_cb[:, kc:kc + 1]))
        P.do("vector", lambda e: e.tensor_copy(s_xcb[:], s_xc[:]))
        for kc in range(KE):
            P.do("tensor", lambda e, kc=kc: e.matmul(
                p_xd[:], s_xpT[:, kc * 96:(kc + 1) * 96],
                s_xcb[:, kc * tile:(kc + 1) * tile],
                start=(kc == 0), stop=(kc == KE - 1)))
        P.do("vector", lambda e: e.tensor_copy(s_dl[:], p_xd[0:64, :]))
        P.do("vector", lambda e: e.tensor_copy(s_bc16[:], p_xd[64:96, :]))
        P.do("gpsimd", lambda e, t0=t0: e.dma_start(
            out=o_B[:, t0:t0 + tile], in_=s_bc16[0:16, :]), dma=True)
        P.do("gpsimd", lambda e, t0=t0: e.dma_start(
            out=o_C[:, t0:t0 + tile], in_=s_bc16[16:32, :]), dma=True)
        for mc in range(KE):
            pt = p_mm[mc % 2]
            P.do("tensor", lambda e, pt=pt, mc=mc: e.matmul(
                pt[:], s_dtpT[:, mc * 128:(mc + 1) * 128], s_dl[:],
                start=True, stop=True))
            P.do("scalar", lambda e, pt=pt, mc=mc: e.activation(
                s_t3[:], pt[:], AF.Abs, bias=s_db[:, mc:mc + 1]))
            P.do("scalar", lambda e: e.activation(s_t3[:], s_t3[:], AF.Exp,
                                                  bias=0.0, scale=c_m1))
            P.do("scalar", lambda e: e.activation(s_t3[:], s_t3[:], AF.Ln, bias=c_p1))
            P.do("vector", lambda e, pt=pt, mc=mc: e.tensor_scalar(
                s_t4[:], pt[:], s_db[:, mc:mc + 1], 0.0, ALU.add, ALU.max))
            P.do("vector", lambda e, mc=mc: e.tensor_tensor(
                s_dt[:, mc * tile:(mc + 1) * tile], s_t3[:], s_t4[:], ALU.add))
        P.do("vector", lambda e: e.tensor_tensor(s_acc[:], s_dt[:], s_xc[:], ALU.mult))
        for kc in range(KE):
            P.do("vector", lambda e, kc=kc: e.scalar_tensor_tensor(
                s_xcd[:, kc * tile:(kc + 1) * tile],
                s_xc[:, kc * tile:(kc + 1) * tile], s_dv[:, kc:kc + 1],
                s_sz[:, kc * tile:(kc + 1) * tile], ALU.mult, ALU.mult))
        P.do("gpsimd", lambda e, t0=t0: e.dma_start(
            out=fmslice(o_dt, t0, tile), in_=chunks(s_dt, tile, KE)), dma=True)
        P.do("gpsimd", lambda e, t0=t0: e.dma_start(
            out=fmslice(o_w, t0, tile), in_=chunks(s_acc, tile, KE)), dma=True)
        P.do("gpsimd", lambda e, t0=t0: e.dma_start(
            out=fmslice(o_xcd, t0, tile), in_=chunks(s_xcd, tile, KE)), dma=True)
    return P.finish()


def build_L5(slab=512):
    P = Prog("L5")
    ns = T_ALL // slab
    a_e = P.din("a_e", [128, 2 * S1])
    dt_e = P.din("dt_e", [EC, T_ALL])
    w_e = P.din("w_e", [EC, T_ALL])
    Bsm = P.din("Bsm", [S1, T_ALL])
    Csm = P.din("Csm", [S1, T_ALL])
    sel = P.din("sel", [S1, S1 * 128])
    o_ys = P.dout("o_ys", [EC, T_ALL])

    s_A = P.sbuf("s_A", [128, 2 * S1], F32)
    s_dt = P.sbuf("s_dt", [128, 2 * slab], F32)
    s_w = P.sbuf("s_w", [128, 2 * slab], F32)
    s_B16 = P.sbuf("s_B16", [S1, slab], F32)
    s_C16 = P.sbuf("s_C16", [S1, slab], F32)
    s_sel = P.sbuf("s_sel", [S1, S1 * 128], F32)
    s_Bb = P.sbuf("s_Bb", [128, S1 * slab], F32)
    s_Cb = P.sbuf("s_Cb", [128, S1 * slab], F32)
    s_dA = P.sbuf("s_dA", [128, S1 * slab], F32)
    s_P = P.sbuf("s_P", [128, S1 * slab], F32)
    s_h = P.sbuf("s_h", [128, S1 * slab], F32)
    s_y = P.sbuf("s_y", [128, 2 * slab], F32)
    s_cy = P.sbuf("s_cy", [128, 2 * S1], F32)
    ones_row = P.sbuf("ones_row", [1, 128], F32)
    p_bc = P.psum("p_bc", [128, slab])

    P.do("vector", lambda e: e.memset(ones_row[:], 1.0))
    P.do("sync", lambda e: e.dma_start(out=s_sel[:], in_=sel), dma=True)
    P.do("sync", lambda e: e.dma_start(out=s_A[:], in_=a_e), dma=True)
    P.do("scalar", lambda e: e.activation(s_A[:], s_A[:], AF.Exp))
    P.do("vector", lambda e: e.tensor_scalar(s_A[:], s_A[:], -1.0, None, ALU.mult))
    nb = slab // 512
    for sl in range(ns):
        t0 = sl * slab
        P.do("sync", lambda e, t0=t0: e.dma_start(
            out=chunks(s_dt, slab, 2),
            in_=dt_e.rearrange("(c p) t -> p c t", p=128)[:, :, t0:t0 + slab]), dma=True)
        P.do("sync", lambda e, t0=t0: e.dma_start(
            out=chunks(s_w, slab, 2),
            in_=w_e.rearrange("(c p) t -> p c t", p=128)[:, :, t0:t0 + slab]), dma=True)
        P.do("sync", lambda e, t0=t0: e.dma_start(
            out=s_B16[:], in_=Bsm[:, t0:t0 + slab]), dma=True)
        P.do("sync", lambda e, t0=t0: e.dma_start(
            out=s_C16[:], in_=Csm[:, t0:t0 + slab]), dma=True)
        for s in range(S1):
            for b in range(nb):
                P.do("tensor", lambda e, s=s, b=b: e.matmul(
                    p_bc[:, 0:512], s_sel[:, s * 128:(s + 1) * 128],
                    s_B16[:, b * 512:(b + 1) * 512], start=True, stop=True))
                P.do("vector", lambda e, s=s, b=b: e.tensor_copy(
                    s_Bb[:, s * slab + b * 512:s * slab + (b + 1) * 512], p_bc[:, 0:512]))
                P.do("tensor", lambda e, s=s, b=b: e.matmul(
                    p_bc[:, 0:512], s_sel[:, s * 128:(s + 1) * 128],
                    s_C16[:, b * 512:(b + 1) * 512], start=True, stop=True))
                P.do("vector", lambda e, s=s, b=b: e.tensor_copy(
                    s_Cb[:, s * slab + b * 512:s * slab + (b + 1) * 512], p_bc[:, 0:512]))
        for ch in range(2):
            dts = s_dt[:, ch * slab:(ch + 1) * slab]
            ws = s_w[:, ch * slab:(ch + 1) * slab]
            for s in range(S1):
                P.do("scalar", lambda e, s=s, ch=ch, dts=dts: e.activation(
                    s_dA[:, s * slab:(s + 1) * slab], dts, AF.Exp, bias=0.0,
                    scale=s_A[:, ch * S1 + s:ch * S1 + s + 1]))
            wv = ws.unsqueeze(1).to_broadcast([128, S1, slab])
            P.do("vector", lambda e, wv=wv: e.tensor_tensor(
                chunks(s_P, slab, S1), wv, chunks(s_Bb, slab, S1), ALU.mult))
            for s in range(S1):
                init = 0.0 if sl == 0 else s_cy[:, ch * S1 + s:ch * S1 + s + 1]
                P.do("vector", lambda e, s=s, init=init: e.tensor_tensor_scan(
                    s_h[:, s * slab:(s + 1) * slab], s_dA[:, s * slab:(s + 1) * slab],
                    s_P[:, s * slab:(s + 1) * slab], init, ALU.mult, ALU.add))
            P.do("vector", lambda e, ch=ch: e.tensor_copy(
                s_cy[:, ch * S1:(ch + 1) * S1],
                s_h[:].rearrange("p (s t) -> p s t", s=S1)[:, :, slab - 1:slab]))
            P.do("vector", lambda e: e.tensor_tensor(
                chunks(s_dA, slab, S1), chunks(s_h, slab, S1),
                chunks(s_Cb, slab, S1), ALU.mult))
            P.do("vector", lambda e, ch=ch: e.tensor_reduce(
                s_y[:, ch * slab:(ch + 1) * slab],
                s_dA[:].rearrange("p (s t) -> p t s", s=S1), AX.X, ALU.add))
        P.do("gpsimd", lambda e, t0=t0: e.dma_start(
            out=o_ys.rearrange("(c p) t -> p c t", p=128)[:, :, t0:t0 + slab],
            in_=chunks(s_y, slab, 2)), dma=True)
    return P.finish()


def build_L6(tile=256):
    P = Prog("L6")
    ntile = TC // tile
    x3 = P.din("x3", [D, TC])
    ln3 = P.din("ln3", [D, TC], BF16)
    ys = P.din("ys", [E1, TC])
    M1 = P.din("M1", [E1, TC], BF16)
    xcd = P.din("xcd", [E1, TC], BF16)
    opT = P.din("opT", [E1, D])
    cfcT = P.din("cfcT", [D, 2 * D])
    cprT = P.din("cprT", [2 * D, D])
    o_x6 = P.dout("o_x6", [D, TC])

    s_opT = P.sbuf("s_opT", [128, KE * D], BF16)
    s_cfcT = P.sbuf("s_cfcT", [128, KD * 2 * D], BF16)
    s_cprT = P.sbuf("s_cprT", [128, KE * D], BF16)
    s_y = P.sbuf("s_y", [128, KE * tile], BF16)
    s_m1 = P.sbuf("s_m1", [128, KE * tile], BF16)
    s_xcd = P.sbuf("s_xcd", [128, KE * tile], BF16)
    s_ln3 = P.sbuf("s_ln3", [128, KD * tile], BF16)
    sx = P.sbuf("sx", [128, KD * tile], F32)
    s_xi1 = P.sbuf("s_xi1", [128, KD * tile], F32)
    scr = P.sbuf("scr", [128, KD * tile], F32)
    s_h1 = P.sbuf("s_h1", [128, KD * tile], BF16)
    s_hh = P.sbuf("s_hh", [128, KE * tile], BF16)
    srow = P.sbuf("srow", [1, 4 * tile], F32)
    ones_col = P.sbuf("ones_col", [128, 1], F32)
    ones_row = P.sbuf("ones_row", [1, 128], F32)
    p_sum = P.psum("p_sum", [1, tile])
    p_ssq = P.psum("p_ssq", [1, tile])
    p_mb = P.psum("p_mb", [128, tile])
    p_rb = P.psum("p_rb", [128, tile])
    p_mm = [P.psum(f"p_mm{i}", [128, tile]) for i in range(2)]

    P.do("vector", lambda e: e.memset(ones_col[:], 1.0))
    P.do("vector", lambda e: e.memset(ones_row[:], 1.0))
    P.do("gpsimd", lambda e: e.dma_start(
        out=chunks(s_opT, D, KE), in_=opT.rearrange("(c p) m -> p c m", p=128)), dma=True)
    P.do("gpsimd", lambda e: e.dma_start(
        out=chunks(s_cfcT, 2 * D, KD), in_=cfcT.rearrange("(c p) m -> p c m", p=128)), dma=True)
    P.do("gpsimd", lambda e: e.dma_start(
        out=chunks(s_cprT, D, KE), in_=cprT.rearrange("(c p) m -> p c m", p=128)), dma=True)
    for it in range(ntile):
        t0 = it * tile
        P.do("sync", lambda e, t0=t0: e.dma_start(
            out=chunks(sx, tile, KD), in_=fmslice(x3, t0, tile)), dma=True)
        P.do("sync", lambda e, t0=t0: e.dma_start(
            out=chunks(s_ln3, tile, KD), in_=fmslice(ln3, t0, tile)), dma=True)
        P.do("gpsimd", lambda e, t0=t0: e.dma_start(
            out=chunks(s_y, tile, KE), in_=fmslice(ys, t0, tile)), dma=True)
        P.do("sync", lambda e, t0=t0: e.dma_start(
            out=chunks(s_m1, tile, KE), in_=fmslice(M1, t0, tile)), dma=True)
        P.do("sync", lambda e, t0=t0: e.dma_start(
            out=chunks(s_xcd, tile, KE), in_=fmslice(xcd, t0, tile)), dma=True)
        P.do("vector", lambda e: e.tensor_tensor(s_y[:], s_y[:], s_m1[:], ALU.mult))
        P.do("vector", lambda e: e.tensor_tensor(s_y[:], s_y[:], s_xcd[:], ALU.add))
        for mc in range(KD):
            pt = p_mm[mc % 2]
            for kc in range(KE):
                P.do("tensor", lambda e, pt=pt, mc=mc, kc=kc: e.matmul(
                    pt[:], s_opT[:, kc * D + mc * 128:kc * D + (mc + 1) * 128],
                    s_y[:, kc * tile:(kc + 1) * tile],
                    start=(kc == 0), stop=(kc == KE - 1)))
            P.do("vector", lambda e, pt=pt, mc=mc: e.tensor_tensor(
                s_xi1[:, mc * tile:(mc + 1) * tile], pt[:],
                s_ln3[:, mc * tile:(mc + 1) * tile], ALU.add))
            P.do("vector", lambda e, mc=mc: e.tensor_tensor(
                sx[:, mc * tile:(mc + 1) * tile], s_xi1[:, mc * tile:(mc + 1) * tile],
                sx[:, mc * tile:(mc + 1) * tile], ALU.add))
        _layernorm_emit(P, s_xi1, KD, tile, ones_col, ones_row, srow, p_sum[0:1, :],
                        p_ssq[0:1, :], p_mb, p_rb, scr, chunks(s_h1, tile, KD),
                        eps=1.1920928955078125e-07, sub_mean=False)
        for mc in range(2 * KD):
            pt = p_mm[mc % 2]
            for kc in range(KD):
                P.do("tensor", lambda e, pt=pt, mc=mc, kc=kc: e.matmul(
                    pt[:], s_cfcT[:, kc * 2 * D + mc * 128:kc * 2 * D + (mc + 1) * 128],
                    s_h1[:, kc * tile:(kc + 1) * tile],
                    start=(kc == 0), stop=(kc == KD - 1)))
            P.do("scalar", lambda e, pt=pt: e.activation(
                scr[:, 0:tile], pt[:], AF.Relu))
            P.do("scalar", lambda e, mc=mc: e.activation(
                s_hh[:, mc * tile:(mc + 1) * tile], scr[:, 0:tile], AF.Square))
        for mc in range(KD):
            pt = p_mm[mc % 2]
            for kc in range(KE):
                P.do("tensor", lambda e, pt=pt, mc=mc, kc=kc: e.matmul(
                    pt[:], s_cprT[:, kc * D + mc * 128:kc * D + (mc + 1) * 128],
                    s_hh[:, kc * tile:(kc + 1) * tile],
                    start=(kc == 0), stop=(kc == KE - 1)))
            P.do("vector", lambda e, pt=pt, mc=mc: e.tensor_tensor(
                sx[:, mc * tile:(mc + 1) * tile], pt[:],
                sx[:, mc * tile:(mc + 1) * tile], ALU.add))
        P.do("gpsimd", lambda e, t0=t0: e.dma_start(
            out=fmslice(o_x6, t0, tile), in_=chunks(sx, tile, KD)), dma=True)
    return P.finish()


def build_L7(tile=512):
    P = Prog("L7")
    ntile = TC // tile
    x6 = P.din("x6", [D, TC])
    xhalo = P.din("xhalo", [128, KD])
    murT = P.din("murT", [128, KD])
    mukT = P.din("mukT", [128, KD])
    wrT = P.din("wrT", [D, D])
    wkT = P.din("wkT", [D, D])
    wvT = P.din("wvT", [D, D])
    o_xf = P.dout("o_xf", [D, TC])

    s_wr = P.sbuf("s_wr", [128, KD * D], BF16)
    s_wk = P.sbuf("s_wk", [128, KD * D], BF16)
    s_wv = P.sbuf("s_wv", [128, KD * D], BF16)
    s_mur = P.sbuf("s_mur", [128, KD], F32)
    s_muk = P.sbuf("s_muk", [128, KD], F32)
    s_murm = P.sbuf("s_murm", [128, KD], F32)
    s_mukm = P.sbuf("s_mukm", [128, KD], F32)
    s_halo = P.sbuf("s_halo", [128, KD], F32)
    s_ln4p = P.sbuf("s_ln4p", [128, KD], F32)
    sx = P.sbuf("sx", [128, KD * tile], F32)
    scr = P.sbuf("scr", [128, KD * tile], F32)
    s_ln4 = P.sbuf("s_ln4", [128, KD * tile], F32)
    s_prev = P.sbuf("s_prev", [128, KD * tile], F32)
    s_xr = P.sbuf("s_xr", [128, KD * tile], BF16)
    s_xk = P.sbuf("s_xk", [128, KD * tile], BF16)
    s_r = P.sbuf("s_r", [128, KD * tile], BF16)
    s_k = P.sbuf("s_k", [128, KD * tile], BF16)
    s_t = P.sbuf("s_t", [128, tile], F32)
    srow = P.sbuf("srow", [1, 4 * tile], F32)
    ones_col = P.sbuf("ones_col", [128, 1], F32)
    ones_row = P.sbuf("ones_row", [1, 128], F32)
    p_sum = P.psum("p_sum", [1, tile])
    p_ssq = P.psum("p_ssq", [1, tile])
    p_mb = P.psum("p_mb", [128, tile])
    p_rb = P.psum("p_rb", [128, tile])
    p_mm = [P.psum(f"p_mm{i}", [128, tile]) for i in range(2)]

    P.do("vector", lambda e: e.memset(ones_col[:], 1.0))
    P.do("vector", lambda e: e.memset(ones_row[:], 1.0))
    P.do("sync", lambda e: e.dma_start(out=s_mur[:], in_=murT), dma=True)
    P.do("sync", lambda e: e.dma_start(out=s_muk[:], in_=mukT), dma=True)
    P.do("sync", lambda e: e.dma_start(out=s_halo[:], in_=xhalo), dma=True)
    P.do("gpsimd", lambda e: e.dma_start(
        out=chunks(s_wr, D, KD), in_=wrT.rearrange("(c p) m -> p c m", p=128)), dma=True)
    P.do("gpsimd", lambda e: e.dma_start(
        out=chunks(s_wk, D, KD), in_=wkT.rearrange("(c p) m -> p c m", p=128)), dma=True)
    P.do("gpsimd", lambda e: e.dma_start(
        out=chunks(s_wv, D, KD), in_=wvT.rearrange("(c p) m -> p c m", p=128)), dma=True)
    P.do("vector", lambda e: e.tensor_scalar(s_murm[:], s_mur[:], -1.0, 1.0, ALU.mult, ALU.add))
    P.do("vector", lambda e: e.tensor_scalar(s_mukm[:], s_muk[:], -1.0, 1.0, ALU.mult, ALU.add))
    P.do("scalar", lambda e: e.square(scr[:, 0:KD], s_halo[:]))
    for kc in range(KD):
        P.do("tensor", lambda e, kc=kc: e.matmul(
            p_sum[0:1, 0:1], ones_col[:], s_halo[:, kc:kc + 1],
            start=(kc == 0), stop=(kc == KD - 1)))
        P.do("tensor", lambda e, kc=kc: e.matmul(
            p_ssq[0:1, 0:1], ones_col[:], scr[:, kc:kc + 1],
            start=(kc == 0), stop=(kc == KD - 1)))
    r0 = srow[0:1, 0:1]
    r1 = srow[0:1, 1:2]
    c_eps = P.const(1e-5, 1)
    c_mh = P.const(-0.5, 1)
    P.do("vector", lambda e: e.tensor_scalar(r0, p_sum[0:1, 0:1], 1.0 / D, None, ALU.mult))
    P.do("vector", lambda e: e.tensor_scalar(r1, p_ssq[0:1, 0:1], 1.0 / D, None, ALU.mult))
    P.do("vector", lambda e: e.scalar_tensor_tensor(r1, r0, 0.0, r1, ALU.mult, ALU.add)) if False else None
    P.do("vector", lambda e: e.tensor_tensor(srow[0:1, 2:3], r0, r0, ALU.mult))
    P.do("vector", lambda e: e.tensor_tensor(r1, r1, srow[0:1, 2:3], ALU.subtract))
    P.do("scalar", lambda e: e.activation(r1, r1, AF.Ln, bias=c_eps))
    P.do("scalar", lambda e: e.activation(r1, r1, AF.Exp, bias=0.0, scale=c_mh))
    P.do("tensor", lambda e: e.matmul(p_mb[:, 0:1], ones_row[:], r0, start=True, stop=True))
    P.do("tensor", lambda e: e.matmul(p_rb[:, 0:1], ones_row[:], r1, start=True, stop=True))
    mbb = p_mb[:, 0:1].unsqueeze(1).to_broadcast([128, KD, 1])
    rbb = p_rb[:, 0:1].unsqueeze(1).to_broadcast([128, KD, 1])
    hv = s_halo[:].unsqueeze(2)
    pv = s_ln4p[:].unsqueeze(2)
    P.do("vector", lambda e: e.tensor_tensor(hv, hv, mbb, ALU.subtract))
    P.do("vector", lambda e: e.tensor_tensor(pv, hv, rbb, ALU.mult))

    for it in range(ntile):
        t0 = it * tile
        P.do("sync", lambda e, t0=t0: e.dma_start(
            out=chunks(sx, tile, KD), in_=fmslice(x6, t0, tile)), dma=True)
        _layernorm_emit(P, sx, KD, tile, ones_col, ones_row, srow, p_sum[0:1, :],
                        p_ssq[0:1, :], p_mb, p_rb, scr, chunks(s_ln4, tile, KD))
        P.do("vector", lambda e: e.tensor_copy(
            chunks(s_prev, tile, KD)[:, :, 1:tile],
            chunks(s_ln4, tile, KD)[:, :, 0:tile - 1]))
        P.do("vector", lambda e: e.tensor_copy(
            chunks(s_prev, tile, KD)[:, :, 0:1], s_ln4p[:].unsqueeze(2)))
        P.do("vector", lambda e: e.tensor_copy(
            s_ln4p[:].unsqueeze(2), chunks(s_ln4, tile, KD)[:, :, tile - 1:tile]))
        for kc in range(KD):
            ln = s_ln4[:, kc * tile:(kc + 1) * tile]
            pr = s_prev[:, kc * tile:(kc + 1) * tile]
            P.do("vector", lambda e, kc=kc, ln=ln: e.tensor_scalar(
                s_t[:], ln, s_mur[:, kc:kc + 1], None, ALU.mult))
            P.do("vector", lambda e, kc=kc, pr=pr: e.scalar_tensor_tensor(
                s_xr[:, kc * tile:(kc + 1) * tile], pr, s_murm[:, kc:kc + 1],
                s_t[:], ALU.mult, ALU.add))
            P.do("vector", lambda e, kc=kc, ln=ln: e.tensor_scalar(
                s_t[:], ln, s_muk[:, kc:kc + 1], None, ALU.mult))
            P.do("vector", lambda e, kc=kc, pr=pr: e.scalar_tensor_tensor(
                s_xk[:, kc * tile:(kc + 1) * tile], pr, s_mukm[:, kc:kc + 1],
                s_t[:], ALU.mult, ALU.add))
        for mc in range(KD):
            pt = p_mm[mc % 2]
            for kc in range(KD):
                P.do("tensor", lambda e, pt=pt, mc=mc, kc=kc: e.matmul(
                    pt[:], s_wr[:, kc * D + mc * 128:kc * D + (mc + 1) * 128],
                    s_xr[:, kc * tile:(kc + 1) * tile],
                    start=(kc == 0), stop=(kc == KD - 1)))
            P.do("scalar", lambda e, pt=pt, mc=mc: e.activation(
                s_r[:, mc * tile:(mc + 1) * tile], pt[:], AF.Sigmoid))
        for mc in range(KD):
            pt = p_mm[mc % 2]
            for kc in range(KD):
                P.do("tensor", lambda e, pt=pt, mc=mc, kc=kc: e.matmul(
                    pt[:], s_wk[:, kc * D + mc * 128:kc * D + (mc + 1) * 128],
                    s_xk[:, kc * tile:(kc + 1) * tile],
                    start=(kc == 0), stop=(kc == KD - 1)))
            P.do("scalar", lambda e, pt=pt: e.activation(s_t[:], pt[:], AF.Relu))
            P.do("scalar", lambda e, mc=mc: e.activation(
                s_k[:, mc * tile:(mc + 1) * tile], s_t[:], AF.Square))
        for mc in range(KD):
            pt = p_mm[mc % 2]
            for kc in range(KD):
                P.do("tensor", lambda e, pt=pt, mc=mc, kc=kc: e.matmul(
                    pt[:], s_wv[:, kc * D + mc * 128:kc * D + (mc + 1) * 128],
                    s_k[:, kc * tile:(kc + 1) * tile],
                    start=(kc == 0), stop=(kc == KD - 1)))
            P.do("vector", lambda e, pt=pt, mc=mc: e.tensor_tensor(
                s_t[:], pt[:], s_r[:, mc * tile:(mc + 1) * tile], ALU.mult))
            P.do("vector", lambda e, mc=mc: e.tensor_tensor(
                sx[:, mc * tile:(mc + 1) * tile], s_t[:],
                sx[:, mc * tile:(mc + 1) * tile], ALU.add))
        P.do("gpsimd", lambda e, t0=t0: e.dma_start(
            out=fmslice(o_xf, t0, tile), in_=chunks(sx, tile, KD)), dma=True)
    return P.finish()


def _to_fm(x2d):
    return np.ascontiguousarray(x2d.T)


_NC_CACHE = {}


def _nc(name, builder):
    if name not in _NC_CACHE:
        _NC_CACHE[name] = builder()
    return _NC_CACHE[name]


def _col(v):
    n = v.shape[0] // 128
    return np.ascontiguousarray(v.reshape(n, 128).T)


def _run(nc, in_maps):
    res = run_bass_kernel_spmd(nc, in_maps, core_ids=list(range(NCORES)))
    return res.results


def kernel(**inputs):
    f32 = np.float32
    x_full = inputs["L0_attn_x"][0]
    xfm = [np.ascontiguousarray(x_full[c * TC:(c + 1) * TC].T) for c in range(NCORES)]

    im1 = []
    d_vec = _col(inputs["L0_attn_D"].astype(f32))
    a_log = np.array(inputs["L0_attn_A_log"], f32).reshape(1, 1)
    for c in range(NCORES):
        im1.append({
            "xfm": xfm[c],
            "w_in": inputs["L0_attn_W_in"],
            "w_xp": inputs["L0_attn_W_xp"],
            "w_b": inputs["L0_attn_W_B"],
            "w_c": inputs["L0_attn_W_C"],
            "d_vec": d_vec,
            "a_log": a_log,
        })
    r1 = _run(_nc("L1", build_L1), im1)

    U_all = np.concatenate([r["o_U"] for r in r1], axis=1)
    abar_all = np.concatenate([r["o_abar"][0] for r in r1])
    ab_b = np.ascontiguousarray(np.broadcast_to(abar_all[None, :], (128, T_ALL)))
    im2 = [{"u_e": np.ascontiguousarray(U_all[c * EC:(c + 1) * EC]),
            "ab": ab_b} for c in range(NCORES)]
    r2 = _run(_nc("L2", build_L2), im2)

    h_all = np.concatenate([r["o_h"] for r in r2], axis=0)
    ident = np.eye(128, dtype=BF)
    sel8 = np.zeros((8, 8 * 128), dtype=BF)
    for j in range(8):
        sel8[j, j * 128:(j + 1) * 128] = 1
    im3 = []
    for c in range(NCORES):
        im3.append({
            "xfm": xfm[c],
            "h0": np.ascontiguousarray(h_all[:, c * TC:(c + 1) * TC]),
            "M0": r1[c]["o_M0"],
            "DXY": r1[c]["o_DXY"],
            "w_out": inputs["L0_attn_W_out"],
            "tsig": inputs["L0_ffn_tile_sigs"],
            "wff": inputs["L0_ffn_W"],
            "bff": inputs["L0_ffn_b"],
            "ipT": np.ascontiguousarray(inputs["L1_attn_in_proj_w"].T),
            "ident": ident,
            "sel8": sel8,
        })
    r3 = _run(_nc("L3", build_L3), im3)

    xin_all = np.concatenate([r["o_xin"] for r in r3], axis=1)
    cw = np.ascontiguousarray(
        inputs["L1_attn_conv_w"].reshape(KE, 128, 4).transpose(1, 0, 2).reshape(128, KE * 4))
    cb = _col(inputs["L1_attn_conv_b"])
    dvec1 = _col(inputs["L1_attn_D"])
    dtb = _col(inputs["L1_attn_dt_proj_b"])
    im4 = []
    for c in range(NCORES):
        if c == 0:
            halo = np.zeros((E1, 3), dtype=BF)
        else:
            halo = xin_all[:, c * TC - 3:c * TC]
        im4.append({
            "xin_ext": np.ascontiguousarray(
                np.concatenate([halo, xin_all[:, c * TC:(c + 1) * TC]], axis=1)),
            "cw": cw, "cb": cb, "dvec1": dvec1, "dtb": dtb,
            "sz1": r3[c]["o_sz1"],
            "xpT": np.ascontiguousarray(inputs["L1_attn_x_proj_w"].T),
            "dtpT": np.ascontiguousarray(inputs["L1_attn_dt_proj_w"].T),
        })
    r4 = _run(_nc("L4", build_L4), im4)

    dt_all = np.concatenate([r["o_dt"] for r in r4], axis=1)
    w_all = np.concatenate([r["o_w"] for r in r4], axis=1)
    Bsm = np.concatenate([r["o_B"] for r in r4], axis=1)
    Csm = np.concatenate([r["o_C"] for r in r4], axis=1)
    sel = np.zeros((S1, S1 * 128), dtype=f32)
    for s in range(S1):
        sel[s, s * 128:(s + 1) * 128] = 1
    im5 = []
    for c in range(NCORES):
        a_e = np.ascontiguousarray(
            inputs["L1_attn_A_log"][c * EC:(c + 1) * EC].reshape(2, 128, S1)
            .transpose(1, 0, 2).reshape(128, 2 * S1))
        im5.append({
            "a_e": a_e,
            "dt_e": np.ascontiguousarray(dt_all[c * EC:(c + 1) * EC]),
            "w_e": np.ascontiguousarray(w_all[c * EC:(c + 1) * EC]),
            "Bsm": Bsm, "Csm": Csm, "sel": sel,
        })
    r5 = _run(_nc("L5", build_L5), im5)

    ys_all = np.concatenate([r["o_ys"] for r in r5], axis=0)
    im6 = []
    for c in range(NCORES):
        im6.append({
            "x3": r3[c]["o_x"],
            "ln3": r3[c]["o_ln3"],
            "ys": np.ascontiguousarray(ys_all[:, c * TC:(c + 1) * TC]),
            "M1": r3[c]["o_sz1"],
            "xcd": r4[c]["o_xcd"],
            "opT": np.ascontiguousarray(inputs["L1_attn_out_proj_w"].T),
            "cfcT": np.ascontiguousarray(inputs["L1_attn_c_fc_w"].T),
            "cprT": np.ascontiguousarray(inputs["L1_attn_c_proj_w"].T),
        })
    r6 = _run(_nc("L6", build_L6), im6)

    murT = _col(inputs["L1_ffn_mu_r"])
    mukT = _col(inputs["L1_ffn_mu_k"])
    im7 = []
    for c in range(NCORES):
        if c == 0:
            xhalo = np.zeros((128, KD), f32)
        else:
            xhalo = np.ascontiguousarray(
                r6[c - 1]["o_x6"][:, TC - 1].reshape(KD, 128).T)
        im7.append({
            "x6": r6[c]["o_x6"],
            "xhalo": xhalo,
            "murT": murT, "mukT": mukT,
            "wrT": np.ascontiguousarray(inputs["L1_ffn_W_r"].T),
            "wkT": np.ascontiguousarray(inputs["L1_ffn_W_k"].T),
            "wvT": np.ascontiguousarray(inputs["L1_ffn_W_v"].T),
        })
    r7 = _run(_nc("L7", build_L7), im7)

    out = np.empty((T_ALL, D), f32)
    for c in range(NCORES):
        out[c * TC:(c + 1) * TC] = r7[c]["o_xf"].T
    return out[None]


# version 03
# speedup vs baseline: 1.3205x; 1.1892x over previous
import numpy as np
import ml_dtypes
from contextlib import ExitStack

import concourse.bass as bass
import concourse.mybir as mybir
from concourse.bass_utils import run_bass_kernel_spmd

F32 = mybir.dt.float32
BF16 = mybir.dt.bfloat16
AF = mybir.ActivationFunctionType
ALU = mybir.AluOpType
AX = mybir.AxisListType
BF = ml_dtypes.bfloat16

NCORES = 8
T_ALL = 16384
TC = T_ALL // NCORES
D = 1024
E0 = 2048
E1 = 2048
S1 = 16
EC = E1 // NCORES
KD = D // 128
KE = E0 // 128


class Prog:

    ENGINES = ("sync", "gpsimd", "vector", "scalar", "tensor")

    def __init__(self, name="L"):
        self.nc = bass.Bass("TRN2", target_bir_lowering=False)
        self.stk = ExitStack()
        self.items = []

    def do(self, eng, fn, dma=False):
        assert eng in self.ENGINES
        self.items.append((eng, fn, 16 if dma else 1))

    def const(self, val, p=128):
        assert not getattr(self, "_finishing", False), "const() inside emit lambda"
        if not hasattr(self, "_ctile"):
            self._ctile = self.stk.enter_context(
                self.nc.sbuf_tensor("consts", [128, 16], F32))
            self._cmap = {}
        if val not in self._cmap:
            col = len(self._cmap)
            assert col < 16
            self._cmap[val] = col
            self.do("vector", lambda e, col=col, val=val: e.memset(
                self._ctile[:, col:col + 1], float(val)))
        col = self._cmap[val]
        return self._ctile[0:p, col:col + 1]

    ARENA_COLS = 50 * 1024

    def _ensure_arena(self):
        if not hasattr(self, "_arena"):
            self._arena = self.stk.enter_context(
                self.nc.sbuf_tensor("ARENA", [128, self.ARENA_COLS], F32))
            self._arena_off = 0
            self._pbanks = [self.stk.enter_context(
                self.nc.psum_tensor(f"PB{i}", [128, 512], F32)) for i in range(8)]
            self._pboff = 0

    def new_phase(self):
        self._ensure_arena()
        self._arena_off = 0
        self._pboff = 0

    def sbuf(self, name, shape, dt):
        self._ensure_arena()
        p, cols = shape
        f32cols = cols if dt == F32 else (cols + 1) // 2
        f32cols = (f32cols + 3) & ~3
        assert self._arena_off + f32cols <= self.ARENA_COLS,             f"arena overflow at {name}: {self._arena_off}+{f32cols}"
        base = self._arena_off
        self._arena_off += f32cols
        view = self._arena[0:p, base:base + f32cols]
        if dt == BF16:
            view = view.bitcast(BF16)[:, 0:cols]
        elif dt != F32:
            view = view.bitcast(dt)[:, 0:cols]
        else:
            view = view[:, 0:cols]
        return _SbufView(view)

    def psum(self, name, shape, dt=F32):
        self._ensure_arena()
        assert self._pboff < 8, "psum pool exhausted"
        bank = self._pbanks[self._pboff]
        self._pboff += 1
        p, cols = shape
        if dt == F32:
            return _SbufView(bank[0:p, 0:cols])
        return _SbufView(bank[:].bitcast(dt)[0:p, 0:cols])

    def din(self, name, shape, dt=F32):
        return self.nc.dram_tensor(name, shape, dt, kind="ExternalInput").ap()

    def dout(self, name, shape, dt=F32):
        return self.nc.dram_tensor(name, shape, dt, kind="ExternalOutput").ap()

    def finish(self):
        self._finishing = True
        g = self.stk.enter_context(self.nc.semaphore("g"))
        pre = []
        tot = 0
        for (_, _, amt) in self.items:
            pre.append(tot)
            tot += amt
        with self.stk:
            blk = self.stk.enter_context(self.nc.Block())
            for engname in self.ENGINES:
                def body(eng, engname=engname):
                    covered = 0
                    own_since = 0
                    elide = engname in ("tensor", "sync", "gpsimd")
                    for (e, fn, amt), w in zip(self.items, pre):
                        if e != engname:
                            continue
                        if (w - own_since > covered) if elide else (w > covered):
                            eng.wait_ge(g, w)
                            covered = w
                            own_since = 0
                        instr = fn(eng)
                        instr.then_inc(g, amt)
                        own_since += amt
                    if engname == "gpsimd":
                        eng.wait_ge(g, tot)
                getattr(blk, engname)(body)
        return self.nc


class _SbufView:

    def __init__(self, ap):
        self._ap = ap

    def __getitem__(self, key):
        return self._ap[key]

    def ap(self):
        return self._ap


def fmslice(hbm_ap, t0, tn, p=128):
    return hbm_ap.rearrange("(c p) t -> p c t", p=p)[:, :, t0:t0 + tn]


def chunks(sb, tile, n):
    return sb[:].rearrange("p (c t) -> p c t", c=n)


def _layernorm_emit(P, sx, nch, tile, ones_col, ones_row, srow, p_sum, p_ssq,
                    p_mb, p_rb, scratch, out_v, eps=1e-5, sub_mean=True):
    row_m = srow[0:1, 0 * tile:1 * tile]
    row_s = srow[0:1, 1 * tile:2 * tile]
    row_v = srow[0:1, 2 * tile:3 * tile]
    P.do("scalar", lambda e: e.square(scratch[:], sx[:]))
    if sub_mean:
        for kc in range(nch):
            P.do("tensor", lambda e, kc=kc: e.matmul(
                p_sum, ones_col[:], sx[:, kc * tile:(kc + 1) * tile],
                start=(kc == 0), stop=(kc == nch - 1)))
    for kc in range(nch):
        P.do("tensor", lambda e, kc=kc: e.matmul(
            p_ssq, ones_col[:], scratch[:, kc * tile:(kc + 1) * tile],
            start=(kc == 0), stop=(kc == nch - 1)))
    nd = nch * 128
    P.do("vector", lambda e: e.tensor_scalar(row_s, p_ssq, 1.0 / nd, None, ALU.mult))
    if sub_mean:
        P.do("vector", lambda e: e.tensor_scalar(row_m, p_sum, 1.0 / nd, None, ALU.mult))
        P.do("vector", lambda e: e.tensor_tensor(row_v, row_m, row_m, ALU.mult))
        P.do("vector", lambda e: e.tensor_tensor(row_v, row_s, row_v, ALU.subtract))
    else:
        P.do("vector", lambda e: e.tensor_copy(row_v, row_s))
    c_eps = P.const(eps, 1)
    c_mh = P.const(-0.5, 1)
    P.do("scalar", lambda e: e.activation(row_v, row_v, AF.Ln, bias=c_eps))
    P.do("scalar", lambda e: e.activation(row_v, row_v, AF.Exp, bias=0.0, scale=c_mh))
    if sub_mean:
        P.do("tensor", lambda e: e.matmul(p_mb[:], ones_row[:], row_m, start=True, stop=True))
    P.do("tensor", lambda e: e.matmul(p_rb[:], ones_row[:], row_v, start=True, stop=True))
    xv = sx[:].rearrange("p (c t) -> p c t", c=nch)
    sv = chunks(scratch, tile, nch)
    ov = out_v
    rb = p_rb[:].unsqueeze(1).to_broadcast([128, nch, tile])
    if sub_mean:
        mb = p_mb[:].unsqueeze(1).to_broadcast([128, nch, tile])
        P.do("vector", lambda e: e.tensor_tensor(sv, xv, mb, ALU.subtract))
        P.do("vector", lambda e: e.tensor_tensor(ov, sv, rb, ALU.mult))
    else:
        P.do("vector", lambda e: e.tensor_tensor(ov, xv, rb, ALU.mult))


def emit_L1(P, io, t_base=0, tile=256):
    P.new_phase()
    ntile = TC // tile
    xfm = io["xfm"]
    w_in = io["w_in"]
    w_xp = io["w_xp"]
    w_b = io["w_b"]
    w_c = io["w_c"]
    d_vec = io["d_vec"]
    a_log = io["a_log"]
    o_U = io["o_U"]
    o_abar = io["o_abar"]
    o_M0 = io["o_M0"]
    o_DXY = io["o_DXY"]

    sw_in = P.sbuf("sw_in", [128, KD * 2 * E0], BF16)
    sw_xp = P.sbuf("sw_xp", [128, KE * 129], BF16)
    sw_b = P.sbuf("sw_b", [64, E0], BF16)
    sw_c = P.sbuf("sw_c", [64, E0], BF16)
    sd = P.sbuf("sd", [128, KE], F32)
    sal = P.sbuf("sal", [1, 1], F32)
    s_A = P.sbuf("s_A", [1, 1], F32)
    ones_col = P.sbuf("ones_col", [128, 1], F32)
    ones_row = P.sbuf("ones_row", [1, 128], F32)
    sx = P.sbuf("sx", [128, KD * tile], F32)
    scr = P.sbuf("scr", [128, KD * tile], F32)
    sxn = P.sbuf("sxn", [128, KD * tile], BF16)
    sxi = P.sbuf("sxi", [128, KE * tile], BF16)
    ssz = P.sbuf("ssz", [128, KE * tile], BF16)
    sU = P.sbuf("sU", [128, KE * tile], F32)
    sM0 = P.sbuf("sM0", [128, KE * tile], BF16)
    sDXY = P.sbuf("sDXY", [128, KE * tile], BF16)
    sbc = P.sbuf("sbc", [64, tile], BF16)
    scc = P.sbuf("scc", [64, tile], BF16)
    srow = P.sbuf("srow", [1, 8 * tile], F32)
    sabar = P.sbuf("sabar", [1, TC], F32)
    p_stat = P.psum("p_stat", [1, 2 * tile])
    p_mb = P.psum("p_mb", [128, tile])
    p_rb = P.psum("p_rb", [128, tile])
    p_mm = [P.psum(f"p_mm{i}", [128, tile]) for i in range(2)]
    p_dr = P.psum("p_dr", [1, tile])
    p_br = P.psum("p_br", [64, tile])
    p_cr = P.psum("p_cr", [64, tile])

    def row(i):
        return srow[0:1, i * tile:(i + 1) * tile]

    P.do("vector", lambda e: e.memset(ones_col[:], 1.0))
    P.do("vector", lambda e: e.memset(ones_row[:], 1.0))
    P.do("gpsimd", lambda e: e.dma_start(
        out=chunks(sw_in, 2 * E0, KD), in_=w_in.rearrange("(c p) m -> p c m", p=128)), dma=True)
    P.do("gpsimd", lambda e: e.dma_start(
        out=chunks(sw_xp, 129, KE), in_=w_xp.rearrange("(c p) m -> p c m", p=128)), dma=True)
    P.do("gpsimd", lambda e: e.dma_start(out=sw_b[:], in_=w_b), dma=True)
    P.do("gpsimd", lambda e: e.dma_start(out=sw_c[:], in_=w_c), dma=True)
    P.do("sync", lambda e: e.dma_start(out=sd[:], in_=d_vec), dma=True)
    P.do("sync", lambda e: e.dma_start(out=sal[:], in_=a_log), dma=True)
    P.do("scalar", lambda e: e.activation(s_A[:], sal[:], AF.Exp))
    P.do("vector", lambda e: e.tensor_scalar(s_A[:], s_A[:], -1.0, None, ALU.mult))

    for it in range(ntile):
        t0 = it * tile
        P.do("sync", lambda e, t0=t0: e.dma_start(
            out=chunks(sx, tile, KD), in_=fmslice(xfm, t_base + t0, tile)), dma=True)
        _layernorm_emit(P, sx, KD, tile, ones_col, ones_row, srow,
                        p_stat[0:1, 0:tile], p_stat[0:1, tile:2 * tile],
                        p_mb, p_rb, scr, chunks(sxn, tile, KD))
        for mc in range(2 * KE):
            pt = p_mm[mc % 2]
            for kc in range(KD):
                P.do("tensor", lambda e, pt=pt, mc=mc, kc=kc: e.matmul(
                    pt[:], sw_in[:, kc * 2 * E0 + mc * 128: kc * 2 * E0 + (mc + 1) * 128],
                    sxn[:, kc * tile:(kc + 1) * tile],
                    start=(kc == 0), stop=(kc == KD - 1)))
            if mc < KE:
                P.do("vector", lambda e, pt=pt, mc=mc: e.tensor_copy(
                    sxi[:, mc * tile:(mc + 1) * tile], pt[:]))
            else:
                P.do("scalar", lambda e, pt=pt, mc=mc: e.activation(
                    ssz[:, (mc - KE) * tile:(mc - KE + 1) * tile], pt[:], AF.Silu))
        for kc in range(KE):
            P.do("tensor", lambda e, kc=kc: e.matmul(
                p_dr[:], sw_xp[:, kc * 129:kc * 129 + 1],
                sxi[:, kc * tile:(kc + 1) * tile],
                start=(kc == 0), stop=(kc == KE - 1)))
        for kc in range(KE):
            P.do("tensor", lambda e, kc=kc: e.matmul(
                p_br[:], sw_xp[:, kc * 129 + 1:kc * 129 + 65],
                sxi[:, kc * tile:(kc + 1) * tile],
                start=(kc == 0), stop=(kc == KE - 1)))
        for kc in range(KE):
            P.do("tensor", lambda e, kc=kc: e.matmul(
                p_cr[:], sw_xp[:, kc * 129 + 65:kc * 129 + 129],
                sxi[:, kc * tile:(kc + 1) * tile],
                start=(kc == 0), stop=(kc == KE - 1)))
        P.do("vector", lambda e: e.tensor_copy(sbc[:], p_br[:]))
        P.do("vector", lambda e: e.tensor_copy(scc[:], p_cr[:]))
        c_m1 = P.const(-1.0, 1)
        c_p1 = P.const(1.0, 1)
        P.do("scalar", lambda e: e.activation(row(3), p_dr[:], AF.Abs))
        P.do("scalar", lambda e, c_m1=c_m1: e.activation(row(3), row(3), AF.Exp, bias=0.0, scale=c_m1))
        P.do("scalar", lambda e, c_p1=c_p1: e.activation(row(3), row(3), AF.Ln, bias=c_p1))
        P.do("vector", lambda e: e.tensor_scalar(row(4), p_dr[:], 0.0, None, ALU.max))
        P.do("vector", lambda e: e.tensor_tensor(row(4), row(4), row(3), ALU.add))
        P.do("vector", lambda e: e.tensor_scalar(row(4), row(4), s_A[0:1, 0:1], None, ALU.mult))
        P.do("scalar", lambda e, t0=t0: e.activation(
            sabar[0:1, t0:t0 + tile], row(4), AF.Exp))
        for mc in range(KE):
            P.do("tensor", lambda e, mc=mc: e.matmul(
                p_mm[0][:], sw_b[:, mc * 128:(mc + 1) * 128], sbc[:], start=True, stop=True))
            P.do("vector", lambda e, mc=mc: e.tensor_tensor(
                sU[:, mc * tile:(mc + 1) * tile], p_mm[0][:],
                sxi[:, mc * tile:(mc + 1) * tile], ALU.mult))
            P.do("tensor", lambda e, mc=mc: e.matmul(
                p_mm[1][:], sw_c[:, mc * 128:(mc + 1) * 128], scc[:], start=True, stop=True))
            P.do("vector", lambda e, mc=mc: e.tensor_tensor(
                sM0[:, mc * tile:(mc + 1) * tile], p_mm[1][:],
                ssz[:, mc * tile:(mc + 1) * tile], ALU.mult))
            P.do("vector", lambda e, mc=mc: e.scalar_tensor_tensor(
                sDXY[:, mc * tile:(mc + 1) * tile], sxi[:, mc * tile:(mc + 1) * tile],
                sd[:, mc:mc + 1], ssz[:, mc * tile:(mc + 1) * tile], ALU.mult, ALU.mult))
        P.do("gpsimd", lambda e, t0=t0: e.dma_start(
            out=fmslice(o_U, t_base + t0, tile), in_=chunks(sU, tile, KE)), dma=True)
        P.do("gpsimd", lambda e, t0=t0: e.dma_start(
            out=fmslice(o_M0, t_base + t0, tile), in_=chunks(sM0, tile, KE)), dma=True)
        P.do("gpsimd", lambda e, t0=t0: e.dma_start(
            out=fmslice(o_DXY, t_base + t0, tile), in_=chunks(sDXY, tile, KE)), dma=True)
    P.do("gpsimd", lambda e: e.dma_start(out=o_abar[0:1, t_base:t_base + TC], in_=sabar[:]), dma=True)



def build_L1(tile=256):
    P = Prog("L1")
    io = {
        "xfm": P.din("xfm", [D, TC]),
        "w_in": P.din("w_in", [D, 2 * E0]),
        "w_xp": P.din("w_xp", [E0, 129]),
        "w_b": P.din("w_b", [64, E0]),
        "w_c": P.din("w_c", [64, E0]),
        "d_vec": P.din("d_vec", [128, KE]),
        "a_log": P.din("a_log", [1, 1]),
        "o_U": P.dout("o_U", [E0, TC]),
        "o_abar": P.dout("o_abar", [1, TC]),
        "o_M0": P.dout("o_M0", [E0, TC], BF16),
        "o_DXY": P.dout("o_DXY", [E0, TC], BF16),
    }
    emit_L1(P, io, 0, tile)
    return P.finish()


def emit_L2(P, io, ec_base=0, slab=4096, ab_row=None):
    P.new_phase()
    ns = T_ALL // slab
    u_e = io["u_e"][ec_base:ec_base + EC, :]
    o_h = io["o_h"][ec_base:ec_base + EC, :]
    s_u = P.sbuf("s_u", [128, 2 * slab], F32)
    s_ab = P.sbuf("s_ab", [128, slab], F32)
    s_h = P.sbuf("s_h", [128, 2 * slab], F32)
    s_cy = P.sbuf("s_cy", [128, 2], F32)
    if ab_row is not None:
        s_abr = P.sbuf("s_abr", [1, slab], F32)
        ones_row = P.sbuf("ones_rowL2", [1, 128], F32)
        p_bc = P.psum("p_bcL2", [128, 512])
        P.do("vector", lambda e: e.memset(ones_row[:], 1.0))
    else:
        ab = io["ab"]
    for sl in range(ns):
        t0 = sl * slab
        P.do("sync", lambda e, t0=t0: e.dma_start(
            out=chunks(s_u, slab, 2),
            in_=u_e.rearrange("(c p) t -> p c t", p=128)[:, :, t0:t0 + slab]), dma=True)
        if ab_row is None:
            P.do("sync", lambda e, t0=t0: e.dma_start(
                out=s_ab[:], in_=ab[:, t0:t0 + slab]), dma=True)
        else:
            P.do("sync", lambda e, t0=t0: e.dma_start(
                out=s_abr[:], in_=ab_row[0:1, t0:t0 + slab]), dma=True)
            for b in range(slab // 512):
                P.do("tensor", lambda e, b=b: e.matmul(
                    p_bc[:, 0:512], ones_row[:], s_abr[0:1, b * 512:(b + 1) * 512],
                    start=True, stop=True))
                P.do("vector", lambda e, b=b: e.tensor_copy(
                    s_ab[:, b * 512:(b + 1) * 512], p_bc[:, 0:512]))
        for ch in range(2):
            init = 0.0 if sl == 0 else s_cy[:, ch:ch + 1]
            P.do("vector", lambda e, ch=ch, init=init: e.tensor_tensor_scan(
                s_h[:, ch * slab:(ch + 1) * slab], s_ab[:],
                s_u[:, ch * slab:(ch + 1) * slab], init, ALU.mult, ALU.add))
            P.do("vector", lambda e, ch=ch: e.tensor_copy(
                s_cy[:, ch:ch + 1], s_h[:, (ch + 1) * slab - 1:(ch + 1) * slab]))
        P.do("gpsimd", lambda e, t0=t0: e.dma_start(
            out=o_h.rearrange("(c p) t -> p c t", p=128)[:, :, t0:t0 + slab],
            in_=chunks(s_h, slab, 2)), dma=True)


def build_L2(slab=4096):
    P = Prog("L2")
    io = {
        "u_e": P.din("u_e", [EC, T_ALL]),
        "ab": P.din("ab", [128, T_ALL]),
        "o_h": P.dout("o_h", [EC, T_ALL], BF16),
    }
    emit_L2(P, io, 0, slab)
    return P.finish()


def emit_L3(P, io, t_base=0, tile=256, xin_off=0):
    P.new_phase()
    ntile = TC // tile
    xfm = io["xfm"]
    h0 = io["h0"]
    M0 = io["M0"]
    DXY = io["DXY"]
    w_out = io["w_out"]
    tsig = io["tsig"]
    wff = io["wff"]
    bff = io["bff"]
    ipT = io["ipT"]
    ident = io["ident"]
    sel8 = io["sel8"]
    o_x = io["o_x"]
    o_ln3 = io["o_ln3"]
    o_xin = io["o_xin"]
    o_sz1 = io["o_sz1"]
    x_scr = io["x_scr"]

    ARB = P.sbuf("ARB", [128, 48 * 1024], BF16)
    aw_out = ARB[:, 0:16 * 1024]
    ah0 = ARB[:, 16 * 1024:20 * 1024]
    aM0 = ARB[:, 20 * 1024:24 * 1024]
    aDXY = ARB[:, 24 * 1024:28 * 1024]
    awj = ARB[:, 0:8 * 1024]
    axm = ARB[:, 8 * 1024:12 * 1024]
    aipT = ARB[:, 0:32 * 1024]
    aln3 = ARB[:, 32 * 1024:34 * 1024]
    axn3 = ARB[:, 34 * 1024:36 * 1024]
    axin = ARB[:, 36 * 1024:40 * 1024]
    asz1 = ARB[:, 40 * 1024:44 * 1024]
    sxn2 = P.sbuf("sxn2", [128, KD * TC], BF16)
    sacc = P.sbuf("sacc", [128, KD * TC], BF16)
    smask = P.sbuf("smask", [8, TC], BF16)
    smtm = P.sbuf("smtm", [128, 8], BF16)
    sident = P.sbuf("sident", [128, 128], BF16)
    s_sel8 = P.sbuf("s_sel8", [8, 8 * 128], BF16)
    ssgn = P.sbuf("ssgn", [8, D], BF16)
    ssgnT = P.sbuf("ssgnT", [128, KD * 8], BF16)
    sbff = P.sbuf("sbff", [8, D], BF16)
    sx = P.sbuf("sx", [128, KD * tile], F32)
    scr = P.sbuf("scr", [128, KD * tile], F32)
    srow = P.sbuf("srow", [1, 4 * tile], F32)
    smax = P.sbuf("smax", [128, 1], F32)
    ones_col = P.sbuf("ones_col", [128, 1], F32)
    ones_row = P.sbuf("ones_row", [1, 128], F32)
    ones_rowb = P.sbuf("ones_rowb", [1, 128], BF16)
    p_sum = P.psum("p_sum", [1, tile])
    p_ssq = P.psum("p_ssq", [1, tile])
    p_mb = P.psum("p_mb", [128, tile])
    p_rb = P.psum("p_rb", [128, tile])
    p_mm = [P.psum(f"p_mm{i}", [128, 2 * tile]) for i in range(2)]
    p_tr = P.psum("p_tr", [128, 128], BF16)
    p_bc = P.psum("p_bc", [128, 2 * tile])

    P.do("vector", lambda e: e.memset(ones_col[:], 1.0))
    P.do("vector", lambda e: e.memset(ones_row[:], 1.0))
    P.do("vector", lambda e: e.memset(ones_rowb[:], 1.0))
    P.do("sync", lambda e: e.dma_start(out=sident[:], in_=ident), dma=True)
    P.do("sync", lambda e: e.dma_start(out=s_sel8[:], in_=sel8), dma=True)
    P.do("gpsimd", lambda e: e.dma_start(out=ssgn[:], in_=tsig), dma=True)
    P.do("gpsimd", lambda e: e.dma_start(out=sbff[:], in_=bff), dma=True)
    P.do("gpsimd", lambda e: e.dma_start(
        out=chunks(aw_out, D, KE), in_=w_out.rearrange("(c p) m -> p c m", p=128)), dma=True)
    P.do("scalar", lambda e: e.sign(ssgn[:], ssgn[:]))
    for kc in range(KD):
        P.do("tensor", lambda e, kc=kc: e.transpose(
            p_tr[:, 0:8], ssgn[:, kc * 128:(kc + 1) * 128], sident[0:8, 0:8]))
        P.do("vector", lambda e, kc=kc: e.tensor_copy(
            ssgnT[:, kc * 8:(kc + 1) * 8], p_tr[:, 0:8]))

    for it in range(ntile):
        t0 = it * tile
        P.do("sync", lambda e, t0=t0: e.dma_start(
            out=chunks(sx, tile, KD), in_=fmslice(xfm, t_base + t0, tile)), dma=True)
        P.do("sync", lambda e, t0=t0: e.dma_start(
            out=chunks(ah0, tile, KE), in_=fmslice(h0, t_base + t0, tile)), dma=True)
        P.do("sync", lambda e, t0=t0: e.dma_start(
            out=chunks(aM0, tile, KE), in_=fmslice(M0, t_base + t0, tile)), dma=True)
        P.do("sync", lambda e, t0=t0: e.dma_start(
            out=chunks(aDXY, tile, KE), in_=fmslice(DXY, t_base + t0, tile)), dma=True)
        P.do("vector", lambda e: e.tensor_tensor(ah0, ah0, aM0, ALU.mult))
        P.do("vector", lambda e: e.tensor_tensor(ah0, ah0, aDXY, ALU.add))
        for mc in range(KD):
            pt = p_mm[mc % 2][:, 0:tile]
            for kc in range(KE):
                P.do("tensor", lambda e, pt=pt, mc=mc, kc=kc: e.matmul(
                    pt, aw_out[:, kc * D + mc * 128:kc * D + (mc + 1) * 128],
                    ah0[:, kc * tile:(kc + 1) * tile],
                    start=(kc == 0), stop=(kc == KE - 1)))
            P.do("vector", lambda e, pt=pt, mc=mc: e.tensor_tensor(
                sx[:, mc * tile:(mc + 1) * tile], pt,
                sx[:, mc * tile:(mc + 1) * tile], ALU.add))
        xn2_blk = sxn2[:, it * KD * tile:(it + 1) * KD * tile]
        _layernorm_emit(P, sx, KD, tile, ones_col, ones_row, srow, p_sum[0:1, :],
                        p_ssq[0:1, :], p_mb, p_rb, scr,
                        xn2_blk.rearrange("p (c t) -> p c t", c=KD))
        P.do("gpsimd", lambda e, t0=t0: e.dma_start(
            out=fmslice(x_scr, t0, tile), in_=chunks(sx, tile, KD)), dma=True)
        for tg in range(tile // 128):
            psc = p_mm[0][:, 0:8]
            base = it * KD * tile
            for kc in range(KD):
                P.do("tensor", lambda e, kc=kc, tg=tg, psc=psc, base=base: e.matmul(
                    psc, sxn2[:, base + kc * tile + tg * 128:base + kc * tile + tg * 128 + 128],
                    ssgnT[:, kc * 8:(kc + 1) * 8],
                    start=(kc == 0), stop=(kc == KD - 1)))
            P.do("vector", lambda e, psc=psc: e.reduce_max(smax[:], psc, AX.X))
            P.do("vector", lambda e, psc=psc: e.tensor_scalar(
                smtm[:], psc, smax[:, 0:1], None, ALU.is_ge))
            P.do("tensor", lambda e: e.transpose(p_tr[0:8, :], smtm[:], sident[:]))
            P.do("vector", lambda e, t0=t0, tg=tg: e.tensor_copy(
                smask[:, t0 + tg * 128:t0 + tg * 128 + 128], p_tr[0:8, 0:128]))

    for j in range(8):
        P.do("gpsimd", lambda e, j=j: e.dma_start(
            out=chunks(awj, D, KD),
            in_=wff[j].rearrange("(c p) m -> p c m", p=128)), dma=True)
        for it in range(ntile):
            b0 = it * tile
            base = it * KD * tile
            P.do("tensor", lambda e, j=j, b0=b0: e.matmul(
                p_bc[:, 0:tile], s_sel8[:, j * 128:(j + 1) * 128],
                smask[:, b0:b0 + tile], start=True, stop=True))
            xm_v = axm[:, 0:KD * tile].rearrange("p (c t) -> p c t", c=KD)
            xn_v = sxn2[:, base:base + KD * tile].rearrange("p (c t) -> p c t", c=KD)
            bc = p_bc[:, 0:tile].unsqueeze(1).to_broadcast([128, KD, tile])
            P.do("vector", lambda e, xn_v=xn_v, bc=bc, xm_v=xm_v: e.tensor_tensor(
                xm_v, xn_v, bc, ALU.mult))
            for mc in range(KD):
                pt = p_mm[mc % 2][:, 0:tile]
                for kc in range(KD):
                    P.do("tensor", lambda e, pt=pt, mc=mc, kc=kc: e.matmul(
                        pt, awj[:, kc * D + mc * 128:kc * D + (mc + 1) * 128],
                        axm[:, kc * tile:(kc + 1) * tile],
                        start=(kc == 0), stop=(kc == KD - 1)))
                accs = sacc[:, mc * TC + b0:mc * TC + b0 + tile]
                if j == 0:
                    P.do("vector", lambda e, pt=pt, accs=accs: e.tensor_copy(accs, pt))
                else:
                    P.do("vector", lambda e, pt=pt, accs=accs: e.tensor_tensor(
                        accs, pt, accs, ALU.add))

    P.do("gpsimd", lambda e: e.dma_start(
        out=chunks(aipT, 2 * E1, KD), in_=ipT.rearrange("(c p) m -> p c m", p=128)), dma=True)
    for it in range(ntile):
        t0 = it * tile
        P.do("sync", lambda e, t0=t0: e.dma_start(
            out=chunks(sx, tile, KD), in_=fmslice(x_scr, t0, tile)), dma=True)
        for mc in range(KD):
            pt = p_mm[mc % 2][:, 0:tile]
            P.do("tensor", lambda e, pt=pt, mc=mc, t0=t0: e.matmul(
                pt, sbff[:, mc * 128:(mc + 1) * 128], smask[:, t0:t0 + tile],
                start=True, stop=True))
            accs = sacc[:, mc * TC + t0:mc * TC + t0 + tile]
            sxs = sx[:, mc * tile:(mc + 1) * tile]
            P.do("vector", lambda e, sxs=sxs, accs=accs: e.tensor_tensor(
                sxs, accs, sxs, ALU.add))
            P.do("vector", lambda e, sxs=sxs, pt=pt: e.tensor_tensor(
                sxs, pt, sxs, ALU.add))
        _layernorm_emit(P, sx, KD, tile, ones_col, ones_row, srow, p_sum[0:1, :],
                        p_ssq[0:1, :], p_mb, p_rb, scr,
                        aln3.rearrange("p (c t) -> p c t", c=KD))
        P.do("gpsimd", lambda e, t0=t0: e.dma_start(
            out=fmslice(o_ln3, t_base + t0, tile), in_=chunks(aln3, tile, KD)), dma=True)
        _layernorm_emit(P, aln3, KD, tile, ones_col, ones_row, srow, p_sum[0:1, :],
                        p_ssq[0:1, :], p_mb, p_rb, scr,
                        axn3.rearrange("p (c t) -> p c t", c=KD),
                        eps=1.1920928955078125e-07, sub_mean=False)
        for mc in range(2 * KE):
            pt = p_mm[mc % 2][:, 0:tile]
            for kc in range(KD):
                P.do("tensor", lambda e, pt=pt, mc=mc, kc=kc: e.matmul(
                    pt, aipT[:, kc * 2 * E1 + mc * 128:kc * 2 * E1 + (mc + 1) * 128],
                    axn3[:, kc * tile:(kc + 1) * tile],
                    start=(kc == 0), stop=(kc == KD - 1)))
            if mc < KE:
                P.do("vector", lambda e, pt=pt, mc=mc: e.tensor_copy(
                    axin[:, mc * tile:(mc + 1) * tile], pt))
            else:
                P.do("scalar", lambda e, pt=pt, mc=mc: e.activation(
                    asz1[:, (mc - KE) * tile:(mc - KE + 1) * tile], pt, AF.Silu))
        P.do("gpsimd", lambda e, t0=t0: e.dma_start(
            out=fmslice(o_xin, t_base + t0 + xin_off, tile), in_=chunks(axin, tile, KE)), dma=True)
        P.do("gpsimd", lambda e, t0=t0: e.dma_start(
            out=fmslice(o_sz1, t_base + t0, tile), in_=chunks(asz1, tile, KE)), dma=True)
        P.do("gpsimd", lambda e, t0=t0: e.dma_start(
            out=fmslice(o_x, t_base + t0, tile), in_=chunks(sx, tile, KD)), dma=True)



def build_L3(tile=256):
    P = Prog("L3")
    io = {
        "xfm": P.din("xfm", [D, TC]),
        "h0": P.din("h0", [E0, TC], BF16),
        "M0": P.din("M0", [E0, TC], BF16),
        "DXY": P.din("DXY", [E0, TC], BF16),
        "w_out": P.din("w_out", [E0, D]),
        "tsig": P.din("tsig", [8, D]),
        "wff": P.din("wff", [8, D, D]),
        "bff": P.din("bff", [8, D]),
        "ipT": P.din("ipT", [D, 2 * E1]),
        "ident": P.din("ident", [128, 128], BF16),
        "sel8": P.din("sel8", [8, 8 * 128], BF16),
        "o_x": P.dout("o_x", [D, TC]),
        "o_ln3": P.dout("o_ln3", [D, TC], BF16),
        "o_xin": P.dout("o_xin", [E1, TC], BF16),
        "o_sz1": P.dout("o_sz1", [E1, TC], BF16),
        "x_scr": P.nc.dram_tensor("x_scr", [D, TC], F32, kind="Internal").ap(),
    }
    emit_L3(P, io, 0, tile)
    return P.finish()


def emit_L4(P, io, t_base=0, tile=512):
    P.new_phase()
    ntile = TC // tile
    xin_ext = io["xin_ext"]
    cw = io["cw"]
    cb = io["cb"]
    dvec1 = io["dvec1"]
    dtb = io["dtb"]
    sz1 = io["sz1"]
    xpT = io["xpT"]
    dtpT = io["dtpT"]
    o_dt = io["o_dt"]
    o_w = io["o_w"]
    o_B = io["o_B"]
    o_C = io["o_C"]
    o_xcd = io["o_xcd"]

    s_cw = P.sbuf("s_cw", [128, KE * 4], F32)
    s_cb = P.sbuf("s_cb", [128, KE], F32)
    s_dv = P.sbuf("s_dv", [128, KE], F32)
    s_db = P.sbuf("s_db", [128, KE], F32)
    s_xpT = P.sbuf("s_xpT", [128, KE * 96], BF16)
    s_dtpT = P.sbuf("s_dtpT", [64, E1], BF16)
    s_win = P.sbuf("s_win", [128, KE * (tile + 3)], F32)
    s_sz = P.sbuf("s_sz", [128, KE * tile], BF16)
    s_acc = P.sbuf("s_acc", [128, KE * tile], F32)
    s_xc = P.sbuf("s_xc", [128, KE * tile], F32)
    s_xcb = P.sbuf("s_xcb", [128, KE * tile], BF16)
    s_dt = P.sbuf("s_dt", [128, KE * tile], F32)
    s_dl = P.sbuf("s_dl", [64, tile], BF16)
    s_bc16 = P.sbuf("s_bc16", [32, tile], F32)
    s_t3 = P.sbuf("s_t3", [128, tile], F32)
    s_t4 = P.sbuf("s_t4", [128, tile], F32)
    s_xcd = P.sbuf("s_xcd", [128, KE * tile], BF16)
    p_xd = P.psum("p_xd", [96, tile])
    p_mm = [P.psum(f"p_mm{i}", [128, tile]) for i in range(2)]

    P.do("sync", lambda e: e.dma_start(out=s_cw[:], in_=cw), dma=True)
    P.do("sync", lambda e: e.dma_start(out=s_cb[:], in_=cb), dma=True)
    P.do("sync", lambda e: e.dma_start(out=s_dv[:], in_=dvec1), dma=True)
    P.do("sync", lambda e: e.dma_start(out=s_db[:], in_=dtb), dma=True)
    P.do("gpsimd", lambda e: e.dma_start(
        out=chunks(s_xpT, 96, KE), in_=xpT.rearrange("(c p) m -> p c m", p=128)), dma=True)
    P.do("gpsimd", lambda e: e.dma_start(out=s_dtpT[:], in_=dtpT), dma=True)
    c_m1 = P.const(-1.0)
    c_p1 = P.const(1.0)
    for it in range(ntile):
        t0 = it * tile
        P.do("gpsimd", lambda e, t0=t0: e.dma_start(
            out=chunks(s_win, tile + 3, KE),
            in_=xin_ext.rearrange("(c p) t -> p c t", p=128)[:, :, t_base + t0:t_base + t0 + tile + 3]), dma=True)
        P.do("sync", lambda e, t0=t0: e.dma_start(
            out=chunks(s_sz, tile, KE), in_=fmslice(sz1, t_base + t0, tile)), dma=True)
        for kc in range(KE):
            accs = s_acc[:, kc * tile:(kc + 1) * tile]
            win = lambda k, kc=kc: s_win[:, kc * (tile + 3) + k:kc * (tile + 3) + k + tile]
            P.do("vector", lambda e, accs=accs, kc=kc, win=win: e.tensor_scalar(
                accs, win(0), s_cw[:, kc * 4:kc * 4 + 1], None, ALU.mult))
            for k in range(1, 4):
                P.do("vector", lambda e, accs=accs, kc=kc, k=k, win=win: e.scalar_tensor_tensor(
                    accs, win(k), s_cw[:, kc * 4 + k:kc * 4 + k + 1], accs,
                    ALU.mult, ALU.add))
            P.do("scalar", lambda e, accs=accs, kc=kc: e.activation(
                s_xc[:, kc * tile:(kc + 1) * tile], accs, AF.Silu,
                bias=s_cb[:, kc:kc + 1]))
        P.do("vector", lambda e: e.tensor_copy(s_xcb[:], s_xc[:]))
        for kc in range(KE):
            P.do("tensor", lambda e, kc=kc: e.matmul(
                p_xd[:], s_xpT[:, kc * 96:(kc + 1) * 96],
                s_xcb[:, kc * tile:(kc + 1) * tile],
                start=(kc == 0), stop=(kc == KE - 1)))
        P.do("vector", lambda e: e.tensor_copy(s_dl[:], p_xd[0:64, :]))
        P.do("vector", lambda e: e.tensor_copy(s_bc16[:], p_xd[64:96, :]))
        P.do("gpsimd", lambda e, t0=t0: e.dma_start(
            out=o_B[:, t_base + t0:t_base + t0 + tile], in_=s_bc16[0:16, :]), dma=True)
        P.do("gpsimd", lambda e, t0=t0: e.dma_start(
            out=o_C[:, t_base + t0:t_base + t0 + tile], in_=s_bc16[16:32, :]), dma=True)
        for mc in range(KE):
            pt = p_mm[mc % 2]
            P.do("tensor", lambda e, pt=pt, mc=mc: e.matmul(
                pt[:], s_dtpT[:, mc * 128:(mc + 1) * 128], s_dl[:],
                start=True, stop=True))
            P.do("scalar", lambda e, pt=pt, mc=mc: e.activation(
                s_t3[:], pt[:], AF.Abs, bias=s_db[:, mc:mc + 1]))
            P.do("scalar", lambda e: e.activation(s_t3[:], s_t3[:], AF.Exp,
                                                  bias=0.0, scale=c_m1))
            P.do("scalar", lambda e: e.activation(s_t3[:], s_t3[:], AF.Ln, bias=c_p1))
            P.do("vector", lambda e, pt=pt, mc=mc: e.tensor_scalar(
                s_t4[:], pt[:], s_db[:, mc:mc + 1], 0.0, ALU.add, ALU.max))
            P.do("vector", lambda e, mc=mc: e.tensor_tensor(
                s_dt[:, mc * tile:(mc + 1) * tile], s_t3[:], s_t4[:], ALU.add))
        P.do("vector", lambda e: e.tensor_tensor(s_acc[:], s_dt[:], s_xc[:], ALU.mult))
        for kc in range(KE):
            P.do("vector", lambda e, kc=kc: e.scalar_tensor_tensor(
                s_xcd[:, kc * tile:(kc + 1) * tile],
                s_xc[:, kc * tile:(kc + 1) * tile], s_dv[:, kc:kc + 1],
                s_sz[:, kc * tile:(kc + 1) * tile], ALU.mult, ALU.mult))
        P.do("gpsimd", lambda e, t0=t0: e.dma_start(
            out=fmslice(o_dt, t_base + t0, tile), in_=chunks(s_dt, tile, KE)), dma=True)
        P.do("gpsimd", lambda e, t0=t0: e.dma_start(
            out=fmslice(o_w, t_base + t0, tile), in_=chunks(s_acc, tile, KE)), dma=True)
        P.do("gpsimd", lambda e, t0=t0: e.dma_start(
            out=fmslice(o_xcd, t_base + t0, tile), in_=chunks(s_xcd, tile, KE)), dma=True)


def build_L4(tile=512):
    P = Prog("L4")
    io = {
        "xin_ext": P.din("xin_ext", [E1, TC + 3], BF16),
        "cw": P.din("cw", [128, KE * 4]),
        "cb": P.din("cb", [128, KE]),
        "dvec1": P.din("dvec1", [128, KE]),
        "dtb": P.din("dtb", [128, KE]),
        "sz1": P.din("sz1", [E1, TC], BF16),
        "xpT": P.din("xpT", [E1, 96]),
        "dtpT": P.din("dtpT", [64, E1]),
        "o_dt": P.dout("o_dt", [E1, TC]),
        "o_w": P.dout("o_w", [E1, TC]),
        "o_B": P.dout("o_B", [16, TC]),
        "o_C": P.dout("o_C", [16, TC]),
        "o_xcd": P.dout("o_xcd", [E1, TC], BF16),
    }
    emit_L4(P, io, 0, tile)
    return P.finish()


def _l5_sbufs(P, slab):
    sb = {}
    sb["s_dt"] = P.sbuf("s_dt", [128, 2 * slab], F32)
    sb["s_w"] = P.sbuf("s_w", [128, 2 * slab], F32)
    sb["s_B16"] = P.sbuf("s_B16", [S1, slab], F32)
    sb["s_C16"] = P.sbuf("s_C16", [S1, slab], F32)
    sb["s_sel"] = P.sbuf("s_sel", [S1, S1 * 128], F32)
    sb["s_Bb"] = P.sbuf("s_Bb", [128, S1 * slab], F32)
    sb["s_Cb"] = P.sbuf("s_Cb", [128, S1 * slab], F32)
    sb["s_dA"] = P.sbuf("s_dA", [128, S1 * slab], F32)
    sb["s_P"] = P.sbuf("s_P", [128, S1 * slab], F32)
    sb["s_h"] = P.sbuf("s_h", [128, S1 * slab], F32)
    sb["s_t1"] = P.sbuf("s_t1", [128, S1], F32)
    sb["s_y"] = P.sbuf("s_y", [128, 2 * slab], F32)
    sb["ones_row"] = P.sbuf("ones_rowL5", [1, 128], F32)
    sb["p_bc"] = P.psum("p_bcL5", [128, 512])
    return sb


def _l5_bcast_emit(P, sb, Bsm, Csm, t0, slab):
    P.do("sync", lambda e: e.dma_start(out=sb["s_B16"][:], in_=Bsm[:, t0:t0 + slab]), dma=True)
    P.do("sync", lambda e: e.dma_start(out=sb["s_C16"][:], in_=Csm[:, t0:t0 + slab]), dma=True)
    nb = slab // 512
    for s in range(S1):
        for b in range(nb):
            P.do("tensor", lambda e, s=s, b=b: e.matmul(
                sb["p_bc"][:, 0:512], sb["s_sel"][:, s * 128:(s + 1) * 128],
                sb["s_B16"][:, b * 512:(b + 1) * 512], start=True, stop=True))
            P.do("vector", lambda e, s=s, b=b: e.tensor_copy(
                sb["s_Bb"][:, s * slab + b * 512:s * slab + (b + 1) * 512], sb["p_bc"][:, 0:512]))
            P.do("tensor", lambda e, s=s, b=b: e.matmul(
                sb["p_bc"][:, 0:512], sb["s_sel"][:, s * 128:(s + 1) * 128],
                sb["s_C16"][:, b * 512:(b + 1) * 512], start=True, stop=True))
            P.do("vector", lambda e, s=s, b=b: e.tensor_copy(
                sb["s_Cb"][:, s * slab + b * 512:s * slab + (b + 1) * 512], sb["p_bc"][:, 0:512]))


def _l5_chan_emit(P, sb, s_A, s_cy, ch, acol, cycol, slab):
    dts = sb["s_dt"][:, ch * slab:(ch + 1) * slab]
    ws = sb["s_w"][:, ch * slab:(ch + 1) * slab]
    dAv = chunks(sb["s_dA"], slab, S1)
    Pv = chunks(sb["s_P"], slab, S1)
    hv = chunks(sb["s_h"], slab, S1)
    Av = s_A[:, acol:acol + S1].unsqueeze(2).to_broadcast([128, S1, slab])
    dtv = dts.unsqueeze(1).to_broadcast([128, S1, slab])
    wv = ws.unsqueeze(1).to_broadcast([128, S1, slab])
    cyv = s_cy[:, cycol:cycol + S1]
    dA_c0 = sb["s_dA"][:].rearrange("p (s t) -> p s t", s=S1)[:, :, 0:1]
    P_c0 = sb["s_P"][:].rearrange("p (s t) -> p s t", s=S1)[:, :, 0:1]
    P.do("vector", lambda e: e.tensor_tensor(dAv, Av, dtv, ALU.mult))
    P.do("scalar", lambda e: e.activation(sb["s_dA"][:], sb["s_dA"][:], AF.Exp))
    P.do("vector", lambda e: e.tensor_tensor(Pv, wv, chunks(sb["s_Bb"], slab, S1), ALU.mult))
    P.do("vector", lambda e: e.tensor_copy(sb["s_t1"][:].unsqueeze(2), dA_c0))
    P.do("vector", lambda e: e.memset(dA_c0, 0.0))
    P.do("vector", lambda e: e.tensor_tensor(sb["s_t1"][:], sb["s_t1"][:], cyv, ALU.mult))
    P.do("vector", lambda e: e.tensor_tensor(P_c0, sb["s_t1"][:].unsqueeze(2), P_c0, ALU.add))
    P.do("vector", lambda e: e.tensor_tensor_scan(
        sb["s_h"][:], sb["s_dA"][:], sb["s_P"][:], 0.0, ALU.mult, ALU.add))
    P.do("vector", lambda e: e.tensor_copy(
        cyv, sb["s_h"][:].rearrange("p (s t) -> p s t", s=S1)[:, :, slab - 1:slab]))
    P.do("vector", lambda e: e.tensor_tensor(dAv, hv, chunks(sb["s_Cb"], slab, S1), ALU.mult))
    P.do("vector", lambda e, ch=ch: e.tensor_reduce(
        sb["s_y"][:, ch * slab:(ch + 1) * slab],
        sb["s_dA"][:].rearrange("p (s t) -> p t s", s=S1), AX.X, ALU.add))


def emit_L5(P, io, ec_base=0, a_off=0, slab=512):
    P.new_phase()
    ns = T_ALL // slab
    dt_e = io["dt_e"][ec_base:ec_base + EC, :]
    w_e = io["w_e"][ec_base:ec_base + EC, :]
    o_ys = io["o_ys"][ec_base:ec_base + EC, :]
    sb = _l5_sbufs(P, slab)
    s_A = P.sbuf("s_A", [128, 2 * S1], F32)
    s_cy = P.sbuf("s_cy", [128, 2 * S1], F32)
    P.do("vector", lambda e: e.memset(sb["ones_row"][:], 1.0))
    P.do("vector", lambda e: e.memset(s_cy[:], 0.0))
    P.do("sync", lambda e: e.dma_start(out=sb["s_sel"][:], in_=io["sel"]), dma=True)
    P.do("sync", lambda e: e.dma_start(out=s_A[:], in_=io["a_e"][:, a_off:a_off + 2 * S1]), dma=True)
    P.do("scalar", lambda e: e.activation(s_A[:], s_A[:], AF.Exp))
    P.do("vector", lambda e: e.tensor_scalar(s_A[:], s_A[:], -1.0, None, ALU.mult))
    for sl in range(ns):
        t0 = sl * slab
        P.do("sync", lambda e, t0=t0: e.dma_start(
            out=chunks(sb["s_dt"], slab, 2),
            in_=dt_e.rearrange("(c p) t -> p c t", p=128)[:, :, t0:t0 + slab]), dma=True)
        P.do("sync", lambda e, t0=t0: e.dma_start(
            out=chunks(sb["s_w"], slab, 2),
            in_=w_e.rearrange("(c p) t -> p c t", p=128)[:, :, t0:t0 + slab]), dma=True)
        _l5_bcast_emit(P, sb, io["Bsm"], io["Csm"], t0, slab)
        for ch in range(2):
            _l5_chan_emit(P, sb, s_A, s_cy, ch, ch * S1, ch * S1, slab)
        P.do("gpsimd", lambda e, t0=t0: e.dma_start(
            out=o_ys.rearrange("(c p) t -> p c t", p=128)[:, :, t0:t0 + slab],
            in_=chunks(sb["s_y"], slab, 2)), dma=True)


def emit_L5_mega(P, io, slab=512):
    P.new_phase()
    ns = T_ALL // slab
    nvc = E1 // EC
    sb = _l5_sbufs(P, slab)
    s_A = P.sbuf("s_A", [128, (E1 // 128) * S1], F32)
    s_cy = P.sbuf("s_cy", [128, (E1 // 128) * S1], F32)
    P.do("vector", lambda e: e.memset(sb["ones_row"][:], 1.0))
    P.do("vector", lambda e: e.memset(s_cy[:], 0.0))
    P.do("sync", lambda e: e.dma_start(out=sb["s_sel"][:], in_=io["sel"]), dma=True)
    P.do("sync", lambda e: e.dma_start(out=s_A[:], in_=io["a_e"]), dma=True)
    P.do("scalar", lambda e: e.activation(s_A[:], s_A[:], AF.Exp))
    P.do("vector", lambda e: e.tensor_scalar(s_A[:], s_A[:], -1.0, None, ALU.mult))
    for sl in range(ns):
        t0 = sl * slab
        _l5_bcast_emit(P, sb, io["Bsm"], io["Csm"], t0, slab)
        for vc in range(nvc):
            dt_e = io["dt_e"][vc * EC:(vc + 1) * EC, :]
            w_e = io["w_e"][vc * EC:(vc + 1) * EC, :]
            o_ys = io["o_ys"][vc * EC:(vc + 1) * EC, :]
            P.do("sync", lambda e, t0=t0, dt_e=dt_e: e.dma_start(
                out=chunks(sb["s_dt"], slab, 2),
                in_=dt_e.rearrange("(c p) t -> p c t", p=128)[:, :, t0:t0 + slab]), dma=True)
            P.do("sync", lambda e, t0=t0, w_e=w_e: e.dma_start(
                out=chunks(sb["s_w"], slab, 2),
                in_=w_e.rearrange("(c p) t -> p c t", p=128)[:, :, t0:t0 + slab]), dma=True)
            for ch in range(2):
                _l5_chan_emit(P, sb, s_A, s_cy, ch,
                              (2 * vc + ch) * S1, (2 * vc + ch) * S1, slab)
            P.do("gpsimd", lambda e, t0=t0, o_ys=o_ys: e.dma_start(
                out=o_ys.rearrange("(c p) t -> p c t", p=128)[:, :, t0:t0 + slab],
                in_=chunks(sb["s_y"], slab, 2)), dma=True)


def build_L5(slab=512):
    P = Prog("L5")
    io = {
        "a_e": P.din("a_e", [128, 2 * S1]),
        "dt_e": P.din("dt_e", [EC, T_ALL]),
        "w_e": P.din("w_e", [EC, T_ALL]),
        "Bsm": P.din("Bsm", [S1, T_ALL]),
        "Csm": P.din("Csm", [S1, T_ALL]),
        "sel": P.din("sel", [S1, S1 * 128]),
        "o_ys": P.dout("o_ys", [EC, T_ALL]),
    }
    emit_L5(P, io, 0, 0, slab)
    return P.finish()


def emit_L6(P, io, t_base=0, tile=256):
    P.new_phase()
    ntile = TC // tile
    x3 = io["x3"]
    ln3 = io["ln3"]
    ys = io["ys"]
    M1 = io["M1"]
    xcd = io["xcd"]
    opT = io["opT"]
    cfcT = io["cfcT"]
    cprT = io["cprT"]
    o_x6 = io["o_x6"]

    s_opT = P.sbuf("s_opT", [128, KE * D], BF16)
    s_cfcT = P.sbuf("s_cfcT", [128, KD * 2 * D], BF16)
    s_cprT = P.sbuf("s_cprT", [128, KE * D], BF16)
    s_y = P.sbuf("s_y", [128, KE * tile], BF16)
    s_m1 = P.sbuf("s_m1", [128, KE * tile], BF16)
    s_xcd = P.sbuf("s_xcd", [128, KE * tile], BF16)
    s_ln3 = P.sbuf("s_ln3", [128, KD * tile], BF16)
    sx = P.sbuf("sx", [128, KD * tile], F32)
    s_xi1 = P.sbuf("s_xi1", [128, KD * tile], F32)
    scr = P.sbuf("scr", [128, KD * tile], F32)
    s_h1 = P.sbuf("s_h1", [128, KD * tile], BF16)
    s_hh = P.sbuf("s_hh", [128, KE * tile], BF16)
    srow = P.sbuf("srow", [1, 4 * tile], F32)
    ones_col = P.sbuf("ones_col", [128, 1], F32)
    ones_row = P.sbuf("ones_row", [1, 128], F32)
    p_sum = P.psum("p_sum", [1, tile])
    p_ssq = P.psum("p_ssq", [1, tile])
    p_mb = P.psum("p_mb", [128, tile])
    p_rb = P.psum("p_rb", [128, tile])
    p_mm = [P.psum(f"p_mm{i}", [128, tile]) for i in range(2)]

    P.do("vector", lambda e: e.memset(ones_col[:], 1.0))
    P.do("vector", lambda e: e.memset(ones_row[:], 1.0))
    P.do("gpsimd", lambda e: e.dma_start(
        out=chunks(s_opT, D, KE), in_=opT.rearrange("(c p) m -> p c m", p=128)), dma=True)
    P.do("gpsimd", lambda e: e.dma_start(
        out=chunks(s_cfcT, 2 * D, KD), in_=cfcT.rearrange("(c p) m -> p c m", p=128)), dma=True)
    P.do("gpsimd", lambda e: e.dma_start(
        out=chunks(s_cprT, D, KE), in_=cprT.rearrange("(c p) m -> p c m", p=128)), dma=True)
    for it in range(ntile):
        t0 = it * tile
        P.do("sync", lambda e, t0=t0: e.dma_start(
            out=chunks(sx, tile, KD), in_=fmslice(x3, t_base + t0, tile)), dma=True)
        P.do("sync", lambda e, t0=t0: e.dma_start(
            out=chunks(s_ln3, tile, KD), in_=fmslice(ln3, t_base + t0, tile)), dma=True)
        P.do("gpsimd", lambda e, t0=t0: e.dma_start(
            out=chunks(s_y, tile, KE), in_=fmslice(ys, t_base + t0, tile)), dma=True)
        P.do("sync", lambda e, t0=t0: e.dma_start(
            out=chunks(s_m1, tile, KE), in_=fmslice(M1, t_base + t0, tile)), dma=True)
        P.do("sync", lambda e, t0=t0: e.dma_start(
            out=chunks(s_xcd, tile, KE), in_=fmslice(xcd, t_base + t0, tile)), dma=True)
        P.do("vector", lambda e: e.tensor_tensor(s_y[:], s_y[:], s_m1[:], ALU.mult))
        P.do("vector", lambda e: e.tensor_tensor(s_y[:], s_y[:], s_xcd[:], ALU.add))
        for mc in range(KD):
            pt = p_mm[mc % 2]
            for kc in range(KE):
                P.do("tensor", lambda e, pt=pt, mc=mc, kc=kc: e.matmul(
                    pt[:], s_opT[:, kc * D + mc * 128:kc * D + (mc + 1) * 128],
                    s_y[:, kc * tile:(kc + 1) * tile],
                    start=(kc == 0), stop=(kc == KE - 1)))
            P.do("vector", lambda e, pt=pt, mc=mc: e.tensor_tensor(
                s_xi1[:, mc * tile:(mc + 1) * tile], pt[:],
                s_ln3[:, mc * tile:(mc + 1) * tile], ALU.add))
            P.do("vector", lambda e, mc=mc: e.tensor_tensor(
                sx[:, mc * tile:(mc + 1) * tile], s_xi1[:, mc * tile:(mc + 1) * tile],
                sx[:, mc * tile:(mc + 1) * tile], ALU.add))
        _layernorm_emit(P, s_xi1, KD, tile, ones_col, ones_row, srow, p_sum[0:1, :],
                        p_ssq[0:1, :], p_mb, p_rb, scr, chunks(s_h1, tile, KD),
                        eps=1.1920928955078125e-07, sub_mean=False)
        for mc in range(2 * KD):
            pt = p_mm[mc % 2]
            for kc in range(KD):
                P.do("tensor", lambda e, pt=pt, mc=mc, kc=kc: e.matmul(
                    pt[:], s_cfcT[:, kc * 2 * D + mc * 128:kc * 2 * D + (mc + 1) * 128],
                    s_h1[:, kc * tile:(kc + 1) * tile],
                    start=(kc == 0), stop=(kc == KD - 1)))
            P.do("scalar", lambda e, pt=pt: e.activation(
                scr[:, 0:tile], pt[:], AF.Relu))
            P.do("scalar", lambda e, mc=mc: e.activation(
                s_hh[:, mc * tile:(mc + 1) * tile], scr[:, 0:tile], AF.Square))
        for mc in range(KD):
            pt = p_mm[mc % 2]
            for kc in range(KE):
                P.do("tensor", lambda e, pt=pt, mc=mc, kc=kc: e.matmul(
                    pt[:], s_cprT[:, kc * D + mc * 128:kc * D + (mc + 1) * 128],
                    s_hh[:, kc * tile:(kc + 1) * tile],
                    start=(kc == 0), stop=(kc == KE - 1)))
            P.do("vector", lambda e, pt=pt, mc=mc: e.tensor_tensor(
                sx[:, mc * tile:(mc + 1) * tile], pt[:],
                sx[:, mc * tile:(mc + 1) * tile], ALU.add))
        P.do("gpsimd", lambda e, t0=t0: e.dma_start(
            out=fmslice(o_x6, t_base + t0, tile), in_=chunks(sx, tile, KD)), dma=True)


def build_L6(tile=256):
    P = Prog("L6")
    io = {
        "x3": P.din("x3", [D, TC]),
        "ln3": P.din("ln3", [D, TC], BF16),
        "ys": P.din("ys", [E1, TC]),
        "M1": P.din("M1", [E1, TC], BF16),
        "xcd": P.din("xcd", [E1, TC], BF16),
        "opT": P.din("opT", [E1, D]),
        "cfcT": P.din("cfcT", [D, 2 * D]),
        "cprT": P.din("cprT", [2 * D, D]),
        "o_x6": P.dout("o_x6", [D, TC]),
    }
    emit_L6(P, io, 0, tile)
    return P.finish()


def emit_L7(P, io, t_base=0, tile=512, halo_col=None, halo_zero=False):
    P.new_phase()
    ntile = TC // tile
    x6 = io["x6"]
    murT = io["murT"]
    mukT = io["mukT"]
    wrT = io["wrT"]
    wkT = io["wkT"]
    wvT = io["wvT"]
    o_xf = io["o_xf"]

    s_wr = P.sbuf("s_wr", [128, KD * D], BF16)
    s_wk = P.sbuf("s_wk", [128, KD * D], BF16)
    s_wv = P.sbuf("s_wv", [128, KD * D], BF16)
    s_mur = P.sbuf("s_mur", [128, KD], F32)
    s_muk = P.sbuf("s_muk", [128, KD], F32)
    s_murm = P.sbuf("s_murm", [128, KD], F32)
    s_mukm = P.sbuf("s_mukm", [128, KD], F32)
    s_halo = P.sbuf("s_halo", [128, KD], F32)
    s_ln4p = P.sbuf("s_ln4p", [128, KD], F32)
    sx = P.sbuf("sx", [128, KD * tile], F32)
    scr = P.sbuf("scr", [128, KD * tile], F32)
    s_ln4 = P.sbuf("s_ln4", [128, KD * tile], F32)
    s_prev = P.sbuf("s_prev", [128, KD * tile], F32)
    s_xr = P.sbuf("s_xr", [128, KD * tile], BF16)
    s_xk = P.sbuf("s_xk", [128, KD * tile], BF16)
    s_r = P.sbuf("s_r", [128, KD * tile], BF16)
    s_k = P.sbuf("s_k", [128, KD * tile], BF16)
    s_t = P.sbuf("s_t", [128, tile], F32)
    srow = P.sbuf("srow", [1, 4 * tile], F32)
    ones_col = P.sbuf("ones_col", [128, 1], F32)
    ones_row = P.sbuf("ones_row", [1, 128], F32)
    p_sum = P.psum("p_sum", [1, tile])
    p_ssq = P.psum("p_ssq", [1, tile])
    p_mb = P.psum("p_mb", [128, tile])
    p_rb = P.psum("p_rb", [128, tile])
    p_mm = [P.psum(f"p_mm{i}", [128, tile]) for i in range(2)]

    P.do("vector", lambda e: e.memset(ones_col[:], 1.0))
    P.do("vector", lambda e: e.memset(ones_row[:], 1.0))
    P.do("sync", lambda e: e.dma_start(out=s_mur[:], in_=murT), dma=True)
    P.do("sync", lambda e: e.dma_start(out=s_muk[:], in_=mukT), dma=True)
    if halo_zero:
        P.do("vector", lambda e: e.memset(s_halo[:], 0.0))
    elif halo_col is not None:
        P.do("sync", lambda e: e.dma_start(out=s_halo[:].unsqueeze(2), in_=halo_col), dma=True)
    else:
        P.do("sync", lambda e: e.dma_start(out=s_halo[:], in_=io["xhalo"]), dma=True)
    P.do("gpsimd", lambda e: e.dma_start(
        out=chunks(s_wr, D, KD), in_=wrT.rearrange("(c p) m -> p c m", p=128)), dma=True)
    P.do("gpsimd", lambda e: e.dma_start(
        out=chunks(s_wk, D, KD), in_=wkT.rearrange("(c p) m -> p c m", p=128)), dma=True)
    P.do("gpsimd", lambda e: e.dma_start(
        out=chunks(s_wv, D, KD), in_=wvT.rearrange("(c p) m -> p c m", p=128)), dma=True)
    P.do("vector", lambda e: e.tensor_scalar(s_murm[:], s_mur[:], -1.0, 1.0, ALU.mult, ALU.add))
    P.do("vector", lambda e: e.tensor_scalar(s_mukm[:], s_muk[:], -1.0, 1.0, ALU.mult, ALU.add))
    P.do("scalar", lambda e: e.square(scr[:, 0:KD], s_halo[:]))
    for kc in range(KD):
        P.do("tensor", lambda e, kc=kc: e.matmul(
            p_sum[0:1, 0:1], ones_col[:], s_halo[:, kc:kc + 1],
            start=(kc == 0), stop=(kc == KD - 1)))
        P.do("tensor", lambda e, kc=kc: e.matmul(
            p_ssq[0:1, 0:1], ones_col[:], scr[:, kc:kc + 1],
            start=(kc == 0), stop=(kc == KD - 1)))
    r0 = srow[0:1, 0:1]
    r1 = srow[0:1, 1:2]
    c_eps = P.const(1e-5, 1)
    c_mh = P.const(-0.5, 1)
    P.do("vector", lambda e: e.tensor_scalar(r0, p_sum[0:1, 0:1], 1.0 / D, None, ALU.mult))
    P.do("vector", lambda e: e.tensor_scalar(r1, p_ssq[0:1, 0:1], 1.0 / D, None, ALU.mult))
    P.do("vector", lambda e: e.scalar_tensor_tensor(r1, r0, 0.0, r1, ALU.mult, ALU.add)) if False else None
    P.do("vector", lambda e: e.tensor_tensor(srow[0:1, 2:3], r0, r0, ALU.mult))
    P.do("vector", lambda e: e.tensor_tensor(r1, r1, srow[0:1, 2:3], ALU.subtract))
    P.do("scalar", lambda e: e.activation(r1, r1, AF.Ln, bias=c_eps))
    P.do("scalar", lambda e: e.activation(r1, r1, AF.Exp, bias=0.0, scale=c_mh))
    P.do("tensor", lambda e: e.matmul(p_mb[:, 0:1], ones_row[:], r0, start=True, stop=True))
    P.do("tensor", lambda e: e.matmul(p_rb[:, 0:1], ones_row[:], r1, start=True, stop=True))
    mbb = p_mb[:, 0:1].unsqueeze(1).to_broadcast([128, KD, 1])
    rbb = p_rb[:, 0:1].unsqueeze(1).to_broadcast([128, KD, 1])
    hv = s_halo[:].unsqueeze(2)
    pv = s_ln4p[:].unsqueeze(2)
    P.do("vector", lambda e: e.tensor_tensor(hv, hv, mbb, ALU.subtract))
    P.do("vector", lambda e: e.tensor_tensor(pv, hv, rbb, ALU.mult))

    for it in range(ntile):
        t0 = it * tile
        P.do("sync", lambda e, t0=t0: e.dma_start(
            out=chunks(sx, tile, KD), in_=fmslice(x6, t_base + t0, tile)), dma=True)
        _layernorm_emit(P, sx, KD, tile, ones_col, ones_row, srow, p_sum[0:1, :],
                        p_ssq[0:1, :], p_mb, p_rb, scr, chunks(s_ln4, tile, KD))
        P.do("vector", lambda e: e.tensor_copy(
            chunks(s_prev, tile, KD)[:, :, 1:tile],
            chunks(s_ln4, tile, KD)[:, :, 0:tile - 1]))
        P.do("vector", lambda e: e.tensor_copy(
            chunks(s_prev, tile, KD)[:, :, 0:1], s_ln4p[:].unsqueeze(2)))
        P.do("vector", lambda e: e.tensor_copy(
            s_ln4p[:].unsqueeze(2), chunks(s_ln4, tile, KD)[:, :, tile - 1:tile]))
        for kc in range(KD):
            ln = s_ln4[:, kc * tile:(kc + 1) * tile]
            pr = s_prev[:, kc * tile:(kc + 1) * tile]
            P.do("vector", lambda e, kc=kc, ln=ln: e.tensor_scalar(
                s_t[:], ln, s_mur[:, kc:kc + 1], None, ALU.mult))
            P.do("vector", lambda e, kc=kc, pr=pr: e.scalar_tensor_tensor(
                s_xr[:, kc * tile:(kc + 1) * tile], pr, s_murm[:, kc:kc + 1],
                s_t[:], ALU.mult, ALU.add))
            P.do("vector", lambda e, kc=kc, ln=ln: e.tensor_scalar(
                s_t[:], ln, s_muk[:, kc:kc + 1], None, ALU.mult))
            P.do("vector", lambda e, kc=kc, pr=pr: e.scalar_tensor_tensor(
                s_xk[:, kc * tile:(kc + 1) * tile], pr, s_mukm[:, kc:kc + 1],
                s_t[:], ALU.mult, ALU.add))
        for mc in range(KD):
            pt = p_mm[mc % 2]
            for kc in range(KD):
                P.do("tensor", lambda e, pt=pt, mc=mc, kc=kc: e.matmul(
                    pt[:], s_wr[:, kc * D + mc * 128:kc * D + (mc + 1) * 128],
                    s_xr[:, kc * tile:(kc + 1) * tile],
                    start=(kc == 0), stop=(kc == KD - 1)))
            P.do("scalar", lambda e, pt=pt, mc=mc: e.activation(
                s_r[:, mc * tile:(mc + 1) * tile], pt[:], AF.Sigmoid))
        for mc in range(KD):
            pt = p_mm[mc % 2]
            for kc in range(KD):
                P.do("tensor", lambda e, pt=pt, mc=mc, kc=kc: e.matmul(
                    pt[:], s_wk[:, kc * D + mc * 128:kc * D + (mc + 1) * 128],
                    s_xk[:, kc * tile:(kc + 1) * tile],
                    start=(kc == 0), stop=(kc == KD - 1)))
            P.do("scalar", lambda e, pt=pt: e.activation(s_t[:], pt[:], AF.Relu))
            P.do("scalar", lambda e, mc=mc: e.activation(
                s_k[:, mc * tile:(mc + 1) * tile], s_t[:], AF.Square))
        for mc in range(KD):
            pt = p_mm[mc % 2]
            for kc in range(KD):
                P.do("tensor", lambda e, pt=pt, mc=mc, kc=kc: e.matmul(
                    pt[:], s_wv[:, kc * D + mc * 128:kc * D + (mc + 1) * 128],
                    s_k[:, kc * tile:(kc + 1) * tile],
                    start=(kc == 0), stop=(kc == KD - 1)))
            P.do("vector", lambda e, pt=pt, mc=mc: e.tensor_tensor(
                s_t[:], pt[:], s_r[:, mc * tile:(mc + 1) * tile], ALU.mult))
            P.do("vector", lambda e, mc=mc: e.tensor_tensor(
                sx[:, mc * tile:(mc + 1) * tile], s_t[:],
                sx[:, mc * tile:(mc + 1) * tile], ALU.add))
        P.do("gpsimd", lambda e, t0=t0: e.dma_start(
            out=fmslice(o_xf, t_base + t0, tile), in_=chunks(sx, tile, KD)), dma=True)


def build_L7(tile=512):
    P = Prog("L7")
    io = {
        "x6": P.din("x6", [D, TC]),
        "xhalo": P.din("xhalo", [128, KD]),
        "murT": P.din("murT", [128, KD]),
        "mukT": P.din("mukT", [128, KD]),
        "wrT": P.din("wrT", [D, D]),
        "wkT": P.din("wkT", [D, D]),
        "wvT": P.din("wvT", [D, D]),
        "o_xf": P.dout("o_xf", [D, TC]),
    }
    emit_L7(P, io, 0, tile)
    return P.finish()


def _to_fm(x2d):
    return np.ascontiguousarray(x2d.T)


_NC_CACHE = {}


def _nc(name, builder):
    if name not in _NC_CACHE:
        _NC_CACHE[name] = builder()
    return _NC_CACHE[name]


def _col(v):
    n = v.shape[0] // 128
    return np.ascontiguousarray(v.reshape(n, 128).T)


def _run(nc, in_maps):
    res = run_bass_kernel_spmd(nc, in_maps, core_ids=list(range(NCORES)))
    return res.results


def kernel(**inputs):
    f32 = np.float32
    x_full = inputs["L0_attn_x"][0]
    xfm = [np.ascontiguousarray(x_full[c * TC:(c + 1) * TC].T) for c in range(NCORES)]

    im1 = []
    d_vec = _col(inputs["L0_attn_D"].astype(f32))
    a_log = np.array(inputs["L0_attn_A_log"], f32).reshape(1, 1)
    for c in range(NCORES):
        im1.append({
            "xfm": xfm[c],
            "w_in": inputs["L0_attn_W_in"],
            "w_xp": inputs["L0_attn_W_xp"],
            "w_b": inputs["L0_attn_W_B"],
            "w_c": inputs["L0_attn_W_C"],
            "d_vec": d_vec,
            "a_log": a_log,
        })
    r1 = _run(_nc("L1", build_L1), im1)

    U_all = np.concatenate([r["o_U"] for r in r1], axis=1)
    abar_all = np.concatenate([r["o_abar"][0] for r in r1])
    ab_b = np.ascontiguousarray(np.broadcast_to(abar_all[None, :], (128, T_ALL)))
    im2 = [{"u_e": np.ascontiguousarray(U_all[c * EC:(c + 1) * EC]),
            "ab": ab_b} for c in range(NCORES)]
    r2 = _run(_nc("L2", build_L2), im2)

    h_all = np.concatenate([r["o_h"] for r in r2], axis=0)
    ident = np.eye(128, dtype=BF)
    sel8 = np.zeros((8, 8 * 128), dtype=BF)
    for j in range(8):
        sel8[j, j * 128:(j + 1) * 128] = 1
    im3 = []
    for c in range(NCORES):
        im3.append({
            "xfm": xfm[c],
            "h0": np.ascontiguousarray(h_all[:, c * TC:(c + 1) * TC]),
            "M0": r1[c]["o_M0"],
            "DXY": r1[c]["o_DXY"],
            "w_out": inputs["L0_attn_W_out"],
            "tsig": inputs["L0_ffn_tile_sigs"],
            "wff": inputs["L0_ffn_W"],
            "bff": inputs["L0_ffn_b"],
            "ipT": np.ascontiguousarray(inputs["L1_attn_in_proj_w"].T),
            "ident": ident,
            "sel8": sel8,
        })
    r3 = _run(_nc("L3", build_L3), im3)

    xin_all = np.concatenate([r["o_xin"] for r in r3], axis=1)
    cw = np.ascontiguousarray(
        inputs["L1_attn_conv_w"].reshape(KE, 128, 4).transpose(1, 0, 2).reshape(128, KE * 4))
    cb = _col(inputs["L1_attn_conv_b"])
    dvec1 = _col(inputs["L1_attn_D"])
    dtb = _col(inputs["L1_attn_dt_proj_b"])
    im4 = []
    for c in range(NCORES):
        if c == 0:
            halo = np.zeros((E1, 3), dtype=BF)
        else:
            halo = xin_all[:, c * TC - 3:c * TC]
        im4.append({
            "xin_ext": np.ascontiguousarray(
                np.concatenate([halo, xin_all[:, c * TC:(c + 1) * TC]], axis=1)),
            "cw": cw, "cb": cb, "dvec1": dvec1, "dtb": dtb,
            "sz1": r3[c]["o_sz1"],
            "xpT": np.ascontiguousarray(inputs["L1_attn_x_proj_w"].T),
            "dtpT": np.ascontiguousarray(inputs["L1_attn_dt_proj_w"].T),
        })
    r4 = _run(_nc("L4", build_L4), im4)

    dt_all = np.concatenate([r["o_dt"] for r in r4], axis=1)
    w_all = np.concatenate([r["o_w"] for r in r4], axis=1)
    Bsm = np.concatenate([r["o_B"] for r in r4], axis=1)
    Csm = np.concatenate([r["o_C"] for r in r4], axis=1)
    sel = np.zeros((S1, S1 * 128), dtype=f32)
    for s in range(S1):
        sel[s, s * 128:(s + 1) * 128] = 1
    im5 = []
    for c in range(NCORES):
        a_e = np.ascontiguousarray(
            inputs["L1_attn_A_log"][c * EC:(c + 1) * EC].reshape(2, 128, S1)
            .transpose(1, 0, 2).reshape(128, 2 * S1))
        im5.append({
            "a_e": a_e,
            "dt_e": np.ascontiguousarray(dt_all[c * EC:(c + 1) * EC]),
            "w_e": np.ascontiguousarray(w_all[c * EC:(c + 1) * EC]),
            "Bsm": Bsm, "Csm": Csm, "sel": sel,
        })
    r5 = _run(_nc("L5", build_L5), im5)

    ys_all = np.concatenate([r["o_ys"] for r in r5], axis=0)
    im6 = []
    for c in range(NCORES):
        im6.append({
            "x3": r3[c]["o_x"],
            "ln3": r3[c]["o_ln3"],
            "ys": np.ascontiguousarray(ys_all[:, c * TC:(c + 1) * TC]),
            "M1": r3[c]["o_sz1"],
            "xcd": r4[c]["o_xcd"],
            "opT": np.ascontiguousarray(inputs["L1_attn_out_proj_w"].T),
            "cfcT": np.ascontiguousarray(inputs["L1_attn_c_fc_w"].T),
            "cprT": np.ascontiguousarray(inputs["L1_attn_c_proj_w"].T),
        })
    r6 = _run(_nc("L6", build_L6), im6)

    murT = _col(inputs["L1_ffn_mu_r"])
    mukT = _col(inputs["L1_ffn_mu_k"])
    im7 = []
    for c in range(NCORES):
        if c == 0:
            xhalo = np.zeros((128, KD), f32)
        else:
            xhalo = np.ascontiguousarray(
                r6[c - 1]["o_x6"][:, TC - 1].reshape(KD, 128).T)
        im7.append({
            "x6": r6[c]["o_x6"],
            "xhalo": xhalo,
            "murT": murT, "mukT": mukT,
            "wrT": np.ascontiguousarray(inputs["L1_ffn_W_r"].T),
            "wkT": np.ascontiguousarray(inputs["L1_ffn_W_k"].T),
            "wvT": np.ascontiguousarray(inputs["L1_ffn_W_v"].T),
        })
    r7 = _run(_nc("L7", build_L7), im7)

    out = np.empty((T_ALL, D), f32)
    for c in range(NCORES):
        out[c * TC:(c + 1) * TC] = r7[c]["o_xf"].T
    return out[None]
